# version 15
# baseline (speedup 1.0000x reference)
import numpy as np
import ml_dtypes
from contextlib import ExitStack
import concourse.bass as bass
import concourse.mybir as mybir
from concourse.bass_isa import ReduceOp
from concourse import library_config
from concourse.bass_utils import run_bass_kernel_spmd

F32 = mybir.dt.float32
BF16 = mybir.dt.bfloat16
FP8 = mybir.dt.float8e4
AF = mybir.ActivationFunctionType
ALU = mybir.AluOpType
AX = mybir.AxisListType

B, L, D = 4, 4096, 1024
HALF = L // 2
H, DQH, DVH = 4, 128, 256
NT = L // 128
NTO = HALF // 128
DI = 4096

SEM_NAMES = ("qs", "qt", "vv", "aa", "tt", "gg", "wa", "wb", "wc", "wd")


def build_program():
    nc = bass.Bass("TRN2", target_bir_lowering=False)

    dram = {}
    def din(name, shape, dt=F32):
        dram[name] = nc.dram_tensor(name, list(shape), dt, kind="ExternalInput").ap()
        return dram[name]
    def dscr(name, shape, dt):
        dram[name] = nc.dram_tensor(name, list(shape), dt, kind="Internal").ap()
        return dram[name]

    X = din("x_seq", [L, D])
    XO = din("x_own", [HALF, D])
    WQ = din("Wq", [D, 512]); WK = din("Wk", [D, 512]); WV = din("Wv", [D, 1024])
    WB = din("Wb", [D, 4]); WG = din("Wg", [D, 1024]); WO = din("Wo", [D, 1024])
    CW = din("conv_w", [D, 4])
    NW = din("norm_w_rep", [128, 256])
    GW = din("gate_w", [DI, D]); UW = din("up_w", [DI, D])
    DW = din("down_w", [D, DI])
    SIN = din("s_in_rep", [128, 1]); SID = din("s_in_down_rep", [128, 1])
    HSEL = din("half_sel_rep", [128, 2])
    MSK = din("masks", [128, 6 * 64 + 128])
    OUT = nc.dram_tensor("out_half", [HALF, D], F32, kind="ExternalOutput").ap()
    HQS = dscr("hq_scratch", [HALF, DI], BF16)
    WMX = dscr("wmax_scratch", [128], F32)
    OWA = dscr("o_woA", [(NTO + 1) * 128, D], F32)
    OWB = dscr("o_woB", [(NTO + 1) * 128, D], F32)
    WGT = dscr("wgT_hbm", [8, 128, DI], BF16)
    WUT = dscr("wuT_hbm", [8, 128, DI], BF16)
    WDT = dscr("wdT_hbm", [32, 128, D], BF16)
    WSC = dscr("wsc_scratch", [3, 128, 2048], F32)
    WDQ = dscr("wdq_scratch", [3, 128, 2048], F32)
    WRD = dscr("wrd_scratch", [3, 128, 2048], F32)

    st_outer = ExitStack()
    en = st_outer.enter_context
    sem = {n: en(nc.semaphore(n)) for n in SEM_NAMES}

    s_msk = en(nc.sbuf_tensor("s_msk", [128, 6 * 64 + 128], F32))
    s_nw = en(nc.sbuf_tensor("s_nw", [128, 256], F32))
    s_cw = en(nc.sbuf_tensor("s_cw", [128, 8 * 4], F32))
    s_gs = en(nc.sbuf_tensor("s_gs", [128, 16], F32))
    s_q1 = en(nc.sbuf_tensor("s_q1", [128, 1024], F32))
    s_q2 = en(nc.sbuf_tensor("s_q2", [128, 1024], F32))
    s_q3 = en(nc.sbuf_tensor("s_q3", [128, 1024], F32))
    s_q4 = en(nc.sbuf_tensor("s_q4", [128, 1024], F32))
    s_qsg = en(nc.sbuf_tensor("s_qsg", [128, 1024], F32))
    s_qam = en(nc.sbuf_tensor("s_qam", [128, 64], F32))
    s_qs8 = en(nc.sbuf_tensor("s_qs8", [128, 64], F32))
    s_qdq = en(nc.sbuf_tensor("s_qdq", [128, 64], F32))
    s_qrd = en(nc.sbuf_tensor("s_qrd", [128, 64], F32))
    s_ln = en(nc.sbuf_tensor("s_ln", [128, 16], F32))
    s_xt = en(nc.sbuf_tensor("s_xt", [128, 1024], F32))
    s_xn = en(nc.sbuf_tensor("s_xn", [128, 1024], F32))
    s_mh = en(nc.sbuf_tensor("s_mh", [128, 1], F32))
    s_hsel = en(nc.sbuf_tensor("s_hsel", [128, 2], F32))
    s_id = en(nc.sbuf_tensor("s_id", [128, 128], BF16))

    ts = dict(nc=nc, dram=dram, sem=sem, OUT=OUT, X=X, XO=XO,
              s_msk=s_msk, s_nw=s_nw, s_cw=s_cw, s_gs=s_gs,
              s_q1=s_q1, s_q2=s_q2, s_q3=s_q3, s_q4=s_q4, s_qsg=s_qsg,
              s_qam=s_qam, s_qs8=s_qs8, s_qdq=s_qdq, s_qrd=s_qrd,
              s_ln=s_ln, s_xt=s_xt, s_xn=s_xn, s_mh=s_mh, s_hsel=s_hsel,
              s_id=s_id)

    st_a = ExitStack()
    ena = st_a.enter_context
    s_xnT = ena(nc.sbuf_tensor("s_xnT", [128, 8 * (L + 16)], BF16))
    s_xnb = ena(nc.sbuf_tensor("s_xnb", [128, 1024], BF16))
    s_hT = ena(nc.sbuf_tensor("s_hT", [128, 8 * 128], BF16))
    s_Wq = ena(nc.sbuf_tensor("s_Wq", [128, 8 * 512], BF16))
    s_Wk = ena(nc.sbuf_tensor("s_Wk", [128, 8 * 512], BF16))
    s_Wv = ena(nc.sbuf_tensor("s_Wv", [128, 8 * 1024], BF16))
    s_Wb = ena(nc.sbuf_tensor("s_Wb", [128, 8 * 4], BF16))
    s_Wg = ena(nc.sbuf_tensor("s_Wg", [128, 8 * 1024], BF16))
    s_Wo = ena(nc.sbuf_tensor("s_Wo", [128, 8 * 1024], BF16))
    s_qk = ena(nc.sbuf_tensor("s_qk", [128, 1024], BF16))
    s_v = ena(nc.sbuf_tensor("s_v", [128, 1024], BF16))
    s_g = ena(nc.sbuf_tensor("s_g", [128, 1024], BF16))
    s_qkT = ena(nc.sbuf_tensor("s_qkT", [128, 1024], BF16))
    s_beta = ena(nc.sbuf_tensor("s_beta", [128, 4], F32))
    s_st = ena(nc.sbuf_tensor("s_st", [128, 24], F32))
    s_Sbf = ena(nc.sbuf_tensor("s_Sbf", [128, 1024], BF16))
    s_Sf = ena(nc.sbuf_tensor("s_Sf", [128, 1024], F32))
    s_uf = ena(nc.sbuf_tensor("s_uf", [128, 1024], F32))
    s_ub = ena(nc.sbuf_tensor("s_ub", [128, 1024], BF16))
    s_C = ena(nc.sbuf_tensor("s_C", [128, 256], BF16))
    s_dg = ena(nc.sbuf_tensor("s_dg", [128, 256], BF16))
    s_At = ena(nc.sbuf_tensor("s_At", [128, 256], BF16))
    s_AtT = ena(nc.sbuf_tensor("s_AtT", [128, 256], BF16))
    s_pw = ena(nc.sbuf_tensor("s_pw", [128, 9 * 256], BF16))
    s_Lt = ena(nc.sbuf_tensor("s_Lt", [128, 256], BF16))
    s_o = ena(nc.sbuf_tensor("s_o", [128, 1024], F32))
    s_gtd = ena(nc.sbuf_tensor("s_gtd", [128, 1024], BF16))
    s_gTt = ena(nc.sbuf_tensor("s_gTt", [128, 8 * 128], BF16))
    s_wsrc = ena(nc.sbuf_tensor("s_wsrc", [128, 512], F32))
    s_wq1 = ena(nc.sbuf_tensor("s_wq1", [128, 512], F32))
    s_wq3 = ena(nc.sbuf_tensor("s_wq3", [128, 512], F32))
    s_wlad = ena(nc.sbuf_tensor("s_wlad", [128, 5 * 32], F32))
    ts.update(s_xnT=s_xnT, s_xnb=s_xnb, s_hT=s_hT, s_Wq=s_Wq, s_Wk=s_Wk,
              s_Wv=s_Wv, s_Wb=s_Wb, s_Wg=s_Wg, s_Wo=s_Wo, s_qk=s_qk,
              s_v=s_v, s_g=s_g, s_qkT=s_qkT, s_beta=s_beta,
              s_st=s_st, s_Sbf=s_Sbf, s_uf=s_uf, s_ub=s_ub, s_C=s_C,
              s_dg=s_dg, s_At=s_At, s_AtT=s_AtT, s_pw=s_pw, s_Lt=s_Lt,
              s_o=s_o, s_Sf=s_Sf, s_gtd=s_gtd, s_gTt=s_gTt,
              s_wsrc=s_wsrc, s_wq1=s_wq1, s_wq3=s_wq3, s_wlad=s_wlad)
    st_a.close()
    s_xqT = en(nc.sbuf_tensor("s_xqT", [128, 8 * 2048], BF16))
    s_wgc = en(nc.sbuf_tensor("s_wgc", [128, 8 * 1024], BF16))
    s_wuc = en(nc.sbuf_tensor("s_wuc", [128, 8 * 1024], BF16))
    s_sg = en(nc.sbuf_tensor("s_sg", [128, 1024], F32))
    s_hid = en(nc.sbuf_tensor("s_hid", [128, 1024], F32))
    s_hqb = en(nc.sbuf_tensor("s_hqb", [128, 1024], BF16))
    s_hqT = en(nc.sbuf_tensor("s_hqT", [128, 32 * 512], BF16))
    s_gf = en(nc.sbuf_tensor("s_gf", [128, 1024], F32))
    s_wT = en(nc.sbuf_tensor("s_wT", [128, 32 * 1024], BF16))
    ts.update(s_xqT=s_xqT, s_wgc=s_wgc, s_wuc=s_wuc, s_sg=s_sg,
              s_hid=s_hid, s_hqb=s_hqb, s_hqT=s_hqT, s_gf=s_gf, s_wT=s_wT)
    p_S = en(nc.psum_tensor("p_S", [128, 1024], F32))
    p_A = en(nc.psum_tensor("p_A", [128, 1024], F32))
    p_B = en(nc.psum_tensor("p_B", [128, 512], F32))
    p_C = en(nc.psum_tensor("p_C", [128, 256], F32))
    p_D = en(nc.psum_tensor("p_D", [128, 1024], F32))
    ts.update(p_S=p_S, p_A=p_A, p_B=p_B, p_C=p_C, p_D=p_D)

    with nc.Block() as block:
        @block.sync
        def _(eng):
            emit_all("sync", eng, ts)

        @block.scalar
        def _(eng):
            emit_all("scalar", eng, ts)

        @block.vector
        def _(eng):
            emit_all("vector", eng, ts)

        @block.tensor
        def _(eng):
            emit_all("tensor", eng, ts)

        @block.gpsimd
        def _(eng):
            emit_all("gpsimd", eng, ts)

    st_outer.close()
    return nc



class Emit:
    def __init__(self, which, eng, ts):
        self.w = which
        self.e = eng
        self.ts = ts
        self.sem = ts["sem"]
        self.C = {n: 0 for n in SEM_NAMES}

    def op(self, eng_name, fn, inc=None, incby=1):
        i = fn(self.e) if self.w == eng_name else None
        if inc is not None:
            self.C[inc] += incby
            if i is not None:
                i.then_inc(self.sem[inc], incby)
        return i

    def dma(self, queue, out, in_, sem=None):
        eng_name = "sync" if queue == "qs" else "scalar"
        sname = sem if sem is not None else queue
        if self.w == eng_name:
            self.e.dma_start(out=out, in_=in_).then_inc(self.sem[sname], 16)
        self.C[sname] += 16

    def dmaT(self, out, in_):
        if self.w == "scalar":
            self.e.dma_start_transpose(out=out, in_=in_).then_inc(self.sem["qt"], 16)
        self.C["qt"] += 16

    def wait(self, eng_name, sems):
        for s in sems:
            if self.w == eng_name:
                self.e.wait_ge(self.sem[s], self.C[s])
            elif s == "qs" and self.w == "sync" and eng_name != "sync":
                self.e.wait_ge(self.sem[s], self.C[s])
            elif s in ("qt", "wd") and self.w == "scalar" and eng_name != "scalar":
                self.e.wait_ge(self.sem[s], self.C[s])

    def drain(self, eng_name):
        if self.w == eng_name:
            self.e.drain()

    def barrier(self):
        for s in SEM_NAMES:
            self.e.wait_ge(self.sem[s], self.C[s])


def emit_all(which, eng, ts):
    em = Emit(which, eng, ts)
    phase_p0(em, ts)
    em.barrier()
    phase_a(em, ts)
    em.barrier()
    phase_f(em, ts)
    em.wait("sync", ("qs",))
    em.wait("scalar", ("qt",))



def q3v(ap, nb):
    return ap.rearrange("p (b k) -> p b k", k=16)


def bcast16(ap, nb):
    return ap.rearrange("p (b o) -> p b o", o=1).broadcast_to((128, nb, 16))


def emit_quant(em, ts, src_ap, dst_ap, ncols, gcol):
    v, a = "vector", "scalar"
    gs = ts["s_gs"]
    nb = ncols // 16
    q1, q2, q3, q4, sg = (ts["s_q1"][:, 0:ncols], ts["s_q2"][:, 0:ncols],
                          ts["s_q3"][:, 0:ncols], ts["s_q4"][:, 0:ncols],
                          ts["s_qsg"][:, 0:ncols])
    am, s8, dq, rd = (ts["s_qam"][:, 0:nb], ts["s_qs8"][:, 0:nb],
                      ts["s_qdq"][:, 0:nb], ts["s_qrd"][:, 0:nb])
    zt = ts["s_q2"][:, 0:nb]
    pt = ts["s_q2"][:, nb:2 * nb]
    ct = ts["s_q2"][:, 2 * nb:3 * nb]
    vt = ts["s_q3"][:, 0:nb]
    ut = ts["s_q3"][:, nb:2 * nb]
    em.op(v, lambda e: e.tensor_reduce(out=am, in_=q3v(src_ap, nb), axis=AX.X,
                                       op=ALU.max, apply_absolute_value=True))
    em.drain(v)
    em.op(v, lambda e: e.tensor_scalar(out=zt, in0=am, scalar1=gs[:, gcol + 1:gcol + 2],
                                       scalar2=224.0, op0=ALU.mult, op1=ALU.min))
    em.drain(v)
    em.op(v, lambda e: e.memset(pt, 2.0 ** -7))
    em.drain(v)
    for k in (8, 4, 2, 1):
        em.op(v, lambda e, k=k: e.tensor_scalar(out=ct, in0=pt, scalar1=float(2 ** k),
                                                scalar2=None, op0=ALU.mult))
        em.drain(v)
        em.op(v, lambda e: e.tensor_tensor(out=ct, in0=ct, in1=zt, op=ALU.is_le))
        em.drain(v)
        em.op(v, lambda e, k=k: e.tensor_scalar(out=ct, in0=ct,
                                                scalar1=float(2 ** k - 1),
                                                scalar2=1.0, op0=ALU.mult, op1=ALU.add))
        em.drain(v)
        em.op(v, lambda e: e.tensor_tensor(out=pt, in0=pt, in1=ct, op=ALU.mult))
        em.drain(v)
    em.op(v, lambda e: e.reciprocal(out=ct, in_=pt))
    em.drain(v)
    em.op(v, lambda e: e.scalar_tensor_tensor(out=ut, in0=zt, scalar=8.0, in1=ct,
                                              op0=ALU.mult, op1=ALU.mult))
    em.drain(v)
    em.op(v, lambda e: e.memset(vt, 0.0))
    em.drain(v)
    for k in (16, 8, 4, 2, 1):
        em.op(v, lambda e, k=k: e.scalar_tensor_tensor(out=ct, in0=vt,
                                                       scalar=float(k) - 0.5, in1=ut,
                                                       op0=ALU.add, op1=ALU.is_le))
        em.drain(v)
        em.op(v, lambda e, k=k: e.scalar_tensor_tensor(out=vt, in0=ct,
                                                       scalar=float(k), in1=vt,
                                                       op0=ALU.mult, op1=ALU.add))
        em.drain(v)
    em.op(v, lambda e: e.scalar_tensor_tensor(out=s8, in0=vt, scalar=0.125, in1=pt,
                                              op0=ALU.mult, op1=ALU.mult))
    em.drain(v)
    em.op(v, lambda e: e.tensor_scalar(out=dq, in0=s8, scalar1=gs[:, gcol + 2:gcol + 3],
                                       scalar2=None, op0=ALU.mult))
    em.op(v, lambda e: e.tensor_scalar(out=rd, in0=s8, scalar1=1e-30,
                                       scalar2=None, op0=ALU.max))
    em.drain(v)
    em.op(v, lambda e: e.reciprocal(out=rd, in_=rd))
    em.drain(v)
    em.op(v, lambda e: e.tensor_scalar(out=rd, in0=rd, scalar1=gs[:, gcol:gcol + 1],
                                       scalar2=None, op0=ALU.mult), inc="vv")
    em.drain(v)
    em.wait("gpsimd", ("vv", "gg"))
    em.op("gpsimd", lambda e: e.tensor_tensor(out=q3v(q1, nb),
                                              in0=q3v(src_ap, nb),
                                              in1=bcast16(rd, nb), op=ALU.mult),
          inc="gg")
    em.wait(a, ("gg",))
    em.op(a, lambda e: e.activation(out=sg, in_=q1, func=AF.Sign))
    em.op(a, lambda e: e.activation(out=q2, in_=q1, func=AF.Abs), inc="gg")
    em.wait(v, ("gg",))
    em.op(v, lambda e: e.tensor_scalar(out=q3, in0=q2, scalar1=0.5,
                                       scalar2=None, op0=ALU.is_ge))
    em.drain(v)
    for th in (1.5, 2.5, 3.5):
        em.op(v, lambda e, th=th: e.scalar_tensor_tensor(
            out=q3, in0=q2, scalar=th, in1=q3, op0=ALU.is_ge, op1=ALU.add))
        em.drain(v)
    em.op(v, lambda e: e.tensor_scalar(out=q4, in0=q2, scalar1=5.0,
                                       scalar2=None, op0=ALU.is_ge))
    em.drain(v)
    em.op(v, lambda e: e.scalar_tensor_tensor(
        out=q4, in0=q2, scalar=7.0, in1=q4, op0=ALU.is_ge, op1=ALU.add))
    em.drain(v)
    em.op(v, lambda e: e.scalar_tensor_tensor(
        out=q3, in0=q4, scalar=2.0, in1=q3, op0=ALU.mult, op1=ALU.add))
    em.drain(v)
    em.op(v, lambda e: e.tensor_scalar(out=q4, in0=q2, scalar1=10.0,
                                       scalar2=None, op0=ALU.is_ge))
    em.drain(v)
    em.op(v, lambda e: e.scalar_tensor_tensor(
        out=q3, in0=q4, scalar=4.0, in1=q3, op0=ALU.mult, op1=ALU.add),
        inc="vv")
    em.drain(v)
    em.wait("gpsimd", ("vv",))
    em.op("gpsimd", lambda e: e.tensor_tensor(out=q3v(q3, nb), in0=q3v(q3, nb),
                                              in1=bcast16(dq, nb), op=ALU.mult))
    em.drain("gpsimd")
    em.op("gpsimd", lambda e: e.tensor_tensor(out=dst_ap, in0=q3, in1=sg,
                                              op=ALU.mult), inc="vv")
    em.drain("gpsimd")


def emit_ln(em, ts, x_ap, xn_ap):
    v, a = "vector", "scalar"
    ln = ts["s_ln"]
    em.op(v, lambda e: e.bn_stats(out=ln[:, 0:6], in_=x_ap[:, 0:512]))
    em.op(v, lambda e: e.bn_stats(out=ln[:, 6:12], in_=x_ap[:, 512:1024]))
    em.drain(v)
    em.op(v, lambda e: e.bn_aggr(out=ln[:, 12:14], in_=ln[:, 0:12]))
    em.drain(v)
    em.op(v, lambda e: e.tensor_scalar(out=ln[:, 14:15], in0=ln[:, 13:14],
                                       scalar1=1e-5, scalar2=None, op0=ALU.add),
          inc="vv")
    em.drain(v)
    em.wait("gpsimd", ("vv",))
    em.op("gpsimd", lambda e: e.tensor_tensor(
        out=ln[:, 15:16], in0=ln[:, 14:15], in1=ts["s_mh"][:], op=ALU.pow),
        inc="gg")
    em.wait(v, ("gg",))
    em.op(v, lambda e: e.tensor_scalar(out=xn_ap, in0=x_ap, scalar1=ln[:, 12:13],
                                       scalar2=None, op0=ALU.subtract))
    em.drain(v)
    em.op(v, lambda e: e.tensor_scalar(out=xn_ap, in0=xn_ap, scalar1=ln[:, 15:16],
                                       scalar2=None, op0=ALU.mult), inc="vv")
    em.drain(v)



def phase_p0(em, ts):
    v, a, t, s, g = "vector", "scalar", "tensor", "sync", "gpsimd"
    dram = ts["dram"]
    nc_gs = ts["s_gs"]
    em.op(v, lambda e: e.memset(ts["s_mh"][:], -0.5))
    em.dma("qs", ts["s_msk"][:], dram["masks"][:, :])
    em.dma("qs", ts["s_nw"][:], dram["norm_w_rep"][:, :])
    em.dma("qs", nc_gs[:, 0:1], dram["s_in_rep"][:, :])
    em.dma("qs", nc_gs[:, 3:4], dram["s_in_down_rep"][:, :])
    em.dma("qs", ts["s_hsel"][:], dram["half_sel_rep"][:, :])
    cw3 = ts["s_cw"][:].rearrange("p (c i) -> p c i", i=4)
    em.dma("qs", cw3, dram["conv_w"][:, :].rearrange("(c p) i -> p c i", p=128))
    em.wait(v, ("qs",))
    em.op(v, lambda e: e.tensor_copy(out=ts["s_id"][:],
                                     in_=ts["s_msk"][:, 384:512]))
    for c in (0, 3):
        em.op(v, lambda e, c=c: e.tensor_scalar(out=nc_gs[:, c + 1:c + 2],
                                                in0=nc_gs[:, c:c + 1],
                                                scalar1=1.0 / 12.0, scalar2=None,
                                                op0=ALU.mult))
        em.op(v, lambda e, c=c: e.tensor_copy(out=nc_gs[:, c + 2:c + 3],
                                              in_=nc_gs[:, c:c + 1]))
        em.drain(v)
        em.op(v, lambda e, c=c: e.reciprocal(out=nc_gs[:, c + 2:c + 3],
                                             in_=nc_gs[:, c + 2:c + 3]))
        em.drain(v)
    em.op(v, lambda e: e.memset(nc_gs[:, 15:16], 0.0), inc="vv")
    em.drain(v)



def emit_wamax1(em, ts, src, sheet, slot):
    v, s = "vector", "sync"
    slot_ap = (ts["s_wsrc"][:], ts["s_wq1"][:], ts["s_wq3"][:],
               ts["s_qsg"][:, 0:512])[slot]
    grp = (ts["s_qam"], ts["s_qs8"], ts["s_qdq"], ts["s_qrd"])[slot]
    sem = ("wa", "wb", "wc", "wd")[slot]
    em.wait(s, (sem,))
    em.dma("qs", slot_ap, src)
    em.wait(v, ("qs",))
    em.op(v, lambda e: e.tensor_reduce(
        out=grp[:, 0:32], in_=q3v(slot_ap, 32), axis=AX.X,
        op=ALU.max, apply_absolute_value=True))
    em.drain(v)
    em.op(v, lambda e: e.tensor_reduce(
        out=grp[:, 32:33], in_=grp[:, 0:32], axis=AX.X,
        op=ALU.max, apply_absolute_value=True))
    em.drain(v)
    em.op(v, lambda e: e.tensor_tensor(
        out=ts["s_gs"][:, 15:16], in0=ts["s_gs"][:, 15:16],
        in1=grp[:, 32:33], op=ALU.max), inc=sem)
    em.drain(v)
    em.wait(s, (sem,))
    em.dma("qs", sheet, grp[:, 0:32])


def emit_wladder(em, ts, mi, gcol, p):
    v, s = "vector", "sync"
    gs = ts["s_gs"]
    dram = ts["dram"]
    src = ts["s_wsrc"][:]
    zt = ts["s_wq1"][:]
    pt = ts["s_wq3"][:]
    ct = ts["s_qsg"][:, 0:512]
    vt = ts["s_qsg"][:, 512:1024]
    if True:
        em.wait(s, ("vv", "wa", "wb", "wc", "wd"))
        em.dma("qs", src, dram["wsc_scratch"][mi, :, p * 512:(p + 1) * 512])
        em.wait(v, ("qs",))
        em.op(v, lambda e: e.tensor_scalar(out=zt, in0=src,
                                           scalar1=gs[:, gcol + 1:gcol + 2],
                                           scalar2=224.0, op0=ALU.mult,
                                           op1=ALU.min))
        em.op(v, lambda e: e.memset(pt, 2.0 ** -7))
        em.drain(v)
        for k in (8, 4, 2, 1):
            em.op(v, lambda e, k=k: e.tensor_scalar(out=ct, in0=pt,
                                                    scalar1=float(2 ** k),
                                                    scalar2=None, op0=ALU.mult))
            em.drain(v)
            em.op(v, lambda e: e.tensor_tensor(out=ct, in0=ct, in1=zt,
                                               op=ALU.is_le))
            em.drain(v)
            em.op(v, lambda e, k=k: e.tensor_scalar(out=ct, in0=ct,
                                                    scalar1=float(2 ** k - 1),
                                                    scalar2=1.0, op0=ALU.mult,
                                                    op1=ALU.add))
            em.drain(v)
            em.op(v, lambda e: e.tensor_tensor(out=pt, in0=pt, in1=ct,
                                               op=ALU.mult))
            em.drain(v)
        em.op(v, lambda e: e.reciprocal(out=ct, in_=pt))
        em.drain(v)
        em.op(v, lambda e: e.scalar_tensor_tensor(out=src, in0=zt, scalar=8.0,
                                                  in1=ct, op0=ALU.mult,
                                                  op1=ALU.mult))
        em.op(v, lambda e: e.memset(vt, 0.0))
        em.drain(v)
        for k in (16, 8, 4, 2, 1):
            em.op(v, lambda e, k=k: e.scalar_tensor_tensor(out=ct, in0=vt,
                                                           scalar=float(k) - 0.5,
                                                           in1=src, op0=ALU.add,
                                                           op1=ALU.is_le))
            em.drain(v)
            em.op(v, lambda e, k=k: e.scalar_tensor_tensor(out=vt, in0=ct,
                                                           scalar=float(k),
                                                           in1=vt, op0=ALU.mult,
                                                           op1=ALU.add))
            em.drain(v)
        em.op(v, lambda e: e.scalar_tensor_tensor(out=zt, in0=vt, scalar=0.125,
                                                  in1=pt, op0=ALU.mult,
                                                  op1=ALU.mult))
        em.drain(v)
        em.op(v, lambda e: e.tensor_scalar(out=pt, in0=zt,
                                           scalar1=gs[:, gcol + 2:gcol + 3],
                                           scalar2=None, op0=ALU.mult))
        em.op(v, lambda e: e.tensor_scalar(out=ct, in0=zt, scalar1=1e-30,
                                           scalar2=None, op0=ALU.max))
        em.drain(v)
        em.op(v, lambda e: e.reciprocal(out=ct, in_=ct))
        em.drain(v)
        em.op(v, lambda e: e.tensor_scalar(out=ct, in0=ct,
                                           scalar1=gs[:, gcol:gcol + 1],
                                           scalar2=None, op0=ALU.mult),
              inc="vv")
        em.drain(v)
        em.wait(s, ("vv",))
        em.dma("qs", dram["wdq_scratch"][mi, :, p * 512:(p + 1) * 512], pt)
        em.dma("qs", dram["wrd_scratch"][mi, :, p * 512:(p + 1) * 512], ct)


def emit_wbounce(em, ts, gcol):
    v, s = "vector", "sync"
    nc_gs = ts["s_gs"]
    em.wait(s, ("wa", "wb", "wc", "wd"))
    em.dma("qs", ts["dram"]["wmax_scratch"][:].rearrange("(p o) -> p o", o=1),
           nc_gs[:, 15:16])
    em.wait(s, ("qs",))
    em.dma("qs", ts["s_wlad"][0:1, 0:128],
           ts["dram"]["wmax_scratch"][:].rearrange("(o f) -> o f", o=1))
    em.wait(v, ("qs",))
    em.op(v, lambda e: e.tensor_reduce(out=ts["s_wlad"][0:1, 128:129],
                                       in_=ts["s_wlad"][0:1, 0:128], axis=AX.X,
                                       op=ALU.max, apply_absolute_value=True))
    em.drain(v)
    em.op(v, lambda e: e.tensor_copy(
        out=ts["s_wlad"][0:1, 0:128],
        in_=ts["s_wlad"][0:1, 128:129].broadcast_to((1, 128))), inc="vv")
    em.drain(v)
    em.wait(s, ("vv",))
    em.dma("qs", ts["dram"]["wmax_scratch"][:].rearrange("(o f) -> o f", o=1),
           ts["s_wlad"][0:1, 0:128])
    em.wait(s, ("qs",))
    em.dma("qs", nc_gs[:, gcol:gcol + 1],
           ts["dram"]["wmax_scratch"][:].rearrange("(p o) -> p o", o=1))
    em.wait(v, ("qs",))
    em.op(v, lambda e: e.reciprocal(out=nc_gs[:, gcol:gcol + 1],
                                    in_=nc_gs[:, gcol:gcol + 1]))
    em.drain(v)
    em.op(v, lambda e: e.tensor_scalar(out=nc_gs[:, gcol:gcol + 1],
                                       in0=nc_gs[:, gcol:gcol + 1],
                                       scalar1=2688.0, scalar2=None, op0=ALU.mult))
    em.drain(v)
    em.op(v, lambda e: e.tensor_scalar(out=nc_gs[:, gcol + 1:gcol + 2],
                                       in0=nc_gs[:, gcol:gcol + 1],
                                       scalar1=1.0 / 12.0, scalar2=None,
                                       op0=ALU.mult))
    em.op(v, lambda e: e.tensor_copy(out=nc_gs[:, gcol + 2:gcol + 3],
                                     in_=nc_gs[:, gcol:gcol + 1]))
    em.drain(v)
    em.op(v, lambda e: e.reciprocal(out=nc_gs[:, gcol + 2:gcol + 3],
                                    in_=nc_gs[:, gcol + 2:gcol + 3]))
    em.drain(v)
    em.op(v, lambda e: e.memset(nc_gs[:, 15:16], 0.0), inc="vv")
    em.drain(v)


def emit_wload(em, ts, mi, sidx, src_hbm):
    s = "sync"
    dram = ts["dram"]
    em.wait(s, ("wa", "wb", "wc", "wd", "vv"))
    em.dma("qs", ts["s_wsrc"][:], src_hbm)
    em.dma("qs", ts["s_qdq"][:, 0:32],
           dram["wdq_scratch"][mi, :, sidx * 32:(sidx + 1) * 32])
    em.dma("qs", ts["s_qrd"][:, 0:32],
           dram["wrd_scratch"][mi, :, sidx * 32:(sidx + 1) * 32])


def emit_wchain(em, ts, par):
    v, a, s, g = "vector", "scalar", "sync", "gpsimd"
    nb = 32
    q1, q3 = ts["s_wq1"][:], ts["s_wq3"][:]
    q2 = ts["s_qsg"][:, 0:512]
    sg = ts["s_qsg"][:, 512:1024]
    wnt = ts["s_xnb"][:, par * 512:(par + 1) * 512]
    dq, rd = ts["s_qdq"][:, 0:nb], ts["s_qrd"][:, 0:nb]
    src = ts["s_wsrc"][:]
    em.wait(g, ("qs", "wb"))
    em.op(g, lambda e: e.tensor_tensor(out=q3v(q1, nb), in0=q3v(src, nb),
                                       in1=bcast16(rd, nb), op=ALU.mult),
          inc="wa")
    em.wait(a, ("wa", "wc"))
    em.op(a, lambda e: e.activation(out=sg, in_=q1, func=AF.Sign))
    em.op(a, lambda e: e.activation(out=q2, in_=q1, func=AF.Abs), inc="wb")
    em.wait(v, ("wb", "wa"))
    em.op(v, lambda e: e.tensor_scalar(out=q3, in0=q2, scalar1=0.5,
                                       scalar2=None, op0=ALU.is_ge))
    em.drain(v)
    for th in (1.5, 2.5, 3.5):
        em.op(v, lambda e, th=th: e.scalar_tensor_tensor(
            out=q3, in0=q2, scalar=th, in1=q3, op0=ALU.is_ge, op1=ALU.add))
        em.drain(v)
    em.op(v, lambda e: e.tensor_scalar(out=q1, in0=q2, scalar1=5.0,
                                       scalar2=None, op0=ALU.is_ge))
    em.drain(v)
    em.op(v, lambda e: e.scalar_tensor_tensor(
        out=q1, in0=q2, scalar=7.0, in1=q1, op0=ALU.is_ge, op1=ALU.add))
    em.drain(v)
    em.op(v, lambda e: e.scalar_tensor_tensor(
        out=q3, in0=q1, scalar=2.0, in1=q3, op0=ALU.mult, op1=ALU.add))
    em.drain(v)
    em.op(v, lambda e: e.tensor_scalar(out=q1, in0=q2, scalar1=10.0,
                                       scalar2=None, op0=ALU.is_ge))
    em.drain(v)
    em.op(v, lambda e: e.scalar_tensor_tensor(
        out=q3, in0=q1, scalar=4.0, in1=q3, op0=ALU.mult, op1=ALU.add),
        inc="wc")
    em.drain(v)
    em.wait(g, ("wc", "wd"))
    em.op(g, lambda e: e.tensor_tensor(out=q3v(q3, nb), in0=q3v(q3, nb),
                                       in1=bcast16(dq, nb), op=ALU.mult))
    em.drain(g)
    em.op(g, lambda e: e.tensor_tensor(out=wnt, in0=q3, in1=sg,
                                       op=ALU.mult), inc="wa")
    em.drain(g)


def emit_wtrans(em, ts, par, dests):
    a, t, s = "scalar", "tensor", "sync"
    base = par * 512
    em.wait(t, ("wa", "wc"))
    for j in range(4):
        em.op(t, lambda e, j=j: e.matmul(
            out=ts["p_B"][:, j * 128:(j + 1) * 128],
            lhsT=ts["s_xnb"][:, base + j * 128:base + (j + 1) * 128],
            rhs=ts["s_id"][:], start=(j == 0), stop=(j == 3)),
            inc="wb" if j == 3 else None)
    em.wait(a, ("wb", "wd"))
    em.op(a, lambda e: e.copy(out=ts["s_xnb"][:, base:base + 512],
                              in_=ts["p_B"][:, 0:512]), inc="wc")
    em.wait(a, ("wc",))
    for pj in range(4):
        em.dma("qt", dests[pj],
               ts["s_xnb"][:, base + pj * 128:base + (pj + 1) * 128],
               sem="wd")
    em.wait(t, ("wc",))



def phase_f(em, ts):
    v, a, t, s = "vector", "scalar", "tensor", "sync"
    dram = ts["dram"]
    XO = ts["XO"]
    for tt in range(NTO):
        em.wait(s, ("vv",))
        em.dma("qs", ts["s_xt"][:], XO[tt * 128:(tt + 1) * 128, :])
        em.wait(v, ("qs",))
        emit_ln(em, ts, ts["s_xt"][:], ts["s_xn"][:])
        em.wait(v, ("qt",))
        emit_quant(em, ts, ts["s_xn"][:], ts["s_hqb"][:, 0:1024], 1024, 0)
        em.wait("scalar", ("vv",))
        xt3 = ts["s_xqT"][:].rearrange("p (hch t) -> p hch t", hch=8)[
            :, :, tt * 128: tt * 128 + 128]
        em.dmaT(xt3, ts["s_hqb"][:, 0:1024])
    em.wait(t, ("qt",))
    em.wait(v, ("qt",))
    gs = ts["s_gs"]
    nbF = 64
    scr = ts["s_gf"]
    lzt, lpt, lct, lvt, lut = (scr[:, 256:320], scr[:, 320:384],
                               scr[:, 384:448], scr[:, 448:512],
                               scr[:, 512:576])
    def fsmalls(par):
        if par == 0:
            return (ts["s_qam"][:, 0:64], ts["s_qs8"][:, 0:64],
                    ts["s_qdq"][:, 0:64], ts["s_qrd"][:, 0:64])
        return (scr[:, 0:64], scr[:, 64:128], scr[:, 128:192], scr[:, 192:256])
    def fhbuf(par):
        return ts["s_hid"][:] if par == 0 else ts["s_q4"][:]

    def emit_fladder(k):
        par = k % 2
        am, s8, dq, rd = fsmalls(par)
        hb = fhbuf(par)
        em.wait(v, ("vv", "wb"))
        em.op(v, lambda e: e.tensor_reduce(out=am, in_=q3v(hb, nbF), axis=AX.X,
                                           op=ALU.max, apply_absolute_value=True))
        em.drain(v)
        em.op(v, lambda e: e.tensor_scalar(out=lzt, in0=am,
                                           scalar1=gs[:, 4:5], scalar2=224.0,
                                           op0=ALU.mult, op1=ALU.min))
        em.op(v, lambda e: e.memset(lpt, 2.0 ** -7))
        em.drain(v)
        for kk in (8, 4, 2, 1):
            em.op(v, lambda e, kk=kk: e.tensor_scalar(out=lct, in0=lpt,
                                                      scalar1=float(2 ** kk),
                                                      scalar2=None, op0=ALU.mult))
            em.drain(v)
            em.op(v, lambda e: e.tensor_tensor(out=lct, in0=lct, in1=lzt,
                                               op=ALU.is_le))
            em.drain(v)
            em.op(v, lambda e, kk=kk: e.tensor_scalar(out=lct, in0=lct,
                                                      scalar1=float(2 ** kk - 1),
                                                      scalar2=1.0, op0=ALU.mult,
                                                      op1=ALU.add))
            em.drain(v)
            em.op(v, lambda e: e.tensor_tensor(out=lpt, in0=lpt, in1=lct,
                                               op=ALU.mult))
            em.drain(v)
        em.op(v, lambda e: e.reciprocal(out=lct, in_=lpt))
        em.drain(v)
        em.op(v, lambda e: e.scalar_tensor_tensor(out=lut, in0=lzt, scalar=8.0,
                                                  in1=lct, op0=ALU.mult,
                                                  op1=ALU.mult))
        em.op(v, lambda e: e.memset(lvt, 0.0))
        em.drain(v)
        for kk in (16, 8, 4, 2, 1):
            em.op(v, lambda e, kk=kk: e.scalar_tensor_tensor(
                out=lct, in0=lvt, scalar=float(kk) - 0.5, in1=lut,
                op0=ALU.add, op1=ALU.is_le))
            em.drain(v)
            em.op(v, lambda e, kk=kk: e.scalar_tensor_tensor(
                out=lvt, in0=lct, scalar=float(kk), in1=lvt,
                op0=ALU.mult, op1=ALU.add))
            em.drain(v)
        em.op(v, lambda e: e.scalar_tensor_tensor(out=s8, in0=lvt, scalar=0.125,
                                                  in1=lpt, op0=ALU.mult,
                                                  op1=ALU.mult))
        em.drain(v)
        em.op(v, lambda e: e.tensor_scalar(out=dq, in0=s8, scalar1=gs[:, 5:6],
                                           scalar2=None, op0=ALU.mult))
        em.op(v, lambda e: e.tensor_scalar(out=rd, in0=s8, scalar1=1e-30,
                                           scalar2=None, op0=ALU.max))
        em.drain(v)
        em.op(v, lambda e: e.reciprocal(out=rd, in_=rd))
        em.drain(v)
        em.op(v, lambda e: e.tensor_scalar(out=rd, in0=rd, scalar1=gs[:, 3:4],
                                           scalar2=None, op0=ALU.mult),
              inc="wa")
        em.drain(v)

    def emit_fval_head(k):
        par = k % 2
        am, s8, dq, rd = fsmalls(par)
        hb = fhbuf(par)
        q1 = ts["s_q1"][:]
        q2, sgn = ts["s_q2"][:], ts["s_qsg"][:]
        em.wait("gpsimd", ("wa", "wd", "vv"))
        em.op("gpsimd", lambda e: e.tensor_tensor(
            out=q3v(q1, nbF), in0=q3v(hb, nbF), in1=bcast16(rd, nbF),
            op=ALU.mult), inc="wb")
        em.wait(a, ("wb", "vv"))
        em.op(a, lambda e: e.activation(out=sgn, in_=q1, func=AF.Sign))
        em.op(a, lambda e: e.activation(out=q2, in_=q1, func=AF.Abs), inc="wc")

    def emit_fval_tail(k, oc):
        par = k % 2
        am, s8, dq, rd = fsmalls(par)
        q1, q2, q3 = ts["s_q1"][:], ts["s_q2"][:], ts["s_q3"][:]
        sgn = ts["s_qsg"][:]
        em.wait(v, ("wc",))
        em.op(v, lambda e: e.tensor_scalar(out=q3, in0=q2, scalar1=0.5,
                                           scalar2=None, op0=ALU.is_ge))
        em.drain(v)
        for th in (1.5, 2.5, 3.5):
            em.op(v, lambda e, th=th: e.scalar_tensor_tensor(
                out=q3, in0=q2, scalar=th, in1=q3, op0=ALU.is_ge, op1=ALU.add))
            em.drain(v)
        em.op(v, lambda e: e.tensor_scalar(out=q1, in0=q2, scalar1=5.0,
                                           scalar2=None, op0=ALU.is_ge))
        em.drain(v)
        em.op(v, lambda e: e.scalar_tensor_tensor(
            out=q1, in0=q2, scalar=7.0, in1=q1, op0=ALU.is_ge, op1=ALU.add))
        em.drain(v)
        em.op(v, lambda e: e.scalar_tensor_tensor(
            out=q3, in0=q1, scalar=2.0, in1=q3, op0=ALU.mult, op1=ALU.add))
        em.drain(v)
        em.op(v, lambda e: e.tensor_scalar(out=q1, in0=q2, scalar1=10.0,
                                           scalar2=None, op0=ALU.is_ge))
        em.drain(v)
        em.op(v, lambda e: e.scalar_tensor_tensor(
            out=q3, in0=q1, scalar=4.0, in1=q3, op0=ALU.mult, op1=ALU.add),
            inc="wd")
        em.drain(v)
        em.wait("gpsimd", ("wd", "qs"))
        em.op("gpsimd", lambda e: e.tensor_tensor(out=q3v(q3, nbF),
                                                  in0=q3v(q3, nbF),
                                                  in1=bcast16(dq, nbF),
                                                  op=ALU.mult))
        em.drain("gpsimd")
        em.op("gpsimd", lambda e: e.tensor_tensor(out=ts["s_hqb"][:], in0=q3,
                                                  in1=sgn, op=ALU.mult),
              inc="vv")
        em.drain("gpsimd")
        em.wait(s, ("vv",))
        em.dma("qs", dram["hq_scratch"][k * 128:(k + 1) * 128,
                                        oc * 1024:(oc + 1) * 1024],
               ts["s_hqb"][:])

    for oc in range(4):
        em.wait(s, ("tt",))
        for ch in range(8):
            em.dma("qs", ts["s_wgc"][:, ch * 1024:(ch + 1) * 1024],
                   dram["wgT_hbm"][ch, :, oc * 1024:(oc + 1) * 1024])
            em.dma("qs", ts["s_wuc"][:, ch * 1024:(ch + 1) * 1024],
                   dram["wuT_hbm"][ch, :, oc * 1024:(oc + 1) * 1024])
        em.wait(t, ("qs",))
        for tt in range(NTO):
            em.wait(t, ("qt", "aa"))
            for ch in range(8):
                for hf in range(2):
                    em.op(t, lambda e, ch=ch, tt=tt, hf=hf: e.matmul(
                        out=ts["p_A"][:, hf * 512:(hf + 1) * 512],
                        lhsT=ts["s_xqT"][:, ch * 2048 + tt * 128: ch * 2048 + (tt + 1) * 128],
                        rhs=ts["s_wgc"][:, ch * 1024 + hf * 512: ch * 1024 + (hf + 1) * 512],
                        start=(ch == 0), stop=(ch == 7)))
            for ch in range(8):
                for hf in range(2):
                    em.op(t, lambda e, ch=ch, tt=tt, hf=hf: e.matmul(
                        out=ts["p_D"][:, hf * 512:(hf + 1) * 512],
                        lhsT=ts["s_xqT"][:, ch * 2048 + tt * 128: ch * 2048 + (tt + 1) * 128],
                        rhs=ts["s_wuc"][:, ch * 1024 + hf * 512: ch * 1024 + (hf + 1) * 512],
                        start=(ch == 0), stop=(ch == 7)),
                        inc="tt" if (ch == 7 and hf == 1) else None)
            em.wait(a, ("tt",))
            em.op(a, lambda e: e.activation(out=ts["s_sg"][:], in_=ts["p_A"][:],
                                            func=AF.Silu), inc="aa")
            em.wait(v, ("aa", "wb"))
            em.op(v, lambda e, tt=tt: e.tensor_tensor(
                out=fhbuf(tt % 2), in0=ts["s_sg"][:], in1=ts["p_D"][:],
                op=ALU.mult), inc="qt")
            em.drain(v)
            if tt > 0:
                emit_fval_head(tt - 1)
            emit_fladder(tt)
            if tt > 0:
                emit_fval_tail(tt - 1, oc)
        emit_fval_head(NTO - 1)
        emit_fval_tail(NTO - 1, oc)
    em.wait(s, ("qt",))
    for ic in range(32):
        em.dma("qs", ts["s_wT"][:, ic * 1024:(ic + 1) * 1024],
               dram["wdT_hbm"][ic, :, :])
    em.wait(t, ("qs",))
    for tb in range(4):
        em.wait("scalar", ("qs",))
        outT = ts["s_hqT"][:].rearrange("p (ic r) -> p ic r", r=512)
        em.dmaT(outT, dram["hq_scratch"][tb * 512:(tb + 1) * 512, :])
        em.wait(t, ("qt",))
        for tsub in range(4):
            tt = tb * 4 + tsub
            em.wait(t, ("aa",))
            for ic in range(32):
                for half in range(2):
                    em.op(t, lambda e, ic=ic, half=half, tsub=tsub: e.matmul(
                        out=ts["p_A"][:, half * 512:(half + 1) * 512],
                        lhsT=ts["s_hqT"][:, ic * 512 + tsub * 128: ic * 512 + (tsub + 1) * 128],
                        rhs=ts["s_wT"][:, ic * 1024 + half * 512: ic * 1024 + (half + 1) * 512],
                        start=(ic == 0), stop=(ic == 31)),
                        inc="tt" if (ic == 31 and half == 1) else None)
            em.wait(a, ("tt", "qs", "vv"))
            em.op(a, lambda e: e.activation(out=ts["s_gf"][:], in_=ts["p_A"][:],
                                            func=AF.Silu), inc="aa")
            em.wait(s, ("aa", "vv"))
            em.dma("qs", ts["s_q1"][:], dram["o_woA"][tt * 128:(tt + 1) * 128, :])
            em.dma("qs", ts["s_q2"][:], dram["o_woB"][tt * 128:(tt + 1) * 128, :])
            em.dma("qs", ts["s_xt"][:], XO[tt * 128:(tt + 1) * 128, :])
            em.wait(v, ("aa", "qs"))
            em.op(v, lambda e: e.tensor_scalar(
                out=ts["s_q1"][:], in0=ts["s_q1"][:],
                scalar1=ts["s_hsel"][:, 0:1], scalar2=None, op0=ALU.mult))
            em.op(v, lambda e: e.tensor_scalar(
                out=ts["s_q2"][:], in0=ts["s_q2"][:],
                scalar1=ts["s_hsel"][:, 1:2], scalar2=None, op0=ALU.mult))
            em.drain(v)
            em.op(v, lambda e: e.tensor_tensor(out=ts["s_q1"][:], in0=ts["s_q1"][:],
                                               in1=ts["s_q2"][:], op=ALU.add))
            em.drain(v)
            em.op(v, lambda e: e.tensor_tensor(out=ts["s_q3"][:], in0=ts["s_q1"][:],
                                               in1=ts["s_gf"][:], op=ALU.mult))
            em.drain(v)
            em.op(v, lambda e: e.tensor_tensor(out=ts["s_q3"][:], in0=ts["s_q3"][:],
                                               in1=ts["s_xt"][:], op=ALU.add),
                  inc="vv")
            em.drain(v)
            em.wait(s, ("vv",))
            em.dma("qs", ts["OUT"][tt * 128:(tt + 1) * 128, :], ts["s_q3"][:])



def phase_a(em, ts):
    v, a, t, s = "vector", "scalar", "tensor", "sync"
    dram = ts["dram"]
    X = ts["X"]
    msk = ts["s_msk"]
    trilS = msk[:, 0:64]
    triuI = msk[:, 64:128]
    id4 = msk[:, 128:384]
    LP = L + 16

    GWh, UWh, DWh = dram["gate_w"], dram["up_w"], dram["down_w"]
    WGTh, WUTh, WDTh = dram["wgT_hbm"], dram["wuT_hbm"], dram["wdT_hbm"]
    WSCh = dram["wsc_scratch"]
    amax_tasks = []
    ns = [0]
    def amax1(W, mi, pp, h):
        sl = ns[0] % 4
        ns[0] += 1
        amax_tasks.append(lambda: emit_wamax1(
            em, ts, W[pp * 128:(pp + 1) * 128, h * 512:(h + 1) * 512],
            WSCh[mi, :, 64 * pp + 32 * h:64 * pp + 32 * (h + 1)], sl))
    for mi, (W, gcol) in enumerate(((GWh, 6), (UWh, 9))):
        for pp in range(32):
            for h in range(2):
                amax1(W, mi, pp, h)
        amax_tasks.append(lambda gcol=gcol: emit_wbounce(em, ts, gcol))
        for p in range(4):
            amax_tasks.append(lambda mi=mi, gcol=gcol, p=p:
                              emit_wladder(em, ts, mi, gcol, p))
    for rc in range(8):
        for cp in range(8):
            sl = ns[0] % 4
            ns[0] += 1
            amax_tasks.append(lambda rc=rc, cp=cp, sl=sl: emit_wamax1(
                em, ts, DWh[rc * 128:(rc + 1) * 128, cp * 512:(cp + 1) * 512],
                WSCh[2, :, 256 * rc + 32 * cp:256 * rc + 32 * (cp + 1)], sl))
    amax_tasks.append(lambda: emit_wbounce(em, ts, 12))
    for p in range(4):
        amax_tasks.append(lambda p=p: emit_wladder(em, ts, 2, 12, p))
    quant_tasks = []
    for mi, (W, WT) in enumerate(((GWh, WGTh), (UWh, WUTh))):
        for pp in range(32):
            for h in range(2):
                dests = [WT[4 * h + j, :, pp * 128:(pp + 1) * 128]
                         for j in range(4)]
                quant_tasks.append(
                    (mi, 2 * pp + h,
                     W[pp * 128:(pp + 1) * 128, h * 512:(h + 1) * 512], dests))
    for rc in range(8):
        for cp in range(8):
            dests = [WDTh[4 * cp + j, :, rc * 128:(rc + 1) * 128]
                     for j in range(4)]
            quant_tasks.append(
                (2, 8 * rc + cp,
                 DWh[rc * 128:(rc + 1) * 128, cp * 512:(cp + 1) * 512], dests))
    n_amax = len(amax_tasks)
    n_quant = len(quant_tasks)

    for wname, sbname, dout in [("Wq", "s_Wq", 512), ("Wk", "s_Wk", 512),
                                ("Wv", "s_Wv", 1024), ("Wb", "s_Wb", 4),
                                ("Wg", "s_Wg", 1024), ("Wo", "s_Wo", 1024)]:
        for ch in range(8):
            em.wait(s, ("vv",))
            em.dma("qs", ts["s_xt"][:, 0:dout],
                   dram[wname][ch * 128:(ch + 1) * 128, :])
            em.wait(v, ("qs",))
            em.op(v, lambda e, sbname=sbname, ch=ch, dout=dout: e.tensor_copy(
                out=ts[sbname][:, ch * dout:(ch + 1) * dout],
                in_=ts["s_xt"][:, 0:dout]), inc="vv")
            em.drain(v)

    em.op(v, lambda e: e.memset(ts["s_Sbf"][:], 0.0))
    em.op(v, lambda e: e.memset(ts["s_Sf"][:], 0.0))
    em.op(v, lambda e: e.memset(ts["s_mh"][:], -0.5), inc="vv")
    em.wait("gpsimd", ("vv",))
    for ch in range(8):
        em.op(v, lambda e, ch=ch: e.memset(ts["s_xnT"][:, ch * LP:ch * LP + 16], 0.0))
    em.op(v, lambda e: e.memset(ts["s_uf"][:], 0.0), inc="vv")

    adone = 0
    for tt in range(NT):
        em.wait(s, ("vv",))
        em.dma("qs", ts["s_xt"][:], X[tt * 128:(tt + 1) * 128, :])
        em.wait(v, ("qs",))
        emit_ln(em, ts, ts["s_xt"][:], ts["s_xn"][:])
        em.wait(v, ("qt",))
        em.op(v, lambda e: e.tensor_copy(
            out=ts["s_xnb"][:, 0:1024], in_=ts["s_xn"][:]), inc="vv")
        em.drain(v)
        em.wait("scalar", ("vv",))
        nt3 = ts["s_xnT"][:].rearrange("p (dch t) -> p dch t", dch=8)[
            :, :, 16 + tt * 128: 16 + tt * 128 + 128]
        em.dmaT(nt3, ts["s_xnb"][:, 0:1024])
        want = (tt + 1) * n_amax // NT
        while adone < want:
            amax_tasks[adone]()
            adone += 1
    em.wait(v, ("qt",))
    em.wait(t, ("qt",))

    qp = {"ld": 0, "ch": 0, "tr": 0}
    def qstep(lim):
        if qp["tr"] < qp["ch"] and (qp["ch"] - qp["tr"] >= 2 or
                                    qp["ch"] >= min(n_quant, lim)):
            mi, sidx, src, dests = quant_tasks[qp["tr"]]
            emit_wtrans(em, ts, qp["tr"] % 2, dests)
            qp["tr"] += 1
        if qp["ch"] < qp["ld"] and qp["ch"] < min(n_quant, lim):
            emit_wchain(em, ts, qp["ch"] % 2)
            qp["ch"] += 1
        if qp["ld"] < min(n_quant, lim + 1) and qp["ld"] <= qp["ch"]:
            mi, sidx, src, dests = quant_tasks[qp["ld"]]
            emit_wload(em, ts, mi, sidx, src)
            qp["ld"] += 1
    for tt in range(NT):
        cw3 = ts["s_cw"][:].rearrange("p (c i) -> p c i", i=4)
        base = 13 + tt * 128
        def xv(i, base=base):
            return ts["s_xnT"][:].rearrange("p (c t) -> p c t", c=8)[
                :, :, base + i: base + i + 128]
        q13 = ts["s_q1"][:].rearrange("p (c t) -> p c t", t=128)
        q23 = ts["s_q2"][:].rearrange("p (c t) -> p c t", t=128)
        em.wait(v, ("aa", "qs"))
        em.op(v, lambda e: e.tensor_tensor(
            out=q13, in0=xv(0), in1=cw3[:, :, 0:1].broadcast_to((128, 8, 128)),
            op=ALU.mult))
        for i in (1, 2, 3):
            em.op(v, lambda e, i=i: e.tensor_tensor(
                out=q23, in0=xv(i), in1=cw3[:, :, i:i + 1].broadcast_to((128, 8, 128)),
                op=ALU.mult))
            em.drain(v)
            em.op(v, lambda e: e.tensor_tensor(out=ts["s_q1"][:], in0=ts["s_q1"][:],
                                               in1=ts["s_q2"][:], op=ALU.add))
            em.drain(v)
        em.drain(v)
        em.op(v, lambda e: e.nop(), inc="vv")
        em.wait(a, ("vv",))
        em.op(a, lambda e: e.activation(out=ts["s_hT"][:], in_=ts["s_q1"][:],
                                        func=AF.Silu), inc="aa")
        em.wait(t, ("aa", "vv"))

        for ch in range(8):
            hsl = ts["s_hT"][:, ch * 128:(ch + 1) * 128]
            em.op(t, lambda e, ch=ch, hsl=hsl: e.matmul(
                out=ts["p_A"][:, 0:512], lhsT=hsl,
                rhs=ts["s_Wq"][:, ch * 512:(ch + 1) * 512],
                start=(ch == 0), stop=(ch == 7)))
            em.op(t, lambda e, ch=ch, hsl=hsl: e.matmul(
                out=ts["p_A"][:, 512:1024], lhsT=hsl,
                rhs=ts["s_Wk"][:, ch * 512:(ch + 1) * 512],
                start=(ch == 0), stop=(ch == 7)))
            em.op(t, lambda e, ch=ch, hsl=hsl: e.matmul(
                out=ts["p_D"][:, 0:4], lhsT=hsl,
                rhs=ts["s_Wb"][:, ch * 4:(ch + 1) * 4],
                start=(ch == 0), stop=(ch == 7)), inc="tt" if ch == 7 else None)
        em.wait(a, ("tt",))
        st = ts["s_st"]
        em.wait(v, ("tt", "qt"))
        em.op(v, lambda e: e.tensor_copy(out=ts["s_qk"][:], in_=ts["p_A"][:]),
              inc="vv")
        em.drain(v)
        em.wait(a, ("vv",))
        for h in range(8):
            em.op(a, lambda e, h=h: e.activation(
                out=ts["s_q3"][:, h * 128:(h + 1) * 128],
                in_=ts["s_qk"][:, h * 128:(h + 1) * 128],
                func=AF.Square, accum_out=st[:, 4 + h if h < 4 else h - 4:
                                             5 + h if h < 4 else h - 3]))
        em.op(a, lambda e: e.activation(out=ts["s_beta"][:], in_=ts["p_D"][:, 0:4],
                                        func=AF.Sigmoid), inc="aa")
        em.wait("scalar", ("vv", "tt"))
        em.dmaT(ts["s_qkT"][:].rearrange("p (c r) -> p c r", r=128), ts["s_qk"][:])
        qstep(8 * (tt + 1))
        em.wait("gpsimd", ("aa",))
        em.op("gpsimd", lambda e: e.tensor_scalar(out=st[:, 8:16], in0=st[:, 0:8],
                                                  scalar1=1e-24, scalar2=None,
                                                  op0=ALU.max))
        em.drain("gpsimd")
        em.op("gpsimd", lambda e: e.tensor_tensor(
            out=st[:, 8:16], in0=st[:, 8:16],
            in1=ts["s_mh"][:].broadcast_to((128, 8)), op=ALU.pow))
        em.drain("gpsimd")
        em.op("gpsimd", lambda e: e.tensor_scalar(out=st[:, 8:16], in0=st[:, 8:16],
                                                  scalar1=1e12, scalar2=None,
                                                  op0=ALU.min))
        em.drain("gpsimd")
        em.op("gpsimd", lambda e: e.tensor_tensor(out=st[:, 16:20], in0=st[:, 8:12],
                                                  in1=ts["s_beta"][:], op=ALU.mult))
        em.drain("gpsimd")
        em.op("gpsimd", lambda e: e.tensor_copy(out=st[:, 20:24], in_=st[:, 16:20]))
        em.drain("gpsimd")
        em.op("gpsimd", lambda e: e.tensor_tensor(out=st[:, 16:20], in0=st[:, 16:20],
                                                  in1=st[:, 8:12], op=ALU.mult),
              inc="gg")
        em.wait(v, ("gg",))
        em.wait(t, ("gg", "qt"))
        qstep(8 * (tt + 1))
        for ch in range(8):
            hsl = ts["s_hT"][:, ch * 128:(ch + 1) * 128]
            for half in range(2):
                em.op(t, lambda e, ch=ch, hsl=hsl, half=half: e.matmul(
                    out=ts["p_A"][:, half * 512:(half + 1) * 512], lhsT=hsl,
                    rhs=ts["s_Wv"][:, ch * 1024 + half * 512: ch * 1024 + (half + 1) * 512],
                    start=(ch == 0), stop=(ch == 7)),
                    inc="tt" if (ch == 7 and half == 1) else None)
        em.wait(v, ("tt",))
        em.op(v, lambda e: e.tensor_copy(out=ts["s_v"][:], in_=ts["p_A"][:]), inc="vv")
        qstep(8 * (tt + 1))
        em.wait(t, ("vv",))
        for ch in range(8):
            hsl = ts["s_hT"][:, ch * 128:(ch + 1) * 128]
            for half in range(2):
                em.op(t, lambda e, ch=ch, hsl=hsl, half=half: e.matmul(
                    out=ts["p_A"][:, half * 512:(half + 1) * 512], lhsT=hsl,
                    rhs=ts["s_Wg"][:, ch * 1024 + half * 512: ch * 1024 + (half + 1) * 512],
                    start=(ch == 0), stop=(ch == 7)),
                    inc="tt" if (ch == 7 and half == 1) else None)
        em.wait(v, ("tt",))
        em.op(v, lambda e: e.tensor_copy(out=ts["s_g"][:], in_=ts["p_A"][:]), inc="vv")
        em.drain(v)
        qstep(8 * (tt + 1))

        for c in range(2):
            hb = slice(64 * c, 64 * c + 64)
            tpK = 64 * c
            emit_chunk(em, ts, tt, c, hb, tpK, trilS, triuI, id4)
            qstep(8 * (tt + 1))

        st = ts["s_st"]
        em.wait(a, ("vv", "qs"))
        for h in range(4):
            em.op(a, lambda e, h=h: e.activation(
                out=ts["s_q2"][:, h * 256:(h + 1) * 256],
                in_=ts["s_o"][:, h * 256:(h + 1) * 256],
                func=AF.Square, accum_out=st[:, h:h + 1]))
        em.op(a, lambda e: e.activation(out=ts["s_q4"][:], in_=ts["s_g"][:],
                                        func=AF.Silu), inc="aa")
        em.wait("gpsimd", ("aa",))
        em.op("gpsimd", lambda e: e.tensor_scalar(out=st[:, 4:8], in0=st[:, 0:4],
                                                  scalar1=1.0 / 256.0, scalar2=1e-5,
                                                  op0=ALU.mult, op1=ALU.add))
        em.drain("gpsimd")
        em.op("gpsimd", lambda e: e.tensor_tensor(
            out=st[:, 4:8], in0=st[:, 4:8],
            in1=ts["s_mh"][:].broadcast_to((128, 4)), op=ALU.pow), inc="gg")
        em.wait(v, ("gg", "aa"))
        o4 = ts["s_o"][:].rearrange("p (h f) -> p h f", f=256)
        rs4 = st[:, 4:8].rearrange("p (h o) -> p h o", o=1).broadcast_to((128, 4, 256))
        em.op(v, lambda e: e.tensor_tensor(out=o4, in0=o4, in1=rs4, op=ALU.mult))
        em.drain(v)
        nw4 = ts["s_nw"][:].rearrange("p (o f) -> p o f", o=1).broadcast_to((128, 4, 256))
        em.op(v, lambda e: e.tensor_tensor(out=o4, in0=o4, in1=nw4, op=ALU.mult))
        em.drain(v)
        em.wait(v, ("qt",))
        em.op(v, lambda e: e.tensor_tensor(out=ts["s_gtd"][:], in0=ts["s_o"][:],
                                           in1=ts["s_q4"][:], op=ALU.mult), inc="vv")
        em.drain(v)
        qstep(8 * (tt + 1))
        em.wait("scalar", ("vv",))
        outT = ts["s_gTt"][:].rearrange("p (c r) -> p c r", r=128)
        em.dmaT(outT, ts["s_gtd"][:])
        em.wait(t, ("qt", "aa", "vv"))
        for ch in range(8):
            for half in range(2):
                em.op(t, lambda e, ch=ch, half=half: e.matmul(
                    out=ts["p_A"][:, half * 512:(half + 1) * 512],
                    lhsT=ts["s_gTt"][:, ch * 128:(ch + 1) * 128],
                    rhs=ts["s_Wo"][:, ch * 1024 + half * 512: ch * 1024 + (half + 1) * 512],
                    start=(ch == 0), stop=(ch == 7)),
                    inc="tt" if (ch == 7 and half == 1) else None)
        em.wait(v, ("tt", "qs"))
        em.op(v, lambda e: e.tensor_copy(out=ts["s_q2"][:], in_=ts["p_A"][:]),
              inc="vv")
        em.drain(v)
        qstep(8 * (tt + 1))
        em.wait(s, ("vv",))
        rA = tt if tt < NTO else NTO
        rB = tt - NTO if tt >= NTO else NTO
        em.dma("qs", dram["o_woA"][rA * 128:(rA + 1) * 128, :], ts["s_q2"][:])
        em.dma("qs", dram["o_woB"][rB * 128:(rB + 1) * 128, :], ts["s_q2"][:])
        while qp["ch"] < min(n_quant, (tt + 1) * 8) or \
                (tt == NT - 1 and qp["tr"] < n_quant):
            qstep((tt + 1) * 8)


def emit_chunk(em, ts, tt, c, hb, tpK, trilS, triuI, id4):
    v, a, t = "vector", "scalar", "tensor"
    st = ts["s_st"]
    tok0 = 64 * c
    tpO = (0, 64 * c)
    tpKK = (64 * c, 64 * c)

    em.wait(t, ("vv", "aa"))
    for h in range(4):
        kTc = ts["s_qkT"][:, 512 + h * 128 + tok0: 512 + h * 128 + tok0 + 64]
        em.op(t, lambda e, h=h, kTc=kTc: e.matmul(
            out=ts["p_C"][hb, h * 64:(h + 1) * 64], lhsT=kTc, rhs=kTc,
            start=True, stop=True, tile_position=tpO),
            inc="tt" if h == 3 else None)
    em.wait(v, ("tt",))
    g3 = ts["p_C"][hb, :].rearrange("p (h f) -> p h f", f=64)
    tril3 = trilS[hb, :].rearrange("p (o f) -> p o f", o=1).broadcast_to((64, 4, 64))
    em.op(v, lambda e: e.tensor_tensor(out=ts["s_C"][hb, :].rearrange(
        "p (h f) -> p h f", f=64), in0=g3, in1=tril3, op=ALU.mult))
    em.drain(v)
    s3 = st[hb, 16:20].rearrange("p (h o) -> p h o", o=1).broadcast_to((64, 4, 64))
    em.op(v, lambda e: e.tensor_tensor(out=ts["s_dg"][hb, :].rearrange(
        "p (h f) -> p h f", f=64), in0=id4[hb, :].rearrange("p (h f) -> p h f", f=64),
        in1=s3, op=ALU.mult))
    em.op(v, lambda e: e.tensor_tensor(out=ts["s_At"][hb, :].rearrange(
        "p (h f) -> p h f", f=64), in0=ts["s_C"][hb, :].rearrange(
        "p (h f) -> p h f", f=64), in1=s3, op=ALU.mult), inc="vv")
    em.drain(v)
    em.wait(t, ("vv",))
    for h in range(4):
        em.op(t, lambda e, h=h: e.matmul(
            out=ts["p_B"][hb, h * 64:(h + 1) * 64],
            lhsT=ts["s_C"][hb, h * 64:(h + 1) * 64],
            rhs=ts["s_dg"][hb, h * 64:(h + 1) * 64],
            start=True, stop=True, tile_position=tpKK),
            inc="tt" if h == 3 else None)
    em.wait(v, ("tt",))
    em.op(v, lambda e: e.tensor_copy(out=ts["s_AtT"][hb, :], in_=ts["p_B"][hb, 0:256]),
          inc="vv")
    em.drain(v)
    em.wait(t, ("vv",))
    for h in range(4):
        kTc = ts["s_qkT"][:, 512 + h * 128 + tok0: 512 + h * 128 + tok0 + 64]
        em.op(t, lambda e, h=h, kTc=kTc: e.matmul(
            out=ts["p_A"][hb, h * 256:(h + 1) * 256], lhsT=kTc,
            rhs=ts["s_Sbf"][:, h * 256:(h + 1) * 256],
            start=True, stop=True, tile_position=tpO),
            inc="tt" if h == 3 else None)
    def pwY(Lv):
        return ts["s_pw"][hb, Lv * 512: Lv * 512 + 256]
    def pwYT(Lv):
        return ts["s_pw"][hb, Lv * 512 + 256: Lv * 512 + 512]
    prevY, prevYT = ts["s_At"], ts["s_AtT"]
    for Lv in range(4):
        em.wait(t, ("vv",))
        for h in range(4):
            hs = slice(h * 64, (h + 1) * 64)
            em.op(t, lambda e, h=h, hs=hs: e.matmul(
                out=ts["p_B"][hb, h * 64:(h + 1) * 64],
                lhsT=prevYT[hb, hs], rhs=prevY[hb, hs],
                start=True, stop=True, tile_position=tpKK))
            em.op(t, lambda e, h=h, hs=hs: e.matmul(
                out=ts["p_B"][hb, 256 + h * 64:256 + (h + 1) * 64],
                lhsT=prevY[hb, hs], rhs=prevYT[hb, hs],
                start=True, stop=True, tile_position=tpKK),
                inc="tt" if h == 3 else None)
        em.wait(v, ("tt",))
        if Lv < 3:
            em.op(v, lambda e, Lv=Lv: e.tensor_copy(
                out=ts["s_pw"][hb, Lv * 512:(Lv + 1) * 512], in_=ts["p_B"][hb, :]),
                inc="vv")
        else:
            em.op(v, lambda e, Lv=Lv: e.tensor_copy(
                out=ts["s_pw"][hb, Lv * 512:(Lv + 1) * 512], in_=ts["p_B"][hb, :]),
                inc="vv")
        em.drain(v)
        prevY, prevYT = None, None
        prevY = _PwSlice(ts, Lv, 0)
        prevYT = _PwSlice(ts, Lv, 1)
    em.wait(v, ("tt",))
    ks3 = ts["p_A"][hb, :].rearrange("p (h f) -> p h f", f=256)
    rk3 = st[hb, 8:12].rearrange("p (h o) -> p h o", o=1).broadcast_to((64, 4, 256))
    u3 = ts["s_uf"][hb, :].rearrange("p (h f) -> p h f", f=256)
    v3 = ts["s_v"][hb, :].rearrange("p (h f) -> p h f", f=256)
    em.op(v, lambda e: e.tensor_tensor(out=u3, in0=ks3, in1=rk3, op=ALU.mult))
    em.drain(v)
    em.op(v, lambda e: e.tensor_tensor(out=u3, in0=v3, in1=u3, op=ALU.subtract))
    em.drain(v)
    rkb3 = st[hb, 20:24].rearrange("p (h o) -> p h o", o=1).broadcast_to((64, 4, 256))
    ub3 = ts["s_ub"][hb, :].rearrange("p (h f) -> p h f", f=256)
    em.op(v, lambda e: e.tensor_tensor(out=ub3, in0=u3, in1=rkb3, op=ALU.mult),
          inc="vv")
    em.drain(v)
    em.wait(t, ("vv",))
    for h in range(4):
        hs = slice(h * 64, (h + 1) * 64)
        em.op(t, lambda e, h=h, hs=hs: e.matmul(
            out=ts["p_B"][hb, h * 64:(h + 1) * 64],
            lhsT=_PwSlice(ts, 3, 0)[hb, hs], rhs=_PwSlice(ts, 3, 1)[hb, hs],
            start=True, stop=True, tile_position=tpKK),
            inc="tt" if h == 3 else None)
    em.wait(v, ("tt",))
    em.op(v, lambda e: e.tensor_copy(out=ts["s_Lt"][hb, :], in_=ts["p_B"][hb, 0:256]),
          inc="vv")
    em.drain(v)
    app_seq = [(ts["s_Lt"], 1.0), (_PwSlice(ts, 3, 1), 1.0), (_PwSlice(ts, 2, 1), 1.0),
               (_PwSlice(ts, 1, 1), 1.0), (_PwSlice(ts, 0, 1), 1.0),
               (ts["s_AtT"], -1.0)]
    for lhsT_t, sgn in app_seq:
        em.wait(t, ("vv",))
        for h in range(4):
            em.op(t, lambda e, h=h, lhsT_t=lhsT_t: e.matmul(
                out=ts["p_A"][hb, h * 256:(h + 1) * 256],
                lhsT=lhsT_t[hb, h * 64:(h + 1) * 64],
                rhs=ts["s_ub"][hb, h * 256:(h + 1) * 256],
                start=True, stop=True, tile_position=tpKK),
                inc="tt" if h == 3 else None)
        em.wait(v, ("tt",))
        em.op(v, lambda e, sgn=sgn: e.scalar_tensor_tensor(
            out=ts["s_ub"][hb, :], in0=ts["p_A"][hb, :], scalar=sgn,
            in1=ts["s_ub"][hb, :], op0=ALU.mult, op1=ALU.add), inc="vv")
        em.drain(v)
    em.wait(t, ("vv",))
    for h in range(4):
        qTc = ts["s_qkT"][:, h * 128 + tok0: h * 128 + tok0 + 64]
        em.op(t, lambda e, h=h, qTc=qTc: e.matmul(
            out=ts["p_D"][hb, h * 256:(h + 1) * 256], lhsT=qTc,
            rhs=ts["s_Sbf"][:, h * 256:(h + 1) * 256],
            start=(h % 2 == 0), stop=False, tile_position=tpO))
    for h in range(4):
        qTc = ts["s_qkT"][:, h * 128 + tok0: h * 128 + tok0 + 64]
        kTc = ts["s_qkT"][:, 512 + h * 128 + tok0: 512 + h * 128 + tok0 + 64]
        em.op(t, lambda e, h=h, qTc=qTc, kTc=kTc: e.matmul(
            out=ts["p_C"][hb, h * 64:(h + 1) * 64], lhsT=kTc, rhs=qTc,
            start=True, stop=True, tile_position=tpO),
            inc="tt" if h == 3 else None)
    em.wait(v, ("tt",))
    r3 = ts["p_C"][hb, :].rearrange("p (h f) -> p h f", f=64)
    triu3 = triuI[hb, :].rearrange("p (o f) -> p o f", o=1).broadcast_to((64, 4, 64))
    em.op(v, lambda e: e.tensor_tensor(out=ts["s_Lt"][hb, :].rearrange(
        "p (h f) -> p h f", f=64), in0=r3, in1=triu3, op=ALU.mult), inc="vv")
    em.drain(v)
    em.wait(t, ("vv",))
    for h in range(4):
        em.op(t, lambda e, h=h: e.matmul(
            out=ts["p_D"][hb, h * 256:(h + 1) * 256],
            lhsT=ts["s_Lt"][hb, h * 64:(h + 1) * 64],
            rhs=ts["s_ub"][hb, h * 256:(h + 1) * 256],
            start=False, stop=(h % 2 == 1), tile_position=tpKK),
            inc="tt" if h == 3 else None)
    for h in range(4):
        em.op(t, lambda e, h=h: e.matmul(
            out=ts["p_S"][:, h * 256:(h + 1) * 256],
            lhsT=ts["s_qk"][hb, 512 + h * 128: 512 + (h + 1) * 128],
            rhs=ts["s_ub"][hb, h * 256:(h + 1) * 256],
            start=True, stop=True, tile_position=(64 * c, 0)),
            inc="tt" if h == 3 else None)
    em.wait(v, ("tt",))
    o3 = ts["s_o"][hb, :].rearrange("p (h f) -> p h f", f=256)
    rq3 = st[hb, 12:16].rearrange("p (h o) -> p h o", o=1).broadcast_to((64, 4, 256))
    em.op(v, lambda e: e.tensor_tensor(out=o3, in0=ts["p_D"][hb, :].rearrange(
        "p (h f) -> p h f", f=256), in1=rq3, op=ALU.mult))
    em.op(v, lambda e: e.scalar_tensor_tensor(
        out=ts["s_Sf"][:], in0=ts["p_S"][:], scalar=1.0, in1=ts["s_Sf"][:],
        op0=ALU.mult, op1=ALU.add))
    em.drain(v)
    em.op(v, lambda e: e.tensor_copy(out=ts["s_Sbf"][:], in_=ts["s_Sf"][:]), inc="vv")
    em.drain(v)


class _PwSlice:
    def __init__(self, ts, Lv, which):
        self.t = ts["s_pw"]
        self.base = Lv * 512 + which * 256
    def __getitem__(self, key):
        hb, hs = key
        if isinstance(hs, slice):
            return self.t[hb, self.base + hs.start: self.base + hs.stop]
        raise KeyError(key)



_HOST_CONSTS = None

def _host_consts():
    global _HOST_CONSTS
    if _HOST_CONSTS is None:
        tril = np.tril(np.ones((64, 64), np.float32), -1)
        triu = np.triu(np.ones((64, 64), np.float32), 0)
        ident = np.eye(64, dtype=np.float32)
        id4 = np.concatenate([ident] * 4, axis=1)
        row = np.concatenate([tril, triu, id4], axis=1)
        base = np.concatenate([row, row], axis=0)
        _HOST_CONSTS = np.ascontiguousarray(
            np.concatenate([base, np.eye(128, dtype=np.float32)], axis=1))
    return _HOST_CONSTS


def kernel(**inputs):
    x = np.ascontiguousarray(inputs["L0_attn_hidden_states"], dtype=np.float32)
    consts = _host_consts()
    nw_rep = np.ascontiguousarray(
        np.broadcast_to(inputs["L0_attn_norm_w"][None, :], (128, 256)), np.float32)
    sin_rep = np.ascontiguousarray(
        np.broadcast_to(inputs["L0_ffn_s_in"].reshape(1, 1), (128, 1)), np.float32)
    sid_rep = np.ascontiguousarray(
        np.broadcast_to(inputs["L0_ffn_s_in_down"].reshape(1, 1), (128, 1)), np.float32)
    hsel = [np.ascontiguousarray(
        np.broadcast_to(np.array([[1.0 - h, float(h)]], np.float32), (128, 2)))
        for h in (0, 1)]
    shared = {
        "Wq": np.ascontiguousarray(inputs["L0_attn_Wq"], np.float32),
        "Wk": np.ascontiguousarray(inputs["L0_attn_Wk"], np.float32),
        "Wv": np.ascontiguousarray(inputs["L0_attn_Wv"], np.float32),
        "Wb": np.ascontiguousarray(inputs["L0_attn_Wb"], np.float32),
        "Wg": np.ascontiguousarray(inputs["L0_attn_Wg"], np.float32),
        "Wo": np.ascontiguousarray(inputs["L0_attn_Wo"], np.float32),
        "conv_w": np.ascontiguousarray(inputs["L0_attn_conv_w"], np.float32),
        "norm_w_rep": nw_rep,
        "s_in_rep": sin_rep,
        "s_in_down_rep": sid_rep,
        "masks": consts,
        "gate_w": np.ascontiguousarray(inputs["L0_ffn_gate_w"], np.float32),
        "up_w": np.ascontiguousarray(inputs["L0_ffn_up_w"], np.float32),
        "down_w": np.ascontiguousarray(inputs["L0_ffn_down_w"], np.float32),
    }
    in_maps = []
    for c in range(8):
        b, half = c // 2, c % 2
        m = dict(shared)
        m["x_seq"] = np.ascontiguousarray(x[b])
        m["x_own"] = np.ascontiguousarray(x[b, half * HALF:(half + 1) * HALF])
        m["half_sel_rep"] = hsel[half]
        in_maps.append(m)
    nc = build_program()
    res = run_bass_kernel_spmd(nc, in_maps, core_ids=list(range(8)))
    out = np.empty((B, L, D), np.float32)
    for c in range(8):
        b, half = c // 2, c % 2
        out[b, half * HALF:(half + 1) * HALF] = res.results[c]["out_half"]
    return out


# version 16
# speedup vs baseline: 1.0065x; 1.0065x over previous
import numpy as np
import ml_dtypes
from contextlib import ExitStack
import concourse.bass as bass
import concourse.mybir as mybir
from concourse.bass_isa import ReduceOp
from concourse import library_config
from concourse.bass_utils import run_bass_kernel_spmd

F32 = mybir.dt.float32
BF16 = mybir.dt.bfloat16
FP8 = mybir.dt.float8e4
AF = mybir.ActivationFunctionType
ALU = mybir.AluOpType
AX = mybir.AxisListType

B, L, D = 4, 4096, 1024
HALF = L // 2
H, DQH, DVH = 4, 128, 256
NT = L // 128
NTO = HALF // 128
DI = 4096

SEM_NAMES = ("qs", "qt", "vv", "aa", "tt", "gg", "wa", "wb", "wc", "wd")


def build_program():
    nc = bass.Bass("TRN2", target_bir_lowering=False)

    dram = {}
    def din(name, shape, dt=F32):
        dram[name] = nc.dram_tensor(name, list(shape), dt, kind="ExternalInput").ap()
        return dram[name]
    def dscr(name, shape, dt):
        dram[name] = nc.dram_tensor(name, list(shape), dt, kind="Internal").ap()
        return dram[name]

    X = din("x_seq", [L, D])
    XO = din("x_own", [HALF, D])
    WQ = din("Wq", [D, 512]); WK = din("Wk", [D, 512]); WV = din("Wv", [D, 1024])
    WB = din("Wb", [D, 4]); WG = din("Wg", [D, 1024]); WO = din("Wo", [D, 1024])
    CW = din("conv_w", [D, 4])
    NW = din("norm_w_rep", [128, 256])
    GW = din("gate_w", [DI, D]); UW = din("up_w", [DI, D])
    DW = din("down_w", [D, DI])
    SIN = din("s_in_rep", [128, 1]); SID = din("s_in_down_rep", [128, 1])
    HSEL = din("half_sel_rep", [128, 2])
    MSK = din("masks", [128, 6 * 64 + 128])
    OUT = nc.dram_tensor("out_half", [HALF, D], F32, kind="ExternalOutput").ap()
    HQS = dscr("hq_scratch", [HALF, DI], BF16)
    WMX = dscr("wmax_scratch", [128], F32)
    OWA = dscr("o_woA", [(NTO + 1) * 128, D], F32)
    OWB = dscr("o_woB", [(NTO + 1) * 128, D], F32)
    WGT = dscr("wgT_hbm", [8, 128, DI], BF16)
    WUT = dscr("wuT_hbm", [8, 128, DI], BF16)
    WDT = dscr("wdT_hbm", [32, 128, D], BF16)
    WSC = dscr("wsc_scratch", [3, 128, 2048], F32)
    WDQ = dscr("wdq_scratch", [3, 128, 2048], F32)
    WRD = dscr("wrd_scratch", [3, 128, 2048], F32)

    st_outer = ExitStack()
    en = st_outer.enter_context
    sem = {n: en(nc.semaphore(n)) for n in SEM_NAMES}

    s_msk = en(nc.sbuf_tensor("s_msk", [128, 6 * 64 + 128], F32))
    s_nw = en(nc.sbuf_tensor("s_nw", [128, 256], F32))
    s_cw = en(nc.sbuf_tensor("s_cw", [128, 8 * 4], F32))
    s_gs = en(nc.sbuf_tensor("s_gs", [128, 16], F32))
    s_q1 = en(nc.sbuf_tensor("s_q1", [128, 1024], F32))
    s_q2 = en(nc.sbuf_tensor("s_q2", [128, 1024], F32))
    s_q3 = en(nc.sbuf_tensor("s_q3", [128, 1024], F32))
    s_q4 = en(nc.sbuf_tensor("s_q4", [128, 1024], F32))
    s_qsg = en(nc.sbuf_tensor("s_qsg", [128, 1024], F32))
    s_qam = en(nc.sbuf_tensor("s_qam", [128, 64], F32))
    s_qs8 = en(nc.sbuf_tensor("s_qs8", [128, 64], F32))
    s_qdq = en(nc.sbuf_tensor("s_qdq", [128, 64], F32))
    s_qrd = en(nc.sbuf_tensor("s_qrd", [128, 64], F32))
    s_ln = en(nc.sbuf_tensor("s_ln", [128, 16], F32))
    s_xt = en(nc.sbuf_tensor("s_xt", [128, 1024], F32))
    s_xn = en(nc.sbuf_tensor("s_xn", [128, 1024], F32))
    s_mh = en(nc.sbuf_tensor("s_mh", [128, 1], F32))
    s_hsel = en(nc.sbuf_tensor("s_hsel", [128, 2], F32))
    s_id = en(nc.sbuf_tensor("s_id", [128, 128], BF16))

    ts = dict(nc=nc, dram=dram, sem=sem, OUT=OUT, X=X, XO=XO,
              s_msk=s_msk, s_nw=s_nw, s_cw=s_cw, s_gs=s_gs,
              s_q1=s_q1, s_q2=s_q2, s_q3=s_q3, s_q4=s_q4, s_qsg=s_qsg,
              s_qam=s_qam, s_qs8=s_qs8, s_qdq=s_qdq, s_qrd=s_qrd,
              s_ln=s_ln, s_xt=s_xt, s_xn=s_xn, s_mh=s_mh, s_hsel=s_hsel,
              s_id=s_id)

    st_a = ExitStack()
    ena = st_a.enter_context
    s_xnT = ena(nc.sbuf_tensor("s_xnT", [128, 8 * (L + 16)], BF16))
    s_xnb = ena(nc.sbuf_tensor("s_xnb", [128, 1024], BF16))
    s_hT = ena(nc.sbuf_tensor("s_hT", [128, 8 * 128], BF16))
    s_Wq = ena(nc.sbuf_tensor("s_Wq", [128, 8 * 512], BF16))
    s_Wk = ena(nc.sbuf_tensor("s_Wk", [128, 8 * 512], BF16))
    s_Wv = ena(nc.sbuf_tensor("s_Wv", [128, 8 * 1024], BF16))
    s_Wb = ena(nc.sbuf_tensor("s_Wb", [128, 8 * 4], BF16))
    s_Wg = ena(nc.sbuf_tensor("s_Wg", [128, 8 * 1024], BF16))
    s_Wo = ena(nc.sbuf_tensor("s_Wo", [128, 8 * 1024], BF16))
    s_qk = ena(nc.sbuf_tensor("s_qk", [128, 1024], BF16))
    s_v = ena(nc.sbuf_tensor("s_v", [128, 1024], BF16))
    s_g = ena(nc.sbuf_tensor("s_g", [128, 1024], BF16))
    s_qkT = ena(nc.sbuf_tensor("s_qkT", [128, 1024], BF16))
    s_beta = ena(nc.sbuf_tensor("s_beta", [128, 4], F32))
    s_st = ena(nc.sbuf_tensor("s_st", [128, 24], F32))
    s_Sbf = ena(nc.sbuf_tensor("s_Sbf", [128, 1024], BF16))
    s_Sf = ena(nc.sbuf_tensor("s_Sf", [128, 1024], F32))
    s_uf = ena(nc.sbuf_tensor("s_uf", [128, 1024], F32))
    s_ub = ena(nc.sbuf_tensor("s_ub", [128, 1024], BF16))
    s_C = ena(nc.sbuf_tensor("s_C", [128, 256], BF16))
    s_dg = ena(nc.sbuf_tensor("s_dg", [128, 256], BF16))
    s_At = ena(nc.sbuf_tensor("s_At", [128, 256], BF16))
    s_AtT = ena(nc.sbuf_tensor("s_AtT", [128, 256], BF16))
    s_pw = ena(nc.sbuf_tensor("s_pw", [128, 9 * 256], BF16))
    s_Lt = ena(nc.sbuf_tensor("s_Lt", [128, 256], BF16))
    s_o = ena(nc.sbuf_tensor("s_o", [128, 1024], F32))
    s_gtd = ena(nc.sbuf_tensor("s_gtd", [128, 1024], BF16))
    s_gTt = ena(nc.sbuf_tensor("s_gTt", [128, 8 * 128], BF16))
    s_wsrc = ena(nc.sbuf_tensor("s_wsrc", [128, 512], F32))
    s_wq1 = ena(nc.sbuf_tensor("s_wq1", [128, 512], F32))
    s_wq3 = ena(nc.sbuf_tensor("s_wq3", [128, 512], F32))
    s_wlad = ena(nc.sbuf_tensor("s_wlad", [128, 5 * 32], F32))
    ts.update(s_xnT=s_xnT, s_xnb=s_xnb, s_hT=s_hT, s_Wq=s_Wq, s_Wk=s_Wk,
              s_Wv=s_Wv, s_Wb=s_Wb, s_Wg=s_Wg, s_Wo=s_Wo, s_qk=s_qk,
              s_v=s_v, s_g=s_g, s_qkT=s_qkT, s_beta=s_beta,
              s_st=s_st, s_Sbf=s_Sbf, s_uf=s_uf, s_ub=s_ub, s_C=s_C,
              s_dg=s_dg, s_At=s_At, s_AtT=s_AtT, s_pw=s_pw, s_Lt=s_Lt,
              s_o=s_o, s_Sf=s_Sf, s_gtd=s_gtd, s_gTt=s_gTt,
              s_wsrc=s_wsrc, s_wq1=s_wq1, s_wq3=s_wq3, s_wlad=s_wlad)
    st_a.close()
    s_xqT = en(nc.sbuf_tensor("s_xqT", [128, 8 * 2048], BF16))
    s_wgc = en(nc.sbuf_tensor("s_wgc", [128, 8 * 1024], BF16))
    s_wuc = en(nc.sbuf_tensor("s_wuc", [128, 8 * 1024], BF16))
    s_sg = en(nc.sbuf_tensor("s_sg", [128, 1024], F32))
    s_hid = en(nc.sbuf_tensor("s_hid", [128, 1024], F32))
    s_hqb = en(nc.sbuf_tensor("s_hqb", [128, 1024], BF16))
    s_hqT = en(nc.sbuf_tensor("s_hqT", [128, 32 * 512], BF16))
    s_gf = en(nc.sbuf_tensor("s_gf", [128, 1024], F32))
    s_wT = en(nc.sbuf_tensor("s_wT", [128, 32 * 1024], BF16))
    ts.update(s_xqT=s_xqT, s_wgc=s_wgc, s_wuc=s_wuc, s_sg=s_sg,
              s_hid=s_hid, s_hqb=s_hqb, s_hqT=s_hqT, s_gf=s_gf, s_wT=s_wT)
    p_S = en(nc.psum_tensor("p_S", [128, 1024], F32))
    p_A = en(nc.psum_tensor("p_A", [128, 1024], F32))
    p_B = en(nc.psum_tensor("p_B", [128, 512], F32))
    p_C = en(nc.psum_tensor("p_C", [128, 256], F32))
    p_D = en(nc.psum_tensor("p_D", [128, 1024], F32))
    ts.update(p_S=p_S, p_A=p_A, p_B=p_B, p_C=p_C, p_D=p_D)

    with nc.Block() as block:
        @block.sync
        def _(eng):
            emit_all("sync", eng, ts)

        @block.scalar
        def _(eng):
            emit_all("scalar", eng, ts)

        @block.vector
        def _(eng):
            emit_all("vector", eng, ts)

        @block.tensor
        def _(eng):
            emit_all("tensor", eng, ts)

        @block.gpsimd
        def _(eng):
            emit_all("gpsimd", eng, ts)

    st_outer.close()
    return nc



class Emit:
    def __init__(self, which, eng, ts):
        self.w = which
        self.e = eng
        self.ts = ts
        self.sem = ts["sem"]
        self.C = {n: 0 for n in SEM_NAMES}

    def op(self, eng_name, fn, inc=None, incby=1):
        i = fn(self.e) if self.w == eng_name else None
        if inc is not None:
            self.C[inc] += incby
            if i is not None:
                i.then_inc(self.sem[inc], incby)
        return i

    def dma(self, queue, out, in_, sem=None):
        eng_name = "sync" if queue == "qs" else "scalar"
        sname = sem if sem is not None else queue
        if self.w == eng_name:
            self.e.dma_start(out=out, in_=in_).then_inc(self.sem[sname], 16)
        self.C[sname] += 16

    def dmaT(self, out, in_):
        if self.w == "scalar":
            self.e.dma_start_transpose(out=out, in_=in_).then_inc(self.sem["qt"], 16)
        self.C["qt"] += 16

    def wait(self, eng_name, sems):
        for s in sems:
            if self.w == eng_name:
                self.e.wait_ge(self.sem[s], self.C[s])
            elif s == "qs" and self.w == "sync" and eng_name != "sync":
                self.e.wait_ge(self.sem[s], self.C[s])
            elif s in ("qt", "wd") and self.w == "scalar" and eng_name != "scalar":
                self.e.wait_ge(self.sem[s], self.C[s])

    def drain(self, eng_name):
        if self.w == eng_name:
            self.e.drain()

    def barrier(self):
        for s in SEM_NAMES:
            self.e.wait_ge(self.sem[s], self.C[s])


def emit_all(which, eng, ts):
    em = Emit(which, eng, ts)
    phase_p0(em, ts)
    em.barrier()
    phase_a(em, ts)
    em.barrier()
    phase_f(em, ts)
    em.wait("sync", ("qs",))
    em.wait("scalar", ("qt",))



def q3v(ap, nb):
    return ap.rearrange("p (b k) -> p b k", k=16)


def bcast16(ap, nb):
    return ap.rearrange("p (b o) -> p b o", o=1).broadcast_to((128, nb, 16))


def emit_quant(em, ts, src_ap, dst_ap, ncols, gcol):
    v, a = "vector", "scalar"
    gs = ts["s_gs"]
    nb = ncols // 16
    q1, q2, q3, q4, sg = (ts["s_q1"][:, 0:ncols], ts["s_q2"][:, 0:ncols],
                          ts["s_q3"][:, 0:ncols], ts["s_q4"][:, 0:ncols],
                          ts["s_qsg"][:, 0:ncols])
    am, s8, dq, rd = (ts["s_qam"][:, 0:nb], ts["s_qs8"][:, 0:nb],
                      ts["s_qdq"][:, 0:nb], ts["s_qrd"][:, 0:nb])
    zt = ts["s_q2"][:, 0:nb]
    pt = ts["s_q2"][:, nb:2 * nb]
    ct = ts["s_q2"][:, 2 * nb:3 * nb]
    vt = ts["s_q3"][:, 0:nb]
    ut = ts["s_q3"][:, nb:2 * nb]
    em.op(v, lambda e: e.tensor_reduce(out=am, in_=q3v(src_ap, nb), axis=AX.X,
                                       op=ALU.max, apply_absolute_value=True))
    em.drain(v)
    em.op(v, lambda e: e.tensor_scalar(out=zt, in0=am, scalar1=gs[:, gcol + 1:gcol + 2],
                                       scalar2=224.0, op0=ALU.mult, op1=ALU.min))
    em.drain(v)
    em.op(v, lambda e: e.memset(pt, 2.0 ** -7))
    em.drain(v)
    for k in (8, 4, 2, 1):
        em.op(v, lambda e, k=k: e.tensor_scalar(out=ct, in0=pt, scalar1=float(2 ** k),
                                                scalar2=None, op0=ALU.mult))
        em.drain(v)
        em.op(v, lambda e: e.tensor_tensor(out=ct, in0=ct, in1=zt, op=ALU.is_le))
        em.drain(v)
        em.op(v, lambda e, k=k: e.tensor_scalar(out=ct, in0=ct,
                                                scalar1=float(2 ** k - 1),
                                                scalar2=1.0, op0=ALU.mult, op1=ALU.add))
        em.drain(v)
        em.op(v, lambda e: e.tensor_tensor(out=pt, in0=pt, in1=ct, op=ALU.mult))
        em.drain(v)
    em.op(v, lambda e: e.reciprocal(out=ct, in_=pt))
    em.drain(v)
    em.op(v, lambda e: e.scalar_tensor_tensor(out=ut, in0=zt, scalar=8.0, in1=ct,
                                              op0=ALU.mult, op1=ALU.mult))
    em.drain(v)
    em.op(v, lambda e: e.memset(vt, 0.0))
    em.drain(v)
    for k in (16, 8, 4, 2, 1):
        em.op(v, lambda e, k=k: e.scalar_tensor_tensor(out=ct, in0=vt,
                                                       scalar=float(k) - 0.5, in1=ut,
                                                       op0=ALU.add, op1=ALU.is_le))
        em.drain(v)
        em.op(v, lambda e, k=k: e.scalar_tensor_tensor(out=vt, in0=ct,
                                                       scalar=float(k), in1=vt,
                                                       op0=ALU.mult, op1=ALU.add))
        em.drain(v)
    em.op(v, lambda e: e.scalar_tensor_tensor(out=s8, in0=vt, scalar=0.125, in1=pt,
                                              op0=ALU.mult, op1=ALU.mult))
    em.drain(v)
    em.op(v, lambda e: e.tensor_scalar(out=dq, in0=s8, scalar1=gs[:, gcol + 2:gcol + 3],
                                       scalar2=None, op0=ALU.mult))
    em.op(v, lambda e: e.tensor_scalar(out=rd, in0=s8, scalar1=1e-30,
                                       scalar2=None, op0=ALU.max))
    em.drain(v)
    em.op(v, lambda e: e.reciprocal(out=rd, in_=rd))
    em.drain(v)
    em.op(v, lambda e: e.tensor_scalar(out=rd, in0=rd, scalar1=gs[:, gcol:gcol + 1],
                                       scalar2=None, op0=ALU.mult), inc="vv")
    em.drain(v)
    em.wait("gpsimd", ("vv", "gg"))
    em.op("gpsimd", lambda e: e.tensor_tensor(out=q3v(q1, nb),
                                              in0=q3v(src_ap, nb),
                                              in1=bcast16(rd, nb), op=ALU.mult),
          inc="gg")
    em.wait(a, ("gg",))
    em.op(a, lambda e: e.activation(out=sg, in_=q1, func=AF.Sign))
    em.op(a, lambda e: e.activation(out=q2, in_=q1, func=AF.Abs), inc="gg")
    em.wait(v, ("gg",))
    em.op(v, lambda e: e.tensor_scalar(out=q3, in0=q2, scalar1=0.5,
                                       scalar2=None, op0=ALU.is_ge))
    em.drain(v)
    for th in (1.5, 2.5, 3.5):
        em.op(v, lambda e, th=th: e.scalar_tensor_tensor(
            out=q3, in0=q2, scalar=th, in1=q3, op0=ALU.is_ge, op1=ALU.add))
        em.drain(v)
    em.op(v, lambda e: e.tensor_scalar(out=q4, in0=q2, scalar1=5.0,
                                       scalar2=None, op0=ALU.is_ge))
    em.drain(v)
    em.op(v, lambda e: e.scalar_tensor_tensor(
        out=q4, in0=q2, scalar=7.0, in1=q4, op0=ALU.is_ge, op1=ALU.add))
    em.drain(v)
    em.op(v, lambda e: e.scalar_tensor_tensor(
        out=q3, in0=q4, scalar=2.0, in1=q3, op0=ALU.mult, op1=ALU.add))
    em.drain(v)
    em.op(v, lambda e: e.tensor_scalar(out=q4, in0=q2, scalar1=10.0,
                                       scalar2=None, op0=ALU.is_ge))
    em.drain(v)
    em.op(v, lambda e: e.scalar_tensor_tensor(
        out=q3, in0=q4, scalar=4.0, in1=q3, op0=ALU.mult, op1=ALU.add),
        inc="vv")
    em.drain(v)
    em.wait("gpsimd", ("vv",))
    em.op("gpsimd", lambda e: e.tensor_tensor(out=q3v(q3, nb), in0=q3v(q3, nb),
                                              in1=bcast16(dq, nb), op=ALU.mult))
    em.drain("gpsimd")
    em.op("gpsimd", lambda e: e.tensor_tensor(out=dst_ap, in0=q3, in1=sg,
                                              op=ALU.mult), inc="vv")
    em.drain("gpsimd")


def emit_ln(em, ts, x_ap, xn_ap):
    v, a = "vector", "scalar"
    ln = ts["s_ln"]
    em.op(v, lambda e: e.bn_stats(out=ln[:, 0:6], in_=x_ap[:, 0:512]))
    em.op(v, lambda e: e.bn_stats(out=ln[:, 6:12], in_=x_ap[:, 512:1024]))
    em.drain(v)
    em.op(v, lambda e: e.bn_aggr(out=ln[:, 12:14], in_=ln[:, 0:12]))
    em.drain(v)
    em.op(v, lambda e: e.tensor_scalar(out=ln[:, 14:15], in0=ln[:, 13:14],
                                       scalar1=1e-5, scalar2=None, op0=ALU.add),
          inc="vv")
    em.drain(v)
    em.wait("gpsimd", ("vv",))
    em.op("gpsimd", lambda e: e.tensor_tensor(
        out=ln[:, 15:16], in0=ln[:, 14:15], in1=ts["s_mh"][:], op=ALU.pow),
        inc="gg")
    em.wait(v, ("gg",))
    em.op(v, lambda e: e.tensor_scalar(out=xn_ap, in0=x_ap, scalar1=ln[:, 12:13],
                                       scalar2=None, op0=ALU.subtract))
    em.drain(v)
    em.op(v, lambda e: e.tensor_scalar(out=xn_ap, in0=xn_ap, scalar1=ln[:, 15:16],
                                       scalar2=None, op0=ALU.mult), inc="vv")
    em.drain(v)



def phase_p0(em, ts):
    v, a, t, s, g = "vector", "scalar", "tensor", "sync", "gpsimd"
    dram = ts["dram"]
    nc_gs = ts["s_gs"]
    em.op(v, lambda e: e.memset(ts["s_mh"][:], -0.5))
    em.dma("qs", ts["s_msk"][:], dram["masks"][:, :])
    em.dma("qs", ts["s_nw"][:], dram["norm_w_rep"][:, :])
    em.dma("qs", nc_gs[:, 0:1], dram["s_in_rep"][:, :])
    em.dma("qs", nc_gs[:, 3:4], dram["s_in_down_rep"][:, :])
    em.dma("qs", ts["s_hsel"][:], dram["half_sel_rep"][:, :])
    cw3 = ts["s_cw"][:].rearrange("p (c i) -> p c i", i=4)
    em.dma("qs", cw3, dram["conv_w"][:, :].rearrange("(c p) i -> p c i", p=128))
    em.wait(v, ("qs",))
    em.op(v, lambda e: e.tensor_copy(out=ts["s_id"][:],
                                     in_=ts["s_msk"][:, 384:512]))
    for c in (0, 3):
        em.op(v, lambda e, c=c: e.tensor_scalar(out=nc_gs[:, c + 1:c + 2],
                                                in0=nc_gs[:, c:c + 1],
                                                scalar1=1.0 / 12.0, scalar2=None,
                                                op0=ALU.mult))
        em.op(v, lambda e, c=c: e.tensor_copy(out=nc_gs[:, c + 2:c + 3],
                                              in_=nc_gs[:, c:c + 1]))
        em.drain(v)
        em.op(v, lambda e, c=c: e.reciprocal(out=nc_gs[:, c + 2:c + 3],
                                             in_=nc_gs[:, c + 2:c + 3]))
        em.drain(v)
    em.op(v, lambda e: e.memset(nc_gs[:, 15:16], 0.0), inc="vv")
    em.drain(v)



def emit_wamax_pair(em, ts, srcA, srcB, sheetA, sheetB, par):
    v, s = "vector", "sync"
    if par == 0:
        slotA, slotB = ts["s_wsrc"][:], ts["s_wq1"][:]
        grp, glob, sem = ts["s_qam"], ts["s_qdq"][:, 0:1], "vv"
    else:
        slotA, slotB = ts["s_wq3"][:], ts["s_qsg"][:, 0:512]
        grp, glob, sem = ts["s_qs8"], ts["s_qrd"][:, 0:1], "gg"
    em.wait(s, (sem,))
    em.dma("qs", slotA, srcA)
    em.dma("qs", slotB, srcB)
    em.wait(v, ("qs",))
    em.op(v, lambda e: e.tensor_reduce(
        out=grp[:, 0:32], in_=q3v(slotA, 32), axis=AX.X,
        op=ALU.max, apply_absolute_value=True))
    em.op(v, lambda e: e.tensor_reduce(
        out=grp[:, 32:64], in_=q3v(slotB, 32), axis=AX.X,
        op=ALU.max, apply_absolute_value=True))
    em.drain(v)
    em.op(v, lambda e: e.tensor_reduce(
        out=glob, in_=grp[:, 0:64], axis=AX.X,
        op=ALU.max, apply_absolute_value=True))
    em.drain(v)
    em.op(v, lambda e: e.tensor_tensor(
        out=ts["s_gs"][:, 15:16], in0=ts["s_gs"][:, 15:16],
        in1=glob, op=ALU.max), inc=sem)
    em.drain(v)
    em.wait(s, (sem,))
    em.dma("qs", sheetA, grp[:, 0:32])
    em.dma("qs", sheetB, grp[:, 32:64])


def emit_wladder(em, ts, mi, gcol, p):
    v, s = "vector", "sync"
    gs = ts["s_gs"]
    dram = ts["dram"]
    src = ts["s_wsrc"][:]
    zt = ts["s_wq1"][:]
    pt = ts["s_wq3"][:]
    ct = ts["s_qsg"][:, 0:512]
    vt = ts["s_qsg"][:, 512:1024]
    if True:
        em.wait(s, ("vv", "gg"))
        em.dma("qs", src, dram["wsc_scratch"][mi, :, p * 512:(p + 1) * 512])
        em.wait(v, ("qs",))
        em.op(v, lambda e: e.tensor_scalar(out=zt, in0=src,
                                           scalar1=gs[:, gcol + 1:gcol + 2],
                                           scalar2=224.0, op0=ALU.mult,
                                           op1=ALU.min))
        em.op(v, lambda e: e.memset(pt, 2.0 ** -7))
        em.drain(v)
        for k in (8, 4, 2, 1):
            em.op(v, lambda e, k=k: e.tensor_scalar(out=ct, in0=pt,
                                                    scalar1=float(2 ** k),
                                                    scalar2=None, op0=ALU.mult))
            em.drain(v)
            em.op(v, lambda e: e.tensor_tensor(out=ct, in0=ct, in1=zt,
                                               op=ALU.is_le))
            em.drain(v)
            em.op(v, lambda e, k=k: e.tensor_scalar(out=ct, in0=ct,
                                                    scalar1=float(2 ** k - 1),
                                                    scalar2=1.0, op0=ALU.mult,
                                                    op1=ALU.add))
            em.drain(v)
            em.op(v, lambda e: e.tensor_tensor(out=pt, in0=pt, in1=ct,
                                               op=ALU.mult))
            em.drain(v)
        em.op(v, lambda e: e.reciprocal(out=ct, in_=pt))
        em.drain(v)
        em.op(v, lambda e: e.scalar_tensor_tensor(out=src, in0=zt, scalar=8.0,
                                                  in1=ct, op0=ALU.mult,
                                                  op1=ALU.mult))
        em.op(v, lambda e: e.memset(vt, 0.0))
        em.drain(v)
        for k in (16, 8, 4, 2, 1):
            em.op(v, lambda e, k=k: e.scalar_tensor_tensor(out=ct, in0=vt,
                                                           scalar=float(k) - 0.5,
                                                           in1=src, op0=ALU.add,
                                                           op1=ALU.is_le))
            em.drain(v)
            em.op(v, lambda e, k=k: e.scalar_tensor_tensor(out=vt, in0=ct,
                                                           scalar=float(k),
                                                           in1=vt, op0=ALU.mult,
                                                           op1=ALU.add))
            em.drain(v)
        em.op(v, lambda e: e.scalar_tensor_tensor(out=zt, in0=vt, scalar=0.125,
                                                  in1=pt, op0=ALU.mult,
                                                  op1=ALU.mult))
        em.drain(v)
        em.op(v, lambda e: e.tensor_scalar(out=pt, in0=zt,
                                           scalar1=gs[:, gcol + 2:gcol + 3],
                                           scalar2=None, op0=ALU.mult))
        em.op(v, lambda e: e.tensor_scalar(out=ct, in0=zt, scalar1=1e-30,
                                           scalar2=None, op0=ALU.max))
        em.drain(v)
        em.op(v, lambda e: e.reciprocal(out=ct, in_=ct))
        em.drain(v)
        em.op(v, lambda e: e.tensor_scalar(out=ct, in0=ct,
                                           scalar1=gs[:, gcol:gcol + 1],
                                           scalar2=None, op0=ALU.mult),
              inc="vv")
        em.drain(v)
        em.wait(s, ("vv",))
        em.dma("qs", dram["wdq_scratch"][mi, :, p * 512:(p + 1) * 512], pt)
        em.dma("qs", dram["wrd_scratch"][mi, :, p * 512:(p + 1) * 512], ct)


def emit_wbounce(em, ts, gcol):
    v, s = "vector", "sync"
    nc_gs = ts["s_gs"]
    em.wait(s, ("vv", "gg"))
    em.dma("qs", ts["dram"]["wmax_scratch"][:].rearrange("(p o) -> p o", o=1),
           nc_gs[:, 15:16])
    em.wait(s, ("qs",))
    em.dma("qs", ts["s_wlad"][0:1, 0:128],
           ts["dram"]["wmax_scratch"][:].rearrange("(o f) -> o f", o=1))
    em.wait(v, ("qs",))
    em.op(v, lambda e: e.tensor_reduce(out=ts["s_wlad"][0:1, 128:129],
                                       in_=ts["s_wlad"][0:1, 0:128], axis=AX.X,
                                       op=ALU.max, apply_absolute_value=True))
    em.drain(v)
    em.op(v, lambda e: e.tensor_copy(
        out=ts["s_wlad"][0:1, 0:128],
        in_=ts["s_wlad"][0:1, 128:129].broadcast_to((1, 128))), inc="vv")
    em.drain(v)
    em.wait(s, ("vv",))
    em.dma("qs", ts["dram"]["wmax_scratch"][:].rearrange("(o f) -> o f", o=1),
           ts["s_wlad"][0:1, 0:128])
    em.wait(s, ("qs",))
    em.dma("qs", nc_gs[:, gcol:gcol + 1],
           ts["dram"]["wmax_scratch"][:].rearrange("(p o) -> p o", o=1))
    em.wait(v, ("qs",))
    em.op(v, lambda e: e.reciprocal(out=nc_gs[:, gcol:gcol + 1],
                                    in_=nc_gs[:, gcol:gcol + 1]))
    em.drain(v)
    em.op(v, lambda e: e.tensor_scalar(out=nc_gs[:, gcol:gcol + 1],
                                       in0=nc_gs[:, gcol:gcol + 1],
                                       scalar1=2688.0, scalar2=None, op0=ALU.mult))
    em.drain(v)
    em.op(v, lambda e: e.tensor_scalar(out=nc_gs[:, gcol + 1:gcol + 2],
                                       in0=nc_gs[:, gcol:gcol + 1],
                                       scalar1=1.0 / 12.0, scalar2=None,
                                       op0=ALU.mult))
    em.op(v, lambda e: e.tensor_copy(out=nc_gs[:, gcol + 2:gcol + 3],
                                     in_=nc_gs[:, gcol:gcol + 1]))
    em.drain(v)
    em.op(v, lambda e: e.reciprocal(out=nc_gs[:, gcol + 2:gcol + 3],
                                    in_=nc_gs[:, gcol + 2:gcol + 3]))
    em.drain(v)
    em.op(v, lambda e: e.memset(nc_gs[:, 15:16], 0.0), inc="vv")
    em.drain(v)


def emit_wload(em, ts, mi, sidx, src_hbm):
    s = "sync"
    dram = ts["dram"]
    em.wait(s, ("wa",))
    em.dma("qs", ts["s_wsrc"][:], src_hbm)
    em.dma("qs", ts["s_qdq"][:, 0:32],
           dram["wdq_scratch"][mi, :, sidx * 32:(sidx + 1) * 32])
    em.dma("qs", ts["s_qrd"][:, 0:32],
           dram["wrd_scratch"][mi, :, sidx * 32:(sidx + 1) * 32])


def emit_wchain(em, ts, par):
    v, a, s, g = "vector", "scalar", "sync", "gpsimd"
    nb = 32
    q1, q3 = ts["s_wq1"][:], ts["s_wq3"][:]
    q2 = ts["s_qsg"][:, 0:512]
    sg = ts["s_qsg"][:, 512:1024]
    wnt = ts["s_xnb"][:, par * 512:(par + 1) * 512]
    dq, rd = ts["s_qdq"][:, 0:nb], ts["s_qrd"][:, 0:nb]
    src = ts["s_wsrc"][:]
    em.wait(g, ("qs", "wb"))
    em.op(g, lambda e: e.tensor_tensor(out=q3v(q1, nb), in0=q3v(src, nb),
                                       in1=bcast16(rd, nb), op=ALU.mult),
          inc="wa")
    em.wait(a, ("wa", "wc"))
    em.op(a, lambda e: e.activation(out=sg, in_=q1, func=AF.Sign))
    em.op(a, lambda e: e.activation(out=q2, in_=q1, func=AF.Abs), inc="wb")
    em.wait(v, ("wb", "wa"))
    em.op(v, lambda e: e.tensor_scalar(out=q3, in0=q2, scalar1=0.5,
                                       scalar2=None, op0=ALU.is_ge))
    em.drain(v)
    for th in (1.5, 2.5, 3.5):
        em.op(v, lambda e, th=th: e.scalar_tensor_tensor(
            out=q3, in0=q2, scalar=th, in1=q3, op0=ALU.is_ge, op1=ALU.add))
        em.drain(v)
    em.op(v, lambda e: e.tensor_scalar(out=q1, in0=q2, scalar1=5.0,
                                       scalar2=None, op0=ALU.is_ge))
    em.drain(v)
    em.op(v, lambda e: e.scalar_tensor_tensor(
        out=q1, in0=q2, scalar=7.0, in1=q1, op0=ALU.is_ge, op1=ALU.add))
    em.drain(v)
    em.op(v, lambda e: e.scalar_tensor_tensor(
        out=q3, in0=q1, scalar=2.0, in1=q3, op0=ALU.mult, op1=ALU.add))
    em.drain(v)
    em.op(v, lambda e: e.tensor_scalar(out=q1, in0=q2, scalar1=10.0,
                                       scalar2=None, op0=ALU.is_ge))
    em.drain(v)
    em.op(v, lambda e: e.scalar_tensor_tensor(
        out=q3, in0=q1, scalar=4.0, in1=q3, op0=ALU.mult, op1=ALU.add),
        inc="wc")
    em.drain(v)
    em.wait(g, ("wc", "wd"))
    em.op(g, lambda e: e.tensor_tensor(out=q3v(q3, nb), in0=q3v(q3, nb),
                                       in1=bcast16(dq, nb), op=ALU.mult))
    em.drain(g)
    em.op(g, lambda e: e.tensor_tensor(out=wnt, in0=q3, in1=sg,
                                       op=ALU.mult), inc="wa")
    em.drain(g)


def emit_wtrans(em, ts, par, dests):
    a, t, s = "scalar", "tensor", "sync"
    base = par * 512
    em.wait(t, ("wa", "wc"))
    for j in range(4):
        em.op(t, lambda e, j=j: e.matmul(
            out=ts["p_B"][:, j * 128:(j + 1) * 128],
            lhsT=ts["s_xnb"][:, base + j * 128:base + (j + 1) * 128],
            rhs=ts["s_id"][:], start=(j == 0), stop=(j == 3)),
            inc="wb" if j == 3 else None)
    em.wait(a, ("wb", "wd"))
    em.op(a, lambda e: e.copy(out=ts["s_xnb"][:, base:base + 512],
                              in_=ts["p_B"][:, 0:512]), inc="wc")
    em.wait(a, ("wc",))
    for pj in range(4):
        em.dma("qt", dests[pj],
               ts["s_xnb"][:, base + pj * 128:base + (pj + 1) * 128],
               sem="wd")
    em.wait(t, ("wc",))



def phase_f(em, ts):
    v, a, t, s = "vector", "scalar", "tensor", "sync"
    dram = ts["dram"]
    XO = ts["XO"]
    gs = ts["s_gs"]
    nbF = 64
    scr = ts["s_gf"]
    lzt, lpt, lct, lvt, lut = (scr[:, 256:320], scr[:, 320:384],
                               scr[:, 384:448], scr[:, 448:512],
                               scr[:, 512:576])
    def fsmalls(par):
        if par == 0:
            return (ts["s_qam"][:, 0:64], ts["s_qs8"][:, 0:64],
                    ts["s_qdq"][:, 0:64], ts["s_qrd"][:, 0:64])
        return (scr[:, 0:64], scr[:, 64:128], scr[:, 128:192], scr[:, 192:256])
    def fhbuf(par):
        return ts["s_hid"][:] if par == 0 else ts["s_q4"][:]

    def emit_fladder(k, gcol=3):
        par = k % 2
        am, s8, dq, rd = fsmalls(par)
        hb = fhbuf(par)
        em.wait(v, ("vv", "wb"))
        em.op(v, lambda e: e.tensor_reduce(out=am, in_=q3v(hb, nbF), axis=AX.X,
                                           op=ALU.max, apply_absolute_value=True))
        em.drain(v)
        em.op(v, lambda e: e.tensor_scalar(out=lzt, in0=am,
                                           scalar1=gs[:, gcol + 1:gcol + 2], scalar2=224.0,
                                           op0=ALU.mult, op1=ALU.min))
        em.op(v, lambda e: e.memset(lpt, 2.0 ** -7))
        em.drain(v)
        for kk in (8, 4, 2, 1):
            em.op(v, lambda e, kk=kk: e.tensor_scalar(out=lct, in0=lpt,
                                                      scalar1=float(2 ** kk),
                                                      scalar2=None, op0=ALU.mult))
            em.drain(v)
            em.op(v, lambda e: e.tensor_tensor(out=lct, in0=lct, in1=lzt,
                                               op=ALU.is_le))
            em.drain(v)
            em.op(v, lambda e, kk=kk: e.tensor_scalar(out=lct, in0=lct,
                                                      scalar1=float(2 ** kk - 1),
                                                      scalar2=1.0, op0=ALU.mult,
                                                      op1=ALU.add))
            em.drain(v)
            em.op(v, lambda e: e.tensor_tensor(out=lpt, in0=lpt, in1=lct,
                                               op=ALU.mult))
            em.drain(v)
        em.op(v, lambda e: e.reciprocal(out=lct, in_=lpt))
        em.drain(v)
        em.op(v, lambda e: e.scalar_tensor_tensor(out=lut, in0=lzt, scalar=8.0,
                                                  in1=lct, op0=ALU.mult,
                                                  op1=ALU.mult))
        em.op(v, lambda e: e.memset(lvt, 0.0))
        em.drain(v)
        for kk in (16, 8, 4, 2, 1):
            em.op(v, lambda e, kk=kk: e.scalar_tensor_tensor(
                out=lct, in0=lvt, scalar=float(kk) - 0.5, in1=lut,
                op0=ALU.add, op1=ALU.is_le))
            em.drain(v)
            em.op(v, lambda e, kk=kk: e.scalar_tensor_tensor(
                out=lvt, in0=lct, scalar=float(kk), in1=lvt,
                op0=ALU.mult, op1=ALU.add))
            em.drain(v)
        em.op(v, lambda e: e.scalar_tensor_tensor(out=s8, in0=lvt, scalar=0.125,
                                                  in1=lpt, op0=ALU.mult,
                                                  op1=ALU.mult))
        em.drain(v)
        em.op(v, lambda e: e.tensor_scalar(out=dq, in0=s8,
                                           scalar1=gs[:, gcol + 2:gcol + 3],
                                           scalar2=None, op0=ALU.mult))
        em.op(v, lambda e: e.tensor_scalar(out=rd, in0=s8, scalar1=1e-30,
                                           scalar2=None, op0=ALU.max))
        em.drain(v)
        em.op(v, lambda e: e.reciprocal(out=rd, in_=rd))
        em.drain(v)
        em.op(v, lambda e: e.tensor_scalar(out=rd, in0=rd,
                                           scalar1=gs[:, gcol:gcol + 1],
                                           scalar2=None, op0=ALU.mult),
              inc="wa")
        em.drain(v)

    def emit_fval_head(k):
        par = k % 2
        am, s8, dq, rd = fsmalls(par)
        hb = fhbuf(par)
        q1 = ts["s_q1"][:]
        q2, sgn = ts["s_q2"][:], ts["s_qsg"][:]
        em.wait("gpsimd", ("wa", "wd", "vv"))
        em.op("gpsimd", lambda e: e.tensor_tensor(
            out=q3v(q1, nbF), in0=q3v(hb, nbF), in1=bcast16(rd, nbF),
            op=ALU.mult), inc="wb")
        em.wait(a, ("wb", "vv"))
        em.op(a, lambda e: e.activation(out=sgn, in_=q1, func=AF.Sign))
        em.op(a, lambda e: e.activation(out=q2, in_=q1, func=AF.Abs), inc="wc")

    def emit_fval_tail(k):
        par = k % 2
        am, s8, dq, rd = fsmalls(par)
        q1, q2, q3 = ts["s_q1"][:], ts["s_q2"][:], ts["s_q3"][:]
        sgn = ts["s_qsg"][:]
        em.wait(v, ("wc",))
        em.op(v, lambda e: e.tensor_scalar(out=q3, in0=q2, scalar1=0.5,
                                           scalar2=None, op0=ALU.is_ge))
        em.drain(v)
        for th in (1.5, 2.5, 3.5):
            em.op(v, lambda e, th=th: e.scalar_tensor_tensor(
                out=q3, in0=q2, scalar=th, in1=q3, op0=ALU.is_ge, op1=ALU.add))
            em.drain(v)
        em.op(v, lambda e: e.tensor_scalar(out=q1, in0=q2, scalar1=5.0,
                                           scalar2=None, op0=ALU.is_ge))
        em.drain(v)
        em.op(v, lambda e: e.scalar_tensor_tensor(
            out=q1, in0=q2, scalar=7.0, in1=q1, op0=ALU.is_ge, op1=ALU.add))
        em.drain(v)
        em.op(v, lambda e: e.scalar_tensor_tensor(
            out=q3, in0=q1, scalar=2.0, in1=q3, op0=ALU.mult, op1=ALU.add))
        em.drain(v)
        em.op(v, lambda e: e.tensor_scalar(out=q1, in0=q2, scalar1=10.0,
                                           scalar2=None, op0=ALU.is_ge))
        em.drain(v)
        em.op(v, lambda e: e.scalar_tensor_tensor(
            out=q3, in0=q1, scalar=4.0, in1=q3, op0=ALU.mult, op1=ALU.add),
            inc="wd")
        em.drain(v)
        em.wait("gpsimd", ("wd", "qs", "qt"))
        em.op("gpsimd", lambda e: e.tensor_tensor(out=q3v(q3, nbF),
                                                  in0=q3v(q3, nbF),
                                                  in1=bcast16(dq, nbF),
                                                  op=ALU.mult))
        em.drain("gpsimd")
        em.op("gpsimd", lambda e: e.tensor_tensor(out=ts["s_hqb"][:], in0=q3,
                                                  in1=sgn, op=ALU.mult),
              inc="vv")
        em.drain("gpsimd")

    for tt in range(NTO):
        em.wait(s, ("vv",))
        em.dma("qs", ts["s_xt"][:], XO[tt * 128:(tt + 1) * 128, :])
        em.wait(v, ("qs", "wb"))
        emit_ln(em, ts, ts["s_xt"][:], fhbuf(tt % 2))
        if tt > 0:
            emit_fval_head(tt - 1)
        emit_fladder(tt, 0)
        if tt > 0:
            emit_fval_tail(tt - 1)
            em.wait("scalar", ("vv",))
            em.dmaT(ts["s_xqT"][:].rearrange("p (hch t) -> p hch t", hch=8)[
                :, :, (tt - 1) * 128: tt * 128], ts["s_hqb"][:, 0:1024])
    emit_fval_head(NTO - 1)
    emit_fval_tail(NTO - 1)
    em.wait("scalar", ("vv",))
    em.dmaT(ts["s_xqT"][:].rearrange("p (hch t) -> p hch t", hch=8)[
        :, :, (NTO - 1) * 128: NTO * 128], ts["s_hqb"][:, 0:1024])
    em.wait(t, ("qt",))
    em.wait(v, ("qt",))
    for oc in range(4):
        em.wait(s, ("tt",))
        for ch in range(8):
            em.dma("qs", ts["s_wgc"][:, ch * 1024:(ch + 1) * 1024],
                   dram["wgT_hbm"][ch, :, oc * 1024:(oc + 1) * 1024])
            em.dma("qs", ts["s_wuc"][:, ch * 1024:(ch + 1) * 1024],
                   dram["wuT_hbm"][ch, :, oc * 1024:(oc + 1) * 1024])
        em.wait(t, ("qs",))
        for tt in range(NTO):
            em.wait(t, ("qt", "aa"))
            for ch in range(8):
                for hf in range(2):
                    em.op(t, lambda e, ch=ch, tt=tt, hf=hf: e.matmul(
                        out=ts["p_A"][:, hf * 512:(hf + 1) * 512],
                        lhsT=ts["s_xqT"][:, ch * 2048 + tt * 128: ch * 2048 + (tt + 1) * 128],
                        rhs=ts["s_wgc"][:, ch * 1024 + hf * 512: ch * 1024 + (hf + 1) * 512],
                        start=(ch == 0), stop=(ch == 7)))
            for ch in range(8):
                for hf in range(2):
                    em.op(t, lambda e, ch=ch, tt=tt, hf=hf: e.matmul(
                        out=ts["p_D"][:, hf * 512:(hf + 1) * 512],
                        lhsT=ts["s_xqT"][:, ch * 2048 + tt * 128: ch * 2048 + (tt + 1) * 128],
                        rhs=ts["s_wuc"][:, ch * 1024 + hf * 512: ch * 1024 + (hf + 1) * 512],
                        start=(ch == 0), stop=(ch == 7)),
                        inc="tt" if (ch == 7 and hf == 1) else None)
            em.wait(a, ("tt",))
            em.op(a, lambda e: e.activation(out=ts["s_sg"][:], in_=ts["p_A"][:],
                                            func=AF.Silu), inc="aa")
            em.wait(v, ("aa", "wb"))
            em.op(v, lambda e, tt=tt: e.tensor_tensor(
                out=fhbuf(tt % 2), in0=ts["s_sg"][:], in1=ts["p_D"][:],
                op=ALU.mult), inc="qt")
            em.drain(v)
            if tt > 0:
                emit_fval_head(tt - 1)
            emit_fladder(tt)
            if tt > 0:
                emit_fval_tail(tt - 1)
                em.wait(s, ("vv",))
                em.dma("qs", dram["hq_scratch"][(tt - 1) * 128:tt * 128,
                                                oc * 1024:(oc + 1) * 1024],
                       ts["s_hqb"][:])
        emit_fval_head(NTO - 1)
        emit_fval_tail(NTO - 1)
        em.wait(s, ("vv",))
        em.dma("qs", dram["hq_scratch"][(NTO - 1) * 128:NTO * 128,
                                        oc * 1024:(oc + 1) * 1024],
               ts["s_hqb"][:])
    em.wait(s, ("qt",))
    for ic in range(32):
        em.dma("qs", ts["s_wT"][:, ic * 1024:(ic + 1) * 1024],
               dram["wdT_hbm"][ic, :, :])
    em.wait(t, ("qs",))
    for tb in range(4):
        em.wait("scalar", ("qs",))
        outT = ts["s_hqT"][:].rearrange("p (ic r) -> p ic r", r=512)
        em.dmaT(outT, dram["hq_scratch"][tb * 512:(tb + 1) * 512, :])
        em.wait(t, ("qt",))
        for tsub in range(4):
            tt = tb * 4 + tsub
            em.wait(t, ("aa",))
            for ic in range(32):
                for half in range(2):
                    em.op(t, lambda e, ic=ic, half=half, tsub=tsub: e.matmul(
                        out=ts["p_A"][:, half * 512:(half + 1) * 512],
                        lhsT=ts["s_hqT"][:, ic * 512 + tsub * 128: ic * 512 + (tsub + 1) * 128],
                        rhs=ts["s_wT"][:, ic * 1024 + half * 512: ic * 1024 + (half + 1) * 512],
                        start=(ic == 0), stop=(ic == 31)),
                        inc="tt" if (ic == 31 and half == 1) else None)
            em.wait(a, ("tt", "qs", "vv"))
            em.op(a, lambda e: e.activation(out=ts["s_gf"][:], in_=ts["p_A"][:],
                                            func=AF.Silu), inc="aa")
            em.wait(s, ("aa", "vv"))
            em.dma("qs", ts["s_q1"][:], dram["o_woA"][tt * 128:(tt + 1) * 128, :])
            em.dma("qs", ts["s_q2"][:], dram["o_woB"][tt * 128:(tt + 1) * 128, :])
            em.dma("qs", ts["s_xt"][:], XO[tt * 128:(tt + 1) * 128, :])
            em.wait(v, ("aa", "qs"))
            em.op(v, lambda e: e.tensor_scalar(
                out=ts["s_q1"][:], in0=ts["s_q1"][:],
                scalar1=ts["s_hsel"][:, 0:1], scalar2=None, op0=ALU.mult))
            em.op(v, lambda e: e.tensor_scalar(
                out=ts["s_q2"][:], in0=ts["s_q2"][:],
                scalar1=ts["s_hsel"][:, 1:2], scalar2=None, op0=ALU.mult))
            em.drain(v)
            em.op(v, lambda e: e.tensor_tensor(out=ts["s_q1"][:], in0=ts["s_q1"][:],
                                               in1=ts["s_q2"][:], op=ALU.add))
            em.drain(v)
            em.op(v, lambda e: e.tensor_tensor(out=ts["s_q3"][:], in0=ts["s_q1"][:],
                                               in1=ts["s_gf"][:], op=ALU.mult))
            em.drain(v)
            em.op(v, lambda e: e.tensor_tensor(out=ts["s_q3"][:], in0=ts["s_q3"][:],
                                               in1=ts["s_xt"][:], op=ALU.add),
                  inc="vv")
            em.drain(v)
            em.wait(s, ("vv",))
            em.dma("qs", ts["OUT"][tt * 128:(tt + 1) * 128, :], ts["s_q3"][:])



def phase_a(em, ts):
    v, a, t, s = "vector", "scalar", "tensor", "sync"
    dram = ts["dram"]
    X = ts["X"]
    msk = ts["s_msk"]
    trilS = msk[:, 0:64]
    triuI = msk[:, 64:128]
    id4 = msk[:, 128:384]
    LP = L + 16

    GWh, UWh, DWh = dram["gate_w"], dram["up_w"], dram["down_w"]
    WGTh, WUTh, WDTh = dram["wgT_hbm"], dram["wuT_hbm"], dram["wdT_hbm"]
    WSCh = dram["wsc_scratch"]
    amax_tasks = []
    for mi, (W, gcol) in enumerate(((GWh, 6), (UWh, 9))):
        for pp in range(32):
            amax_tasks.append(lambda W=W, mi=mi, pp=pp: emit_wamax_pair(
                em, ts,
                W[pp * 128:(pp + 1) * 128, 0:512],
                W[pp * 128:(pp + 1) * 128, 512:1024],
                WSCh[mi, :, 64 * pp:64 * pp + 32],
                WSCh[mi, :, 64 * pp + 32:64 * pp + 64], pp % 2))
        amax_tasks.append(lambda gcol=gcol: emit_wbounce(em, ts, gcol))
        for p in range(4):
            amax_tasks.append(lambda mi=mi, gcol=gcol, p=p:
                              emit_wladder(em, ts, mi, gcol, p))
    for rc in range(8):
        for j in range(4):
            amax_tasks.append(lambda rc=rc, j=j: emit_wamax_pair(
                em, ts,
                DWh[rc * 128:(rc + 1) * 128, 1024 * j:1024 * j + 512],
                DWh[rc * 128:(rc + 1) * 128, 1024 * j + 512:1024 * (j + 1)],
                WSCh[2, :, 256 * rc + 64 * j:256 * rc + 64 * j + 32],
                WSCh[2, :, 256 * rc + 64 * j + 32:256 * rc + 64 * (j + 1)],
                (4 * rc + j) % 2))
    amax_tasks.append(lambda: emit_wbounce(em, ts, 12))
    for p in range(4):
        amax_tasks.append(lambda p=p: emit_wladder(em, ts, 2, 12, p))
    quant_tasks = []
    for mi, (W, WT) in enumerate(((GWh, WGTh), (UWh, WUTh))):
        for pp in range(32):
            for h in range(2):
                dests = [WT[4 * h + j, :, pp * 128:(pp + 1) * 128]
                         for j in range(4)]
                quant_tasks.append(
                    (mi, 2 * pp + h,
                     W[pp * 128:(pp + 1) * 128, h * 512:(h + 1) * 512], dests))
    for rc in range(8):
        for cp in range(8):
            dests = [WDTh[4 * cp + j, :, rc * 128:(rc + 1) * 128]
                     for j in range(4)]
            quant_tasks.append(
                (2, 8 * rc + cp,
                 DWh[rc * 128:(rc + 1) * 128, cp * 512:(cp + 1) * 512], dests))
    n_amax = len(amax_tasks)
    n_quant = len(quant_tasks)

    for wname, sbname, dout in [("Wq", "s_Wq", 512), ("Wk", "s_Wk", 512),
                                ("Wv", "s_Wv", 1024), ("Wb", "s_Wb", 4),
                                ("Wg", "s_Wg", 1024), ("Wo", "s_Wo", 1024)]:
        for ch in range(8):
            em.wait(s, ("vv",))
            em.dma("qs", ts["s_xt"][:, 0:dout],
                   dram[wname][ch * 128:(ch + 1) * 128, :])
            em.wait(v, ("qs",))
            em.op(v, lambda e, sbname=sbname, ch=ch, dout=dout: e.tensor_copy(
                out=ts[sbname][:, ch * dout:(ch + 1) * dout],
                in_=ts["s_xt"][:, 0:dout]), inc="vv")
            em.drain(v)

    em.op(v, lambda e: e.memset(ts["s_Sbf"][:], 0.0))
    em.op(v, lambda e: e.memset(ts["s_Sf"][:], 0.0))
    em.op(v, lambda e: e.memset(ts["s_mh"][:], -0.5), inc="vv")
    em.wait("gpsimd", ("vv",))
    for ch in range(8):
        em.op(v, lambda e, ch=ch: e.memset(ts["s_xnT"][:, ch * LP:ch * LP + 16], 0.0))
    em.op(v, lambda e: e.memset(ts["s_uf"][:], 0.0), inc="vv")

    adone = 0
    for tt in range(NT):
        em.wait(s, ("vv",))
        em.dma("qs", ts["s_xt"][:], X[tt * 128:(tt + 1) * 128, :])
        em.wait(v, ("qs",))
        emit_ln(em, ts, ts["s_xt"][:], ts["s_xn"][:])
        em.wait(v, ("qt",))
        em.op(v, lambda e: e.tensor_copy(
            out=ts["s_xnb"][:, 0:1024], in_=ts["s_xn"][:]), inc="vv")
        em.drain(v)
        em.wait("scalar", ("vv",))
        nt3 = ts["s_xnT"][:].rearrange("p (dch t) -> p dch t", dch=8)[
            :, :, 16 + tt * 128: 16 + tt * 128 + 128]
        em.dmaT(nt3, ts["s_xnb"][:, 0:1024])
        want = (tt + 1) * n_amax // NT
        while adone < want:
            amax_tasks[adone]()
            adone += 1
    em.wait(v, ("qt",))
    em.wait(t, ("qt",))

    qp = {"ld": 0, "ch": 0, "tr": 0}
    def qstep(lim):
        if qp["tr"] < qp["ch"] and (qp["ch"] - qp["tr"] >= 2 or
                                    qp["ch"] >= min(n_quant, lim)):
            mi, sidx, src, dests = quant_tasks[qp["tr"]]
            emit_wtrans(em, ts, qp["tr"] % 2, dests)
            qp["tr"] += 1
        if qp["ch"] < qp["ld"] and qp["ch"] < min(n_quant, lim):
            emit_wchain(em, ts, qp["ch"] % 2)
            qp["ch"] += 1
        if qp["ld"] < min(n_quant, lim + 1) and qp["ld"] <= qp["ch"]:
            mi, sidx, src, dests = quant_tasks[qp["ld"]]
            emit_wload(em, ts, mi, sidx, src)
            qp["ld"] += 1
    for tt in range(NT):
        cw3 = ts["s_cw"][:].rearrange("p (c i) -> p c i", i=4)
        base = 13 + tt * 128
        def xv(i, base=base):
            return ts["s_xnT"][:].rearrange("p (c t) -> p c t", c=8)[
                :, :, base + i: base + i + 128]
        q13 = ts["s_q1"][:].rearrange("p (c t) -> p c t", t=128)
        q23 = ts["s_q2"][:].rearrange("p (c t) -> p c t", t=128)
        em.wait(v, ("aa", "qs"))
        em.op(v, lambda e: e.tensor_tensor(
            out=q13, in0=xv(0), in1=cw3[:, :, 0:1].broadcast_to((128, 8, 128)),
            op=ALU.mult))
        for i in (1, 2, 3):
            em.op(v, lambda e, i=i: e.tensor_tensor(
                out=q23, in0=xv(i), in1=cw3[:, :, i:i + 1].broadcast_to((128, 8, 128)),
                op=ALU.mult))
            em.drain(v)
            em.op(v, lambda e: e.tensor_tensor(out=ts["s_q1"][:], in0=ts["s_q1"][:],
                                               in1=ts["s_q2"][:], op=ALU.add))
            em.drain(v)
        em.drain(v)
        em.op(v, lambda e: e.nop(), inc="vv")
        em.wait(a, ("vv",))
        em.op(a, lambda e: e.activation(out=ts["s_hT"][:], in_=ts["s_q1"][:],
                                        func=AF.Silu), inc="aa")
        em.wait(t, ("aa", "vv"))

        for ch in range(8):
            hsl = ts["s_hT"][:, ch * 128:(ch + 1) * 128]
            em.op(t, lambda e, ch=ch, hsl=hsl: e.matmul(
                out=ts["p_A"][:, 0:512], lhsT=hsl,
                rhs=ts["s_Wq"][:, ch * 512:(ch + 1) * 512],
                start=(ch == 0), stop=(ch == 7)))
            em.op(t, lambda e, ch=ch, hsl=hsl: e.matmul(
                out=ts["p_A"][:, 512:1024], lhsT=hsl,
                rhs=ts["s_Wk"][:, ch * 512:(ch + 1) * 512],
                start=(ch == 0), stop=(ch == 7)))
            em.op(t, lambda e, ch=ch, hsl=hsl: e.matmul(
                out=ts["p_D"][:, 0:4], lhsT=hsl,
                rhs=ts["s_Wb"][:, ch * 4:(ch + 1) * 4],
                start=(ch == 0), stop=(ch == 7)), inc="tt" if ch == 7 else None)
        em.wait(a, ("tt",))
        st = ts["s_st"]
        em.wait(v, ("tt", "qt"))
        em.op(v, lambda e: e.tensor_copy(out=ts["s_qk"][:], in_=ts["p_A"][:]),
              inc="vv")
        em.drain(v)
        em.wait(a, ("vv",))
        for h in range(8):
            em.op(a, lambda e, h=h: e.activation(
                out=ts["s_q3"][:, h * 128:(h + 1) * 128],
                in_=ts["s_qk"][:, h * 128:(h + 1) * 128],
                func=AF.Square, accum_out=st[:, 4 + h if h < 4 else h - 4:
                                             5 + h if h < 4 else h - 3]))
        em.op(a, lambda e: e.activation(out=ts["s_beta"][:], in_=ts["p_D"][:, 0:4],
                                        func=AF.Sigmoid), inc="aa")
        em.wait("scalar", ("vv", "tt"))
        em.dmaT(ts["s_qkT"][:].rearrange("p (c r) -> p c r", r=128), ts["s_qk"][:])
        qstep(8 * (tt + 1))
        em.wait("gpsimd", ("aa",))
        em.op("gpsimd", lambda e: e.tensor_scalar(out=st[:, 8:16], in0=st[:, 0:8],
                                                  scalar1=1e-24, scalar2=None,
                                                  op0=ALU.max))
        em.drain("gpsimd")
        em.op("gpsimd", lambda e: e.tensor_tensor(
            out=st[:, 8:16], in0=st[:, 8:16],
            in1=ts["s_mh"][:].broadcast_to((128, 8)), op=ALU.pow))
        em.drain("gpsimd")
        em.op("gpsimd", lambda e: e.tensor_scalar(out=st[:, 8:16], in0=st[:, 8:16],
                                                  scalar1=1e12, scalar2=None,
                                                  op0=ALU.min))
        em.drain("gpsimd")
        em.op("gpsimd", lambda e: e.tensor_tensor(out=st[:, 16:20], in0=st[:, 8:12],
                                                  in1=ts["s_beta"][:], op=ALU.mult))
        em.drain("gpsimd")
        em.op("gpsimd", lambda e: e.tensor_copy(out=st[:, 20:24], in_=st[:, 16:20]))
        em.drain("gpsimd")
        em.op("gpsimd", lambda e: e.tensor_tensor(out=st[:, 16:20], in0=st[:, 16:20],
                                                  in1=st[:, 8:12], op=ALU.mult),
              inc="gg")
        em.wait(v, ("gg",))
        em.wait(t, ("gg", "qt"))
        qstep(8 * (tt + 1))
        for ch in range(8):
            hsl = ts["s_hT"][:, ch * 128:(ch + 1) * 128]
            for half in range(2):
                em.op(t, lambda e, ch=ch, hsl=hsl, half=half: e.matmul(
                    out=ts["p_A"][:, half * 512:(half + 1) * 512], lhsT=hsl,
                    rhs=ts["s_Wv"][:, ch * 1024 + half * 512: ch * 1024 + (half + 1) * 512],
                    start=(ch == 0), stop=(ch == 7)),
                    inc="tt" if (ch == 7 and half == 1) else None)
        em.wait(v, ("tt",))
        em.op(v, lambda e: e.tensor_copy(out=ts["s_v"][:], in_=ts["p_A"][:]), inc="vv")
        em.wait(t, ("vv",))
        for ch in range(8):
            hsl = ts["s_hT"][:, ch * 128:(ch + 1) * 128]
            for half in range(2):
                em.op(t, lambda e, ch=ch, hsl=hsl, half=half: e.matmul(
                    out=ts["p_A"][:, half * 512:(half + 1) * 512], lhsT=hsl,
                    rhs=ts["s_Wg"][:, ch * 1024 + half * 512: ch * 1024 + (half + 1) * 512],
                    start=(ch == 0), stop=(ch == 7)),
                    inc="tt" if (ch == 7 and half == 1) else None)
        em.wait(v, ("tt",))
        em.op(v, lambda e: e.tensor_copy(out=ts["s_g"][:], in_=ts["p_A"][:]), inc="vv")
        em.drain(v)
        qstep(8 * (tt + 1))

        for c in range(2):
            hb = slice(64 * c, 64 * c + 64)
            tpK = 64 * c
            emit_chunk(em, ts, tt, c, hb, tpK, trilS, triuI, id4)
            qstep(8 * (tt + 1))

        st = ts["s_st"]
        em.wait(a, ("vv", "qs"))
        for h in range(4):
            em.op(a, lambda e, h=h: e.activation(
                out=ts["s_q2"][:, h * 256:(h + 1) * 256],
                in_=ts["s_o"][:, h * 256:(h + 1) * 256],
                func=AF.Square, accum_out=st[:, h:h + 1]))
        em.op(a, lambda e: e.activation(out=ts["s_q4"][:], in_=ts["s_g"][:],
                                        func=AF.Silu), inc="aa")
        em.wait("gpsimd", ("aa",))
        em.op("gpsimd", lambda e: e.tensor_scalar(out=st[:, 4:8], in0=st[:, 0:4],
                                                  scalar1=1.0 / 256.0, scalar2=1e-5,
                                                  op0=ALU.mult, op1=ALU.add))
        em.drain("gpsimd")
        em.op("gpsimd", lambda e: e.tensor_tensor(
            out=st[:, 4:8], in0=st[:, 4:8],
            in1=ts["s_mh"][:].broadcast_to((128, 4)), op=ALU.pow), inc="gg")
        em.wait(v, ("gg", "aa"))
        o4 = ts["s_o"][:].rearrange("p (h f) -> p h f", f=256)
        rs4 = st[:, 4:8].rearrange("p (h o) -> p h o", o=1).broadcast_to((128, 4, 256))
        em.op(v, lambda e: e.tensor_tensor(out=o4, in0=o4, in1=rs4, op=ALU.mult))
        em.drain(v)
        nw4 = ts["s_nw"][:].rearrange("p (o f) -> p o f", o=1).broadcast_to((128, 4, 256))
        em.op(v, lambda e: e.tensor_tensor(out=o4, in0=o4, in1=nw4, op=ALU.mult))
        em.drain(v)
        em.wait(v, ("qt",))
        em.op(v, lambda e: e.tensor_tensor(out=ts["s_gtd"][:], in0=ts["s_o"][:],
                                           in1=ts["s_q4"][:], op=ALU.mult), inc="vv")
        em.drain(v)
        qstep(8 * (tt + 1))
        em.wait("scalar", ("vv",))
        outT = ts["s_gTt"][:].rearrange("p (c r) -> p c r", r=128)
        em.dmaT(outT, ts["s_gtd"][:])
        em.wait(t, ("qt", "aa", "vv"))
        for ch in range(8):
            for half in range(2):
                em.op(t, lambda e, ch=ch, half=half: e.matmul(
                    out=ts["p_A"][:, half * 512:(half + 1) * 512],
                    lhsT=ts["s_gTt"][:, ch * 128:(ch + 1) * 128],
                    rhs=ts["s_Wo"][:, ch * 1024 + half * 512: ch * 1024 + (half + 1) * 512],
                    start=(ch == 0), stop=(ch == 7)),
                    inc="tt" if (ch == 7 and half == 1) else None)
        em.wait(v, ("tt", "qs"))
        em.op(v, lambda e: e.tensor_copy(out=ts["s_q2"][:], in_=ts["p_A"][:]),
              inc="vv")
        em.drain(v)
        em.wait(s, ("vv",))
        rA = tt if tt < NTO else NTO
        rB = tt - NTO if tt >= NTO else NTO
        em.dma("qs", dram["o_woA"][rA * 128:(rA + 1) * 128, :], ts["s_q2"][:])
        em.dma("qs", dram["o_woB"][rB * 128:(rB + 1) * 128, :], ts["s_q2"][:])
        while qp["ch"] < min(n_quant, (tt + 1) * 8) or \
                (tt == NT - 1 and qp["tr"] < n_quant):
            qstep((tt + 1) * 8)


def emit_chunk(em, ts, tt, c, hb, tpK, trilS, triuI, id4):
    v, a, t = "vector", "scalar", "tensor"
    st = ts["s_st"]
    tok0 = 64 * c
    tpO = (0, 64 * c)
    tpKK = (64 * c, 64 * c)

    em.wait(t, ("vv", "aa"))
    for h in range(4):
        kTc = ts["s_qkT"][:, 512 + h * 128 + tok0: 512 + h * 128 + tok0 + 64]
        em.op(t, lambda e, h=h, kTc=kTc: e.matmul(
            out=ts["p_C"][hb, h * 64:(h + 1) * 64], lhsT=kTc, rhs=kTc,
            start=True, stop=True, tile_position=tpO),
            inc="tt" if h == 3 else None)
    em.wait(v, ("tt",))
    g3 = ts["p_C"][hb, :].rearrange("p (h f) -> p h f", f=64)
    tril3 = trilS[hb, :].rearrange("p (o f) -> p o f", o=1).broadcast_to((64, 4, 64))
    em.op(v, lambda e: e.tensor_tensor(out=ts["s_C"][hb, :].rearrange(
        "p (h f) -> p h f", f=64), in0=g3, in1=tril3, op=ALU.mult))
    em.drain(v)
    s3 = st[hb, 16:20].rearrange("p (h o) -> p h o", o=1).broadcast_to((64, 4, 64))
    em.op(v, lambda e: e.tensor_tensor(out=ts["s_dg"][hb, :].rearrange(
        "p (h f) -> p h f", f=64), in0=id4[hb, :].rearrange("p (h f) -> p h f", f=64),
        in1=s3, op=ALU.mult))
    em.op(v, lambda e: e.tensor_tensor(out=ts["s_At"][hb, :].rearrange(
        "p (h f) -> p h f", f=64), in0=ts["s_C"][hb, :].rearrange(
        "p (h f) -> p h f", f=64), in1=s3, op=ALU.mult), inc="vv")
    em.drain(v)
    em.wait(t, ("vv",))
    for h in range(4):
        em.op(t, lambda e, h=h: e.matmul(
            out=ts["p_B"][hb, h * 64:(h + 1) * 64],
            lhsT=ts["s_C"][hb, h * 64:(h + 1) * 64],
            rhs=ts["s_dg"][hb, h * 64:(h + 1) * 64],
            start=True, stop=True, tile_position=tpKK),
            inc="tt" if h == 3 else None)
    em.wait(v, ("tt",))
    em.op(v, lambda e: e.tensor_copy(out=ts["s_AtT"][hb, :], in_=ts["p_B"][hb, 0:256]),
          inc="vv")
    em.drain(v)
    em.wait(t, ("vv",))
    for h in range(4):
        kTc = ts["s_qkT"][:, 512 + h * 128 + tok0: 512 + h * 128 + tok0 + 64]
        em.op(t, lambda e, h=h, kTc=kTc: e.matmul(
            out=ts["p_A"][hb, h * 256:(h + 1) * 256], lhsT=kTc,
            rhs=ts["s_Sbf"][:, h * 256:(h + 1) * 256],
            start=True, stop=True, tile_position=tpO),
            inc="tt" if h == 3 else None)
    def pwY(Lv):
        return ts["s_pw"][hb, Lv * 512: Lv * 512 + 256]
    def pwYT(Lv):
        return ts["s_pw"][hb, Lv * 512 + 256: Lv * 512 + 512]
    prevY, prevYT = ts["s_At"], ts["s_AtT"]
    for Lv in range(4):
        em.wait(t, ("vv",))
        for h in range(4):
            hs = slice(h * 64, (h + 1) * 64)
            em.op(t, lambda e, h=h, hs=hs: e.matmul(
                out=ts["p_B"][hb, h * 64:(h + 1) * 64],
                lhsT=prevYT[hb, hs], rhs=prevY[hb, hs],
                start=True, stop=True, tile_position=tpKK))
            em.op(t, lambda e, h=h, hs=hs: e.matmul(
                out=ts["p_B"][hb, 256 + h * 64:256 + (h + 1) * 64],
                lhsT=prevY[hb, hs], rhs=prevYT[hb, hs],
                start=True, stop=True, tile_position=tpKK),
                inc="tt" if h == 3 else None)
        em.wait(v, ("tt",))
        if Lv < 3:
            em.op(v, lambda e, Lv=Lv: e.tensor_copy(
                out=ts["s_pw"][hb, Lv * 512:(Lv + 1) * 512], in_=ts["p_B"][hb, :]),
                inc="vv")
        else:
            em.op(v, lambda e, Lv=Lv: e.tensor_copy(
                out=ts["s_pw"][hb, Lv * 512:(Lv + 1) * 512], in_=ts["p_B"][hb, :]),
                inc="vv")
        em.drain(v)
        prevY, prevYT = None, None
        prevY = _PwSlice(ts, Lv, 0)
        prevYT = _PwSlice(ts, Lv, 1)
    em.wait(v, ("tt",))
    ks3 = ts["p_A"][hb, :].rearrange("p (h f) -> p h f", f=256)
    rk3 = st[hb, 8:12].rearrange("p (h o) -> p h o", o=1).broadcast_to((64, 4, 256))
    u3 = ts["s_uf"][hb, :].rearrange("p (h f) -> p h f", f=256)
    v3 = ts["s_v"][hb, :].rearrange("p (h f) -> p h f", f=256)
    em.op(v, lambda e: e.tensor_tensor(out=u3, in0=ks3, in1=rk3, op=ALU.mult))
    em.drain(v)
    em.op(v, lambda e: e.tensor_tensor(out=u3, in0=v3, in1=u3, op=ALU.subtract))
    em.drain(v)
    rkb3 = st[hb, 20:24].rearrange("p (h o) -> p h o", o=1).broadcast_to((64, 4, 256))
    ub3 = ts["s_ub"][hb, :].rearrange("p (h f) -> p h f", f=256)
    em.op(v, lambda e: e.tensor_tensor(out=ub3, in0=u3, in1=rkb3, op=ALU.mult),
          inc="vv")
    em.drain(v)
    em.wait(t, ("vv",))
    for h in range(4):
        hs = slice(h * 64, (h + 1) * 64)
        em.op(t, lambda e, h=h, hs=hs: e.matmul(
            out=ts["p_B"][hb, h * 64:(h + 1) * 64],
            lhsT=_PwSlice(ts, 3, 0)[hb, hs], rhs=_PwSlice(ts, 3, 1)[hb, hs],
            start=True, stop=True, tile_position=tpKK),
            inc="tt" if h == 3 else None)
    em.wait(v, ("tt",))
    em.op(v, lambda e: e.tensor_copy(out=ts["s_Lt"][hb, :], in_=ts["p_B"][hb, 0:256]),
          inc="vv")
    em.drain(v)
    app_seq = [(ts["s_Lt"], 1.0), (_PwSlice(ts, 3, 1), 1.0), (_PwSlice(ts, 2, 1), 1.0),
               (_PwSlice(ts, 1, 1), 1.0), (_PwSlice(ts, 0, 1), 1.0),
               (ts["s_AtT"], -1.0)]
    for lhsT_t, sgn in app_seq:
        em.wait(t, ("vv",))
        for h in range(4):
            em.op(t, lambda e, h=h, lhsT_t=lhsT_t: e.matmul(
                out=ts["p_A"][hb, h * 256:(h + 1) * 256],
                lhsT=lhsT_t[hb, h * 64:(h + 1) * 64],
                rhs=ts["s_ub"][hb, h * 256:(h + 1) * 256],
                start=True, stop=True, tile_position=tpKK),
                inc="tt" if h == 3 else None)
        em.wait(v, ("tt",))
        em.op(v, lambda e, sgn=sgn: e.scalar_tensor_tensor(
            out=ts["s_ub"][hb, :], in0=ts["p_A"][hb, :], scalar=sgn,
            in1=ts["s_ub"][hb, :], op0=ALU.mult, op1=ALU.add), inc="vv")
        em.drain(v)
    em.wait(t, ("vv",))
    for h in range(4):
        qTc = ts["s_qkT"][:, h * 128 + tok0: h * 128 + tok0 + 64]
        em.op(t, lambda e, h=h, qTc=qTc: e.matmul(
            out=ts["p_D"][hb, h * 256:(h + 1) * 256], lhsT=qTc,
            rhs=ts["s_Sbf"][:, h * 256:(h + 1) * 256],
            start=(h % 2 == 0), stop=False, tile_position=tpO))
    for h in range(4):
        qTc = ts["s_qkT"][:, h * 128 + tok0: h * 128 + tok0 + 64]
        kTc = ts["s_qkT"][:, 512 + h * 128 + tok0: 512 + h * 128 + tok0 + 64]
        em.op(t, lambda e, h=h, qTc=qTc, kTc=kTc: e.matmul(
            out=ts["p_C"][hb, h * 64:(h + 1) * 64], lhsT=kTc, rhs=qTc,
            start=True, stop=True, tile_position=tpO),
            inc="tt" if h == 3 else None)
    em.wait(v, ("tt",))
    r3 = ts["p_C"][hb, :].rearrange("p (h f) -> p h f", f=64)
    triu3 = triuI[hb, :].rearrange("p (o f) -> p o f", o=1).broadcast_to((64, 4, 64))
    em.op(v, lambda e: e.tensor_tensor(out=ts["s_Lt"][hb, :].rearrange(
        "p (h f) -> p h f", f=64), in0=r3, in1=triu3, op=ALU.mult), inc="vv")
    em.drain(v)
    em.wait(t, ("vv",))
    for h in range(4):
        em.op(t, lambda e, h=h: e.matmul(
            out=ts["p_D"][hb, h * 256:(h + 1) * 256],
            lhsT=ts["s_Lt"][hb, h * 64:(h + 1) * 64],
            rhs=ts["s_ub"][hb, h * 256:(h + 1) * 256],
            start=False, stop=(h % 2 == 1), tile_position=tpKK),
            inc="tt" if h == 3 else None)
    for h in range(4):
        em.op(t, lambda e, h=h: e.matmul(
            out=ts["p_S"][:, h * 256:(h + 1) * 256],
            lhsT=ts["s_qk"][hb, 512 + h * 128: 512 + (h + 1) * 128],
            rhs=ts["s_ub"][hb, h * 256:(h + 1) * 256],
            start=True, stop=True, tile_position=(64 * c, 0)),
            inc="tt" if h == 3 else None)
    em.wait(v, ("tt",))
    o3 = ts["s_o"][hb, :].rearrange("p (h f) -> p h f", f=256)
    rq3 = st[hb, 12:16].rearrange("p (h o) -> p h o", o=1).broadcast_to((64, 4, 256))
    em.op(v, lambda e: e.tensor_tensor(out=o3, in0=ts["p_D"][hb, :].rearrange(
        "p (h f) -> p h f", f=256), in1=rq3, op=ALU.mult))
    em.op(v, lambda e: e.scalar_tensor_tensor(
        out=ts["s_Sf"][:], in0=ts["p_S"][:], scalar=1.0, in1=ts["s_Sf"][:],
        op0=ALU.mult, op1=ALU.add))
    em.drain(v)
    em.op(v, lambda e: e.tensor_copy(out=ts["s_Sbf"][:], in_=ts["s_Sf"][:]), inc="vv")
    em.drain(v)


class _PwSlice:
    def __init__(self, ts, Lv, which):
        self.t = ts["s_pw"]
        self.base = Lv * 512 + which * 256
    def __getitem__(self, key):
        hb, hs = key
        if isinstance(hs, slice):
            return self.t[hb, self.base + hs.start: self.base + hs.stop]
        raise KeyError(key)



_HOST_CONSTS = None

def _host_consts():
    global _HOST_CONSTS
    if _HOST_CONSTS is None:
        tril = np.tril(np.ones((64, 64), np.float32), -1)
        triu = np.triu(np.ones((64, 64), np.float32), 0)
        ident = np.eye(64, dtype=np.float32)
        id4 = np.concatenate([ident] * 4, axis=1)
        row = np.concatenate([tril, triu, id4], axis=1)
        base = np.concatenate([row, row], axis=0)
        _HOST_CONSTS = np.ascontiguousarray(
            np.concatenate([base, np.eye(128, dtype=np.float32)], axis=1))
    return _HOST_CONSTS


def kernel(**inputs):
    x = np.ascontiguousarray(inputs["L0_attn_hidden_states"], dtype=np.float32)
    consts = _host_consts()
    nw_rep = np.ascontiguousarray(
        np.broadcast_to(inputs["L0_attn_norm_w"][None, :], (128, 256)), np.float32)
    sin_rep = np.ascontiguousarray(
        np.broadcast_to(inputs["L0_ffn_s_in"].reshape(1, 1), (128, 1)), np.float32)
    sid_rep = np.ascontiguousarray(
        np.broadcast_to(inputs["L0_ffn_s_in_down"].reshape(1, 1), (128, 1)), np.float32)
    hsel = [np.ascontiguousarray(
        np.broadcast_to(np.array([[1.0 - h, float(h)]], np.float32), (128, 2)))
        for h in (0, 1)]
    shared = {
        "Wq": np.ascontiguousarray(inputs["L0_attn_Wq"], np.float32),
        "Wk": np.ascontiguousarray(inputs["L0_attn_Wk"], np.float32),
        "Wv": np.ascontiguousarray(inputs["L0_attn_Wv"], np.float32),
        "Wb": np.ascontiguousarray(inputs["L0_attn_Wb"], np.float32),
        "Wg": np.ascontiguousarray(inputs["L0_attn_Wg"], np.float32),
        "Wo": np.ascontiguousarray(inputs["L0_attn_Wo"], np.float32),
        "conv_w": np.ascontiguousarray(inputs["L0_attn_conv_w"], np.float32),
        "norm_w_rep": nw_rep,
        "s_in_rep": sin_rep,
        "s_in_down_rep": sid_rep,
        "masks": consts,
        "gate_w": np.ascontiguousarray(inputs["L0_ffn_gate_w"], np.float32),
        "up_w": np.ascontiguousarray(inputs["L0_ffn_up_w"], np.float32),
        "down_w": np.ascontiguousarray(inputs["L0_ffn_down_w"], np.float32),
    }
    in_maps = []
    for c in range(8):
        b, half = c // 2, c % 2
        m = dict(shared)
        m["x_seq"] = np.ascontiguousarray(x[b])
        m["x_own"] = np.ascontiguousarray(x[b, half * HALF:(half + 1) * HALF])
        m["half_sel_rep"] = hsel[half]
        in_maps.append(m)
    nc = build_program()
    res = run_bass_kernel_spmd(nc, in_maps, core_ids=list(range(8)))
    out = np.empty((B, L, D), np.float32)
    for c in range(8):
        b, half = c // 2, c % 2
        out[b, half * HALF:(half + 1) * HALF] = res.results[c]["out_half"]
    return out


# version 17
# speedup vs baseline: 1.0156x; 1.0090x over previous
import numpy as np
import ml_dtypes
from contextlib import ExitStack
import concourse.bass as bass
import concourse.mybir as mybir
from concourse.bass_isa import ReduceOp
from concourse import library_config
from concourse.bass_utils import run_bass_kernel_spmd

F32 = mybir.dt.float32
BF16 = mybir.dt.bfloat16
FP8 = mybir.dt.float8e4
AF = mybir.ActivationFunctionType
ALU = mybir.AluOpType
AX = mybir.AxisListType

B, L, D = 4, 4096, 1024
HALF = L // 2
H, DQH, DVH = 4, 128, 256
NT = L // 128
NTO = HALF // 128
DI = 4096

SEM_NAMES = ("qs", "qt", "vv", "aa", "tt", "gg", "wa", "wb", "wc", "wd")


def build_program():
    nc = bass.Bass("TRN2", target_bir_lowering=False)

    dram = {}
    def din(name, shape, dt=F32):
        dram[name] = nc.dram_tensor(name, list(shape), dt, kind="ExternalInput").ap()
        return dram[name]
    def dscr(name, shape, dt):
        dram[name] = nc.dram_tensor(name, list(shape), dt, kind="Internal").ap()
        return dram[name]

    X = din("x_seq", [L, D])
    XO = din("x_own", [HALF, D])
    WQ = din("Wq", [D, 512]); WK = din("Wk", [D, 512]); WV = din("Wv", [D, 1024])
    WB = din("Wb", [D, 4]); WG = din("Wg", [D, 1024]); WO = din("Wo", [D, 1024])
    CW = din("conv_w", [D, 4])
    NW = din("norm_w_rep", [128, 256])
    GW = din("gate_w", [DI, D]); UW = din("up_w", [DI, D])
    DW = din("down_w", [D, DI])
    SIN = din("s_in_rep", [128, 1]); SID = din("s_in_down_rep", [128, 1])
    HSEL = din("half_sel_rep", [128, 2])
    MSK = din("masks", [128, 6 * 64 + 128])
    OUT = nc.dram_tensor("out_half", [HALF, D], F32, kind="ExternalOutput").ap()
    HQS = dscr("hq_scratch", [HALF, DI], BF16)
    WMX = dscr("wmax_scratch", [128], F32)
    OWA = dscr("o_woA", [(NTO + 1) * 128, D], F32)
    OWB = dscr("o_woB", [(NTO + 1) * 128, D], F32)
    WGT = dscr("wgT_hbm", [8, 128, DI], BF16)
    WUT = dscr("wuT_hbm", [8, 128, DI], BF16)
    WDT = dscr("wdT_hbm", [32, 128, D], BF16)
    WSC = dscr("wsc_scratch", [3, 128, 2048], F32)
    WDQ = dscr("wdq_scratch", [3, 128, 2048], F32)
    WRD = dscr("wrd_scratch", [3, 128, 2048], F32)

    st_outer = ExitStack()
    en = st_outer.enter_context
    sem = {n: en(nc.semaphore(n)) for n in SEM_NAMES}

    s_msk = en(nc.sbuf_tensor("s_msk", [128, 6 * 64 + 128], F32))
    s_nw = en(nc.sbuf_tensor("s_nw", [128, 256], F32))
    s_cw = en(nc.sbuf_tensor("s_cw", [128, 8 * 4], F32))
    s_gs = en(nc.sbuf_tensor("s_gs", [128, 16], F32))
    s_q1 = en(nc.sbuf_tensor("s_q1", [128, 1024], F32))
    s_q2 = en(nc.sbuf_tensor("s_q2", [128, 1024], F32))
    s_q3 = en(nc.sbuf_tensor("s_q3", [128, 1024], F32))
    s_q4 = en(nc.sbuf_tensor("s_q4", [128, 1024], F32))
    s_qsg = en(nc.sbuf_tensor("s_qsg", [128, 1024], F32))
    s_qam = en(nc.sbuf_tensor("s_qam", [128, 64], F32))
    s_qs8 = en(nc.sbuf_tensor("s_qs8", [128, 64], F32))
    s_qdq = en(nc.sbuf_tensor("s_qdq", [128, 64], F32))
    s_qrd = en(nc.sbuf_tensor("s_qrd", [128, 64], F32))
    s_ln = en(nc.sbuf_tensor("s_ln", [128, 16], F32))
    s_xt = en(nc.sbuf_tensor("s_xt", [128, 1024], F32))
    s_xn = en(nc.sbuf_tensor("s_xn", [128, 1024], F32))
    s_mh = en(nc.sbuf_tensor("s_mh", [128, 1], F32))
    s_hsel = en(nc.sbuf_tensor("s_hsel", [128, 2], F32))
    s_id = en(nc.sbuf_tensor("s_id", [128, 128], BF16))

    ts = dict(nc=nc, dram=dram, sem=sem, OUT=OUT, X=X, XO=XO,
              s_msk=s_msk, s_nw=s_nw, s_cw=s_cw, s_gs=s_gs,
              s_q1=s_q1, s_q2=s_q2, s_q3=s_q3, s_q4=s_q4, s_qsg=s_qsg,
              s_qam=s_qam, s_qs8=s_qs8, s_qdq=s_qdq, s_qrd=s_qrd,
              s_ln=s_ln, s_xt=s_xt, s_xn=s_xn, s_mh=s_mh, s_hsel=s_hsel,
              s_id=s_id)

    st_a = ExitStack()
    ena = st_a.enter_context
    s_xnT = ena(nc.sbuf_tensor("s_xnT", [128, 8 * (L + 16)], BF16))
    s_xnb = ena(nc.sbuf_tensor("s_xnb", [128, 1024], BF16))
    s_hT = ena(nc.sbuf_tensor("s_hT", [128, 8 * 128], BF16))
    s_Wq = ena(nc.sbuf_tensor("s_Wq", [128, 8 * 512], BF16))
    s_Wk = ena(nc.sbuf_tensor("s_Wk", [128, 8 * 512], BF16))
    s_Wv = ena(nc.sbuf_tensor("s_Wv", [128, 8 * 1024], BF16))
    s_Wb = ena(nc.sbuf_tensor("s_Wb", [128, 8 * 4], BF16))
    s_Wg = ena(nc.sbuf_tensor("s_Wg", [128, 8 * 1024], BF16))
    s_Wo = ena(nc.sbuf_tensor("s_Wo", [128, 8 * 1024], BF16))
    s_qk = ena(nc.sbuf_tensor("s_qk", [128, 1024], BF16))
    s_v = ena(nc.sbuf_tensor("s_v", [128, 1024], BF16))
    s_g = ena(nc.sbuf_tensor("s_g", [128, 1024], BF16))
    s_qkT = ena(nc.sbuf_tensor("s_qkT", [128, 1024], BF16))
    s_beta = ena(nc.sbuf_tensor("s_beta", [128, 4], F32))
    s_st = ena(nc.sbuf_tensor("s_st", [128, 24], F32))
    s_Sbf = ena(nc.sbuf_tensor("s_Sbf", [128, 1024], BF16))
    s_Sf = ena(nc.sbuf_tensor("s_Sf", [128, 1024], F32))
    s_uf = ena(nc.sbuf_tensor("s_uf", [128, 1024], F32))
    s_ub = ena(nc.sbuf_tensor("s_ub", [128, 1024], BF16))
    s_C = ena(nc.sbuf_tensor("s_C", [128, 256], BF16))
    s_dg = ena(nc.sbuf_tensor("s_dg", [128, 256], BF16))
    s_At = ena(nc.sbuf_tensor("s_At", [128, 256], BF16))
    s_AtT = ena(nc.sbuf_tensor("s_AtT", [128, 256], BF16))
    s_pw = ena(nc.sbuf_tensor("s_pw", [128, 9 * 256], BF16))
    s_Lt = ena(nc.sbuf_tensor("s_Lt", [128, 256], BF16))
    s_o = ena(nc.sbuf_tensor("s_o", [128, 1024], F32))
    s_gtd = ena(nc.sbuf_tensor("s_gtd", [128, 1024], BF16))
    s_gTt = ena(nc.sbuf_tensor("s_gTt", [128, 8 * 128], BF16))
    s_wsrc = ena(nc.sbuf_tensor("s_wsrc", [128, 512], F32))
    s_wq1 = ena(nc.sbuf_tensor("s_wq1", [128, 512], F32))
    s_wq3 = ena(nc.sbuf_tensor("s_wq3", [128, 512], F32))
    s_wlad = ena(nc.sbuf_tensor("s_wlad", [128, 5 * 32], F32))
    ts.update(s_xnT=s_xnT, s_xnb=s_xnb, s_hT=s_hT, s_Wq=s_Wq, s_Wk=s_Wk,
              s_Wv=s_Wv, s_Wb=s_Wb, s_Wg=s_Wg, s_Wo=s_Wo, s_qk=s_qk,
              s_v=s_v, s_g=s_g, s_qkT=s_qkT, s_beta=s_beta,
              s_st=s_st, s_Sbf=s_Sbf, s_uf=s_uf, s_ub=s_ub, s_C=s_C,
              s_dg=s_dg, s_At=s_At, s_AtT=s_AtT, s_pw=s_pw, s_Lt=s_Lt,
              s_o=s_o, s_Sf=s_Sf, s_gtd=s_gtd, s_gTt=s_gTt,
              s_wsrc=s_wsrc, s_wq1=s_wq1, s_wq3=s_wq3, s_wlad=s_wlad)
    st_a.close()
    s_xqT = en(nc.sbuf_tensor("s_xqT", [128, 8 * 2048], BF16))
    s_wgc = en(nc.sbuf_tensor("s_wgc", [128, 8 * 1024], BF16))
    s_wuc = en(nc.sbuf_tensor("s_wuc", [128, 8 * 1024], BF16))
    s_sg = en(nc.sbuf_tensor("s_sg", [128, 1024], F32))
    s_hid = en(nc.sbuf_tensor("s_hid", [128, 1024], F32))
    s_hqb = en(nc.sbuf_tensor("s_hqb", [128, 1024], BF16))
    s_hqT = en(nc.sbuf_tensor("s_hqT", [128, 32 * 512], BF16))
    s_gf = en(nc.sbuf_tensor("s_gf", [128, 1024], F32))
    s_wT = en(nc.sbuf_tensor("s_wT", [128, 32 * 1024], BF16))
    ts.update(s_xqT=s_xqT, s_wgc=s_wgc, s_wuc=s_wuc, s_sg=s_sg,
              s_hid=s_hid, s_hqb=s_hqb, s_hqT=s_hqT, s_gf=s_gf, s_wT=s_wT)
    p_S = en(nc.psum_tensor("p_S", [128, 1024], F32))
    p_A = en(nc.psum_tensor("p_A", [128, 1024], F32))
    p_B = en(nc.psum_tensor("p_B", [128, 512], F32))
    p_C = en(nc.psum_tensor("p_C", [128, 256], F32))
    p_D = en(nc.psum_tensor("p_D", [128, 1024], F32))
    ts.update(p_S=p_S, p_A=p_A, p_B=p_B, p_C=p_C, p_D=p_D)

    with nc.Block() as block:
        @block.sync
        def _(eng):
            emit_all("sync", eng, ts)

        @block.scalar
        def _(eng):
            emit_all("scalar", eng, ts)

        @block.vector
        def _(eng):
            emit_all("vector", eng, ts)

        @block.tensor
        def _(eng):
            emit_all("tensor", eng, ts)

        @block.gpsimd
        def _(eng):
            emit_all("gpsimd", eng, ts)

    st_outer.close()
    return nc



class Emit:
    def __init__(self, which, eng, ts):
        self.w = which
        self.e = eng
        self.ts = ts
        self.sem = ts["sem"]
        self.C = {n: 0 for n in SEM_NAMES}

    def op(self, eng_name, fn, inc=None, incby=1):
        i = fn(self.e) if self.w == eng_name else None
        if inc is not None:
            self.C[inc] += incby
            if i is not None:
                i.then_inc(self.sem[inc], incby)
        return i

    def dma(self, queue, out, in_, sem=None):
        eng_name = "sync" if queue == "qs" else "scalar"
        sname = sem if sem is not None else queue
        if self.w == eng_name:
            self.e.dma_start(out=out, in_=in_).then_inc(self.sem[sname], 16)
        self.C[sname] += 16

    def dmaT(self, out, in_):
        if self.w == "scalar":
            self.e.dma_start_transpose(out=out, in_=in_).then_inc(self.sem["qt"], 16)
        self.C["qt"] += 16

    def wait(self, eng_name, sems):
        for s in sems:
            if self.w == eng_name:
                self.e.wait_ge(self.sem[s], self.C[s])
            elif s == "qs" and self.w == "sync" and eng_name != "sync":
                self.e.wait_ge(self.sem[s], self.C[s])
            elif s in ("qt", "wd") and self.w == "scalar" and eng_name != "scalar":
                self.e.wait_ge(self.sem[s], self.C[s])

    def drain(self, eng_name):
        if self.w == eng_name:
            self.e.drain()

    def barrier(self):
        for s in SEM_NAMES:
            self.e.wait_ge(self.sem[s], self.C[s])


def emit_all(which, eng, ts):
    em = Emit(which, eng, ts)
    phase_p0(em, ts)
    em.barrier()
    phase_a(em, ts)
    em.barrier()
    phase_f(em, ts)
    em.wait("sync", ("qs",))
    em.wait("scalar", ("qt",))



def q3v(ap, nb):
    return ap.rearrange("p (b k) -> p b k", k=16)


def bcast16(ap, nb):
    return ap.rearrange("p (b o) -> p b o", o=1).broadcast_to((128, nb, 16))


def emit_quant(em, ts, src_ap, dst_ap, ncols, gcol):
    v, a = "vector", "scalar"
    gs = ts["s_gs"]
    nb = ncols // 16
    q1, q2, q3, q4, sg = (ts["s_q1"][:, 0:ncols], ts["s_q2"][:, 0:ncols],
                          ts["s_q3"][:, 0:ncols], ts["s_q4"][:, 0:ncols],
                          ts["s_qsg"][:, 0:ncols])
    am, s8, dq, rd = (ts["s_qam"][:, 0:nb], ts["s_qs8"][:, 0:nb],
                      ts["s_qdq"][:, 0:nb], ts["s_qrd"][:, 0:nb])
    zt = ts["s_q2"][:, 0:nb]
    pt = ts["s_q2"][:, nb:2 * nb]
    ct = ts["s_q2"][:, 2 * nb:3 * nb]
    vt = ts["s_q3"][:, 0:nb]
    ut = ts["s_q3"][:, nb:2 * nb]
    em.op(v, lambda e: e.tensor_reduce(out=am, in_=q3v(src_ap, nb), axis=AX.X,
                                       op=ALU.max, apply_absolute_value=True))
    em.drain(v)
    em.op(v, lambda e: e.tensor_scalar(out=zt, in0=am, scalar1=gs[:, gcol + 1:gcol + 2],
                                       scalar2=224.0, op0=ALU.mult, op1=ALU.min))
    em.drain(v)
    em.op(v, lambda e: e.memset(pt, 2.0 ** -7))
    em.drain(v)
    for k in (8, 4, 2, 1):
        em.op(v, lambda e, k=k: e.tensor_scalar(out=ct, in0=pt, scalar1=float(2 ** k),
                                                scalar2=None, op0=ALU.mult))
        em.drain(v)
        em.op(v, lambda e: e.tensor_tensor(out=ct, in0=ct, in1=zt, op=ALU.is_le))
        em.drain(v)
        em.op(v, lambda e, k=k: e.tensor_scalar(out=ct, in0=ct,
                                                scalar1=float(2 ** k - 1),
                                                scalar2=1.0, op0=ALU.mult, op1=ALU.add))
        em.drain(v)
        em.op(v, lambda e: e.tensor_tensor(out=pt, in0=pt, in1=ct, op=ALU.mult))
        em.drain(v)
    em.op(v, lambda e: e.reciprocal(out=ct, in_=pt))
    em.drain(v)
    em.op(v, lambda e: e.scalar_tensor_tensor(out=ut, in0=zt, scalar=8.0, in1=ct,
                                              op0=ALU.mult, op1=ALU.mult))
    em.drain(v)
    em.op(v, lambda e: e.memset(vt, 0.0))
    em.drain(v)
    for k in (16, 8, 4, 2, 1):
        em.op(v, lambda e, k=k: e.scalar_tensor_tensor(out=ct, in0=vt,
                                                       scalar=float(k) - 0.5, in1=ut,
                                                       op0=ALU.add, op1=ALU.is_le))
        em.drain(v)
        em.op(v, lambda e, k=k: e.scalar_tensor_tensor(out=vt, in0=ct,
                                                       scalar=float(k), in1=vt,
                                                       op0=ALU.mult, op1=ALU.add))
        em.drain(v)
    em.op(v, lambda e: e.scalar_tensor_tensor(out=s8, in0=vt, scalar=0.125, in1=pt,
                                              op0=ALU.mult, op1=ALU.mult))
    em.drain(v)
    em.op(v, lambda e: e.tensor_scalar(out=dq, in0=s8, scalar1=gs[:, gcol + 2:gcol + 3],
                                       scalar2=None, op0=ALU.mult))
    em.op(v, lambda e: e.tensor_scalar(out=rd, in0=s8, scalar1=1e-30,
                                       scalar2=None, op0=ALU.max))
    em.drain(v)
    em.op(v, lambda e: e.reciprocal(out=rd, in_=rd))
    em.drain(v)
    em.op(v, lambda e: e.tensor_scalar(out=rd, in0=rd, scalar1=gs[:, gcol:gcol + 1],
                                       scalar2=None, op0=ALU.mult), inc="vv")
    em.drain(v)
    em.wait("gpsimd", ("vv", "gg"))
    em.op("gpsimd", lambda e: e.tensor_tensor(out=q3v(q1, nb),
                                              in0=q3v(src_ap, nb),
                                              in1=bcast16(rd, nb), op=ALU.mult),
          inc="gg")
    em.wait(a, ("gg",))
    em.op(a, lambda e: e.activation(out=sg, in_=q1, func=AF.Sign))
    em.op(a, lambda e: e.activation(out=q2, in_=q1, func=AF.Abs), inc="gg")
    em.wait(v, ("gg",))
    em.op(v, lambda e: e.tensor_scalar(out=q3, in0=q2, scalar1=0.5,
                                       scalar2=None, op0=ALU.is_ge))
    em.drain(v)
    for th in (1.5, 2.5, 3.5):
        em.op(v, lambda e, th=th: e.scalar_tensor_tensor(
            out=q3, in0=q2, scalar=th, in1=q3, op0=ALU.is_ge, op1=ALU.add))
        em.drain(v)
    em.op(v, lambda e: e.tensor_scalar(out=q4, in0=q2, scalar1=5.0,
                                       scalar2=None, op0=ALU.is_ge))
    em.drain(v)
    em.op(v, lambda e: e.scalar_tensor_tensor(
        out=q4, in0=q2, scalar=7.0, in1=q4, op0=ALU.is_ge, op1=ALU.add))
    em.drain(v)
    em.op(v, lambda e: e.scalar_tensor_tensor(
        out=q3, in0=q4, scalar=2.0, in1=q3, op0=ALU.mult, op1=ALU.add))
    em.drain(v)
    em.op(v, lambda e: e.tensor_scalar(out=q4, in0=q2, scalar1=10.0,
                                       scalar2=None, op0=ALU.is_ge))
    em.drain(v)
    em.op(v, lambda e: e.scalar_tensor_tensor(
        out=q3, in0=q4, scalar=4.0, in1=q3, op0=ALU.mult, op1=ALU.add),
        inc="vv")
    em.drain(v)
    em.wait("gpsimd", ("vv",))
    em.op("gpsimd", lambda e: e.tensor_tensor(out=q3v(q3, nb), in0=q3v(q3, nb),
                                              in1=bcast16(dq, nb), op=ALU.mult))
    em.drain("gpsimd")
    em.op("gpsimd", lambda e: e.tensor_tensor(out=dst_ap, in0=q3, in1=sg,
                                              op=ALU.mult), inc="vv")
    em.drain("gpsimd")


def emit_ln(em, ts, x_ap, xn_ap):
    v, a = "vector", "scalar"
    ln = ts["s_ln"]
    em.op(v, lambda e: e.bn_stats(out=ln[:, 0:6], in_=x_ap[:, 0:512]))
    em.op(v, lambda e: e.bn_stats(out=ln[:, 6:12], in_=x_ap[:, 512:1024]))
    em.drain(v)
    em.op(v, lambda e: e.bn_aggr(out=ln[:, 12:14], in_=ln[:, 0:12]))
    em.drain(v)
    em.op(v, lambda e: e.tensor_scalar(out=ln[:, 14:15], in0=ln[:, 13:14],
                                       scalar1=1e-5, scalar2=None, op0=ALU.add),
          inc="vv")
    em.drain(v)
    em.wait("gpsimd", ("vv",))
    em.op("gpsimd", lambda e: e.tensor_tensor(
        out=ln[:, 15:16], in0=ln[:, 14:15], in1=ts["s_mh"][:], op=ALU.pow),
        inc="gg")
    em.wait(v, ("gg",))
    em.op(v, lambda e: e.tensor_scalar(out=xn_ap, in0=x_ap, scalar1=ln[:, 12:13],
                                       scalar2=None, op0=ALU.subtract))
    em.drain(v)
    em.op(v, lambda e: e.tensor_scalar(out=xn_ap, in0=xn_ap, scalar1=ln[:, 15:16],
                                       scalar2=None, op0=ALU.mult), inc="vv")
    em.drain(v)



def phase_p0(em, ts):
    v, a, t, s, g = "vector", "scalar", "tensor", "sync", "gpsimd"
    dram = ts["dram"]
    nc_gs = ts["s_gs"]
    em.op(v, lambda e: e.memset(ts["s_mh"][:], -0.5))
    em.dma("qs", ts["s_msk"][:], dram["masks"][:, :])
    em.dma("qs", ts["s_nw"][:], dram["norm_w_rep"][:, :])
    em.dma("qs", nc_gs[:, 0:1], dram["s_in_rep"][:, :])
    em.dma("qs", nc_gs[:, 3:4], dram["s_in_down_rep"][:, :])
    em.dma("qs", ts["s_hsel"][:], dram["half_sel_rep"][:, :])
    cw3 = ts["s_cw"][:].rearrange("p (c i) -> p c i", i=4)
    em.dma("qs", cw3, dram["conv_w"][:, :].rearrange("(c p) i -> p c i", p=128))
    em.wait(v, ("qs",))
    em.op(v, lambda e: e.tensor_copy(out=ts["s_id"][:],
                                     in_=ts["s_msk"][:, 384:512]))
    for c in (0, 3):
        em.op(v, lambda e, c=c: e.tensor_scalar(out=nc_gs[:, c + 1:c + 2],
                                                in0=nc_gs[:, c:c + 1],
                                                scalar1=1.0 / 12.0, scalar2=None,
                                                op0=ALU.mult))
        em.op(v, lambda e, c=c: e.tensor_copy(out=nc_gs[:, c + 2:c + 3],
                                              in_=nc_gs[:, c:c + 1]))
        em.drain(v)
        em.op(v, lambda e, c=c: e.reciprocal(out=nc_gs[:, c + 2:c + 3],
                                             in_=nc_gs[:, c + 2:c + 3]))
        em.drain(v)
    em.op(v, lambda e: e.memset(nc_gs[:, 15:16], 0.0), inc="vv")
    em.drain(v)



def emit_wamax_pair(em, ts, srcA, srcB, sheetA, sheetB, par):
    v, s = "vector", "sync"
    if par == 0:
        slotA, slotB = ts["s_wsrc"][:], ts["s_wq1"][:]
        grp, glob, sem = ts["s_qam"], ts["s_qdq"][:, 0:1], "vv"
    else:
        slotA, slotB = ts["s_wq3"][:], ts["s_qsg"][:, 0:512]
        grp, glob, sem = ts["s_qs8"], ts["s_qrd"][:, 0:1], "gg"
    em.wait(s, (sem,))
    em.dma("qs", slotA, srcA)
    em.dma("qs", slotB, srcB)
    em.wait(v, ("qs",))
    em.op(v, lambda e: e.tensor_reduce(
        out=grp[:, 0:32], in_=q3v(slotA, 32), axis=AX.X,
        op=ALU.max, apply_absolute_value=True))
    em.op(v, lambda e: e.tensor_reduce(
        out=grp[:, 32:64], in_=q3v(slotB, 32), axis=AX.X,
        op=ALU.max, apply_absolute_value=True))
    em.drain(v)
    em.op(v, lambda e: e.tensor_reduce(
        out=glob, in_=grp[:, 0:64], axis=AX.X,
        op=ALU.max, apply_absolute_value=True))
    em.drain(v)
    em.op(v, lambda e: e.tensor_tensor(
        out=ts["s_gs"][:, 15:16], in0=ts["s_gs"][:, 15:16],
        in1=glob, op=ALU.max), inc=sem)
    em.drain(v)
    em.wait(s, (sem,))
    em.dma("qs", sheetA, grp[:, 0:32])
    em.dma("qs", sheetB, grp[:, 32:64])


def emit_wladder(em, ts, mi, gcol, p):
    v, s = "vector", "sync"
    gs = ts["s_gs"]
    dram = ts["dram"]
    src = ts["s_wsrc"][:]
    zt = ts["s_wq1"][:]
    pt = ts["s_wq3"][:]
    ct = ts["s_qsg"][:, 0:512]
    vt = ts["s_qsg"][:, 512:1024]
    if True:
        em.wait(s, ("vv", "gg"))
        em.dma("qs", src, dram["wsc_scratch"][mi, :, p * 512:(p + 1) * 512])
        em.wait(v, ("qs",))
        em.op(v, lambda e: e.tensor_scalar(out=zt, in0=src,
                                           scalar1=gs[:, gcol + 1:gcol + 2],
                                           scalar2=224.0, op0=ALU.mult,
                                           op1=ALU.min))
        em.op(v, lambda e: e.memset(pt, 2.0 ** -7))
        em.drain(v)
        for k in (8, 4, 2, 1):
            em.op(v, lambda e, k=k: e.tensor_scalar(out=ct, in0=pt,
                                                    scalar1=float(2 ** k),
                                                    scalar2=None, op0=ALU.mult))
            em.drain(v)
            em.op(v, lambda e: e.tensor_tensor(out=ct, in0=ct, in1=zt,
                                               op=ALU.is_le))
            em.drain(v)
            em.op(v, lambda e, k=k: e.tensor_scalar(out=ct, in0=ct,
                                                    scalar1=float(2 ** k - 1),
                                                    scalar2=1.0, op0=ALU.mult,
                                                    op1=ALU.add))
            em.drain(v)
            em.op(v, lambda e: e.tensor_tensor(out=pt, in0=pt, in1=ct,
                                               op=ALU.mult))
            em.drain(v)
        em.op(v, lambda e: e.reciprocal(out=ct, in_=pt))
        em.drain(v)
        em.op(v, lambda e: e.scalar_tensor_tensor(out=src, in0=zt, scalar=8.0,
                                                  in1=ct, op0=ALU.mult,
                                                  op1=ALU.mult))
        em.op(v, lambda e: e.memset(vt, 0.0))
        em.drain(v)
        for k in (16, 8, 4, 2, 1):
            em.op(v, lambda e, k=k: e.scalar_tensor_tensor(out=ct, in0=vt,
                                                           scalar=float(k) - 0.5,
                                                           in1=src, op0=ALU.add,
                                                           op1=ALU.is_le))
            em.drain(v)
            em.op(v, lambda e, k=k: e.scalar_tensor_tensor(out=vt, in0=ct,
                                                           scalar=float(k),
                                                           in1=vt, op0=ALU.mult,
                                                           op1=ALU.add))
            em.drain(v)
        em.op(v, lambda e: e.scalar_tensor_tensor(out=zt, in0=vt, scalar=0.125,
                                                  in1=pt, op0=ALU.mult,
                                                  op1=ALU.mult))
        em.drain(v)
        em.op(v, lambda e: e.tensor_scalar(out=pt, in0=zt,
                                           scalar1=gs[:, gcol + 2:gcol + 3],
                                           scalar2=None, op0=ALU.mult))
        em.op(v, lambda e: e.tensor_scalar(out=ct, in0=zt, scalar1=1e-30,
                                           scalar2=None, op0=ALU.max))
        em.drain(v)
        em.op(v, lambda e: e.reciprocal(out=ct, in_=ct))
        em.drain(v)
        em.op(v, lambda e: e.tensor_scalar(out=ct, in0=ct,
                                           scalar1=gs[:, gcol:gcol + 1],
                                           scalar2=None, op0=ALU.mult),
              inc="vv")
        em.drain(v)
        em.wait(s, ("vv",))
        em.dma("qs", dram["wdq_scratch"][mi, :, p * 512:(p + 1) * 512], pt)
        em.dma("qs", dram["wrd_scratch"][mi, :, p * 512:(p + 1) * 512], ct)


def emit_wbounce(em, ts, gcol):
    v, s = "vector", "sync"
    nc_gs = ts["s_gs"]
    em.wait(s, ("vv", "gg"))
    em.dma("qs", ts["dram"]["wmax_scratch"][:].rearrange("(p o) -> p o", o=1),
           nc_gs[:, 15:16])
    em.wait(s, ("qs",))
    em.dma("qs", ts["s_wlad"][0:1, 0:128],
           ts["dram"]["wmax_scratch"][:].rearrange("(o f) -> o f", o=1))
    em.wait(v, ("qs",))
    em.op(v, lambda e: e.tensor_reduce(out=ts["s_wlad"][0:1, 128:129],
                                       in_=ts["s_wlad"][0:1, 0:128], axis=AX.X,
                                       op=ALU.max, apply_absolute_value=True))
    em.drain(v)
    em.op(v, lambda e: e.tensor_copy(
        out=ts["s_wlad"][0:1, 0:128],
        in_=ts["s_wlad"][0:1, 128:129].broadcast_to((1, 128))), inc="vv")
    em.drain(v)
    em.wait(s, ("vv",))
    em.dma("qs", ts["dram"]["wmax_scratch"][:].rearrange("(o f) -> o f", o=1),
           ts["s_wlad"][0:1, 0:128])
    em.wait(s, ("qs",))
    em.dma("qs", nc_gs[:, gcol:gcol + 1],
           ts["dram"]["wmax_scratch"][:].rearrange("(p o) -> p o", o=1))
    em.wait(v, ("qs",))
    em.op(v, lambda e: e.reciprocal(out=nc_gs[:, gcol:gcol + 1],
                                    in_=nc_gs[:, gcol:gcol + 1]))
    em.drain(v)
    em.op(v, lambda e: e.tensor_scalar(out=nc_gs[:, gcol:gcol + 1],
                                       in0=nc_gs[:, gcol:gcol + 1],
                                       scalar1=2688.0, scalar2=None, op0=ALU.mult))
    em.drain(v)
    em.op(v, lambda e: e.tensor_scalar(out=nc_gs[:, gcol + 1:gcol + 2],
                                       in0=nc_gs[:, gcol:gcol + 1],
                                       scalar1=1.0 / 12.0, scalar2=None,
                                       op0=ALU.mult))
    em.op(v, lambda e: e.tensor_copy(out=nc_gs[:, gcol + 2:gcol + 3],
                                     in_=nc_gs[:, gcol:gcol + 1]))
    em.drain(v)
    em.op(v, lambda e: e.reciprocal(out=nc_gs[:, gcol + 2:gcol + 3],
                                    in_=nc_gs[:, gcol + 2:gcol + 3]))
    em.drain(v)
    em.op(v, lambda e: e.memset(nc_gs[:, 15:16], 0.0), inc="vv")
    em.drain(v)


def emit_wload(em, ts, mi, sidx, src_hbm):
    s = "sync"
    dram = ts["dram"]
    em.wait(s, ("wa",))
    em.dma("qs", ts["s_wsrc"][:], src_hbm)
    em.dma("qs", ts["s_qdq"][:, 0:32],
           dram["wdq_scratch"][mi, :, sidx * 32:(sidx + 1) * 32])
    em.dma("qs", ts["s_qrd"][:, 0:32],
           dram["wrd_scratch"][mi, :, sidx * 32:(sidx + 1) * 32])


def emit_wchain(em, ts, par):
    v, a, s, g = "vector", "scalar", "sync", "gpsimd"
    nb = 32
    q1, q3 = ts["s_wq1"][:], ts["s_wq3"][:]
    q2 = ts["s_qsg"][:, 0:512]
    sg = ts["s_qsg"][:, 512:1024]
    wnt = ts["s_xnb"][:, par * 512:(par + 1) * 512]
    dq, rd = ts["s_qdq"][:, 0:nb], ts["s_qrd"][:, 0:nb]
    src = ts["s_wsrc"][:]
    em.wait(g, ("qs", "wb"))
    em.op(g, lambda e: e.tensor_tensor(out=q3v(q1, nb), in0=q3v(src, nb),
                                       in1=bcast16(rd, nb), op=ALU.mult),
          inc="wa")
    em.wait(a, ("wa", "wc"))
    em.op(a, lambda e: e.activation(out=sg, in_=q1, func=AF.Sign))
    em.op(a, lambda e: e.activation(out=q2, in_=q1, func=AF.Abs), inc="wb")
    em.wait(v, ("wb", "wa"))
    em.op(v, lambda e: e.tensor_scalar(out=q3, in0=q2, scalar1=0.5,
                                       scalar2=None, op0=ALU.is_ge))
    em.drain(v)
    for th in (1.5, 2.5, 3.5):
        em.op(v, lambda e, th=th: e.scalar_tensor_tensor(
            out=q3, in0=q2, scalar=th, in1=q3, op0=ALU.is_ge, op1=ALU.add))
        em.drain(v)
    em.op(v, lambda e: e.tensor_scalar(out=q1, in0=q2, scalar1=5.0,
                                       scalar2=None, op0=ALU.is_ge))
    em.drain(v)
    em.op(v, lambda e: e.scalar_tensor_tensor(
        out=q1, in0=q2, scalar=7.0, in1=q1, op0=ALU.is_ge, op1=ALU.add))
    em.drain(v)
    em.op(v, lambda e: e.scalar_tensor_tensor(
        out=q3, in0=q1, scalar=2.0, in1=q3, op0=ALU.mult, op1=ALU.add))
    em.drain(v)
    em.op(v, lambda e: e.tensor_scalar(out=q1, in0=q2, scalar1=10.0,
                                       scalar2=None, op0=ALU.is_ge))
    em.drain(v)
    em.op(v, lambda e: e.scalar_tensor_tensor(
        out=q3, in0=q1, scalar=4.0, in1=q3, op0=ALU.mult, op1=ALU.add),
        inc="wc")
    em.drain(v)
    em.wait(g, ("wc", "wd"))
    em.op(g, lambda e: e.tensor_tensor(out=q3v(q3, nb), in0=q3v(q3, nb),
                                       in1=bcast16(dq, nb), op=ALU.mult))
    em.drain(g)
    em.op(g, lambda e: e.tensor_tensor(out=wnt, in0=q3, in1=sg,
                                       op=ALU.mult), inc="wa")
    em.drain(g)


def emit_wtrans(em, ts, par, dests):
    a, t, s = "scalar", "tensor", "sync"
    base = par * 512
    em.wait(t, ("wa", "wc"))
    for j in range(4):
        em.op(t, lambda e, j=j: e.matmul(
            out=ts["p_B"][:, j * 128:(j + 1) * 128],
            lhsT=ts["s_xnb"][:, base + j * 128:base + (j + 1) * 128],
            rhs=ts["s_id"][:], start=(j == 0), stop=(j == 3)),
            inc="wb" if j == 3 else None)
    em.wait(a, ("wb", "wd"))
    em.op(a, lambda e: e.copy(out=ts["s_xnb"][:, base:base + 512],
                              in_=ts["p_B"][:, 0:512]), inc="wc")
    em.wait(a, ("wc",))
    for pj in range(4):
        em.dma("qt", dests[pj],
               ts["s_xnb"][:, base + pj * 128:base + (pj + 1) * 128],
               sem="wd")
    em.wait(t, ("wc",))



def phase_f(em, ts):
    v, a, t, s = "vector", "scalar", "tensor", "sync"
    dram = ts["dram"]
    XO = ts["XO"]
    for tt in range(NTO):
        em.wait(s, ("vv",))
        em.dma("qs", ts["s_xt"][:], XO[tt * 128:(tt + 1) * 128, :])
        em.wait(v, ("qs",))
        emit_ln(em, ts, ts["s_xt"][:], ts["s_xn"][:])
        em.wait(v, ("qt",))
        emit_quant(em, ts, ts["s_xn"][:], ts["s_hqb"][:, 0:1024], 1024, 0)
        em.wait("scalar", ("vv",))
        xt3 = ts["s_xqT"][:].rearrange("p (hch t) -> p hch t", hch=8)[
            :, :, tt * 128: tt * 128 + 128]
        em.dmaT(xt3, ts["s_hqb"][:, 0:1024])
    em.wait(t, ("qt",))
    em.wait(v, ("qt",))
    gs = ts["s_gs"]
    nbF = 64
    scr = ts["s_gf"]
    lzt, lpt, lct, lvt, lut = (scr[:, 256:320], scr[:, 320:384],
                               scr[:, 384:448], scr[:, 448:512],
                               scr[:, 512:576])
    def fsmalls(par):
        if par == 0:
            return (ts["s_qam"][:, 0:64], ts["s_qs8"][:, 0:64],
                    ts["s_qdq"][:, 0:64], ts["s_qrd"][:, 0:64])
        return (scr[:, 0:64], scr[:, 64:128], scr[:, 128:192], scr[:, 192:256])
    def fhbuf(par):
        return ts["s_hid"][:] if par == 0 else ts["s_q4"][:]

    def emit_fladder(k):
        par = k % 2
        am, s8, dq, rd = fsmalls(par)
        hb = fhbuf(par)
        em.wait(v, ("vv", "wb"))
        em.op(v, lambda e: e.tensor_reduce(out=am, in_=q3v(hb, nbF), axis=AX.X,
                                           op=ALU.max, apply_absolute_value=True))
        em.drain(v)
        em.op(v, lambda e: e.tensor_scalar(out=lzt, in0=am,
                                           scalar1=gs[:, 4:5], scalar2=224.0,
                                           op0=ALU.mult, op1=ALU.min))
        em.op(v, lambda e: e.memset(lpt, 2.0 ** -7))
        em.drain(v)
        for kk in (8, 4, 2, 1):
            em.op(v, lambda e, kk=kk: e.tensor_scalar(out=lct, in0=lpt,
                                                      scalar1=float(2 ** kk),
                                                      scalar2=None, op0=ALU.mult))
            em.drain(v)
            em.op(v, lambda e: e.tensor_tensor(out=lct, in0=lct, in1=lzt,
                                               op=ALU.is_le))
            em.drain(v)
            em.op(v, lambda e, kk=kk: e.tensor_scalar(out=lct, in0=lct,
                                                      scalar1=float(2 ** kk - 1),
                                                      scalar2=1.0, op0=ALU.mult,
                                                      op1=ALU.add))
            em.drain(v)
            em.op(v, lambda e: e.tensor_tensor(out=lpt, in0=lpt, in1=lct,
                                               op=ALU.mult))
            em.drain(v)
        em.op(v, lambda e: e.reciprocal(out=lct, in_=lpt))
        em.drain(v)
        em.op(v, lambda e: e.scalar_tensor_tensor(out=lut, in0=lzt, scalar=8.0,
                                                  in1=lct, op0=ALU.mult,
                                                  op1=ALU.mult))
        em.op(v, lambda e: e.memset(lvt, 0.0))
        em.drain(v)
        for kk in (16, 8, 4, 2, 1):
            em.op(v, lambda e, kk=kk: e.scalar_tensor_tensor(
                out=lct, in0=lvt, scalar=float(kk) - 0.5, in1=lut,
                op0=ALU.add, op1=ALU.is_le))
            em.drain(v)
            em.op(v, lambda e, kk=kk: e.scalar_tensor_tensor(
                out=lvt, in0=lct, scalar=float(kk), in1=lvt,
                op0=ALU.mult, op1=ALU.add))
            em.drain(v)
        em.op(v, lambda e: e.scalar_tensor_tensor(out=s8, in0=lvt, scalar=0.125,
                                                  in1=lpt, op0=ALU.mult,
                                                  op1=ALU.mult))
        em.drain(v)
        em.op(v, lambda e: e.tensor_scalar(out=dq, in0=s8, scalar1=gs[:, 5:6],
                                           scalar2=None, op0=ALU.mult))
        em.op(v, lambda e: e.tensor_scalar(out=rd, in0=s8, scalar1=1e-30,
                                           scalar2=None, op0=ALU.max))
        em.drain(v)
        em.op(v, lambda e: e.reciprocal(out=rd, in_=rd))
        em.drain(v)
        em.op(v, lambda e: e.tensor_scalar(out=rd, in0=rd, scalar1=gs[:, 3:4],
                                           scalar2=None, op0=ALU.mult),
              inc="wa")
        em.drain(v)

    def emit_fval_head(k):
        par = k % 2
        am, s8, dq, rd = fsmalls(par)
        hb = fhbuf(par)
        q1 = ts["s_q1"][:]
        q2, sgn = ts["s_q2"][:], ts["s_qsg"][:]
        em.wait("gpsimd", ("wa", "wd", "vv"))
        em.op("gpsimd", lambda e: e.tensor_tensor(
            out=q3v(q1, nbF), in0=q3v(hb, nbF), in1=bcast16(rd, nbF),
            op=ALU.mult), inc="wb")
        em.wait(a, ("wb", "vv"))
        em.op(a, lambda e: e.activation(out=sgn, in_=q1, func=AF.Sign))
        em.op(a, lambda e: e.activation(out=q2, in_=q1, func=AF.Abs), inc="wc")

    def emit_fval_tail(k, oc):
        par = k % 2
        am, s8, dq, rd = fsmalls(par)
        q1, q2, q3 = ts["s_q1"][:], ts["s_q2"][:], ts["s_q3"][:]
        sgn = ts["s_qsg"][:]
        em.wait(v, ("wc",))
        em.op(v, lambda e: e.tensor_scalar(out=q3, in0=q2, scalar1=0.5,
                                           scalar2=None, op0=ALU.is_ge))
        em.drain(v)
        for th in (1.5, 2.5, 3.5):
            em.op(v, lambda e, th=th: e.scalar_tensor_tensor(
                out=q3, in0=q2, scalar=th, in1=q3, op0=ALU.is_ge, op1=ALU.add))
            em.drain(v)
        em.op(v, lambda e: e.tensor_scalar(out=q1, in0=q2, scalar1=5.0,
                                           scalar2=None, op0=ALU.is_ge))
        em.drain(v)
        em.op(v, lambda e: e.scalar_tensor_tensor(
            out=q1, in0=q2, scalar=7.0, in1=q1, op0=ALU.is_ge, op1=ALU.add))
        em.drain(v)
        em.op(v, lambda e: e.scalar_tensor_tensor(
            out=q3, in0=q1, scalar=2.0, in1=q3, op0=ALU.mult, op1=ALU.add))
        em.drain(v)
        em.op(v, lambda e: e.tensor_scalar(out=q1, in0=q2, scalar1=10.0,
                                           scalar2=None, op0=ALU.is_ge))
        em.drain(v)
        em.op(v, lambda e: e.scalar_tensor_tensor(
            out=q3, in0=q1, scalar=4.0, in1=q3, op0=ALU.mult, op1=ALU.add),
            inc="wd")
        em.drain(v)
        em.wait("gpsimd", ("wd", "qs"))
        em.op("gpsimd", lambda e: e.tensor_tensor(out=q3v(q3, nbF),
                                                  in0=q3v(q3, nbF),
                                                  in1=bcast16(dq, nbF),
                                                  op=ALU.mult))
        em.drain("gpsimd")
        em.op("gpsimd", lambda e: e.tensor_tensor(out=ts["s_hqb"][:], in0=q3,
                                                  in1=sgn, op=ALU.mult),
              inc="vv")
        em.drain("gpsimd")
        em.wait(s, ("vv",))
        em.dma("qs", dram["hq_scratch"][k * 128:(k + 1) * 128,
                                        oc * 1024:(oc + 1) * 1024],
               ts["s_hqb"][:])

    for oc in range(4):
        em.wait(s, ("tt",))
        for ch in range(8):
            em.dma("qs", ts["s_wgc"][:, ch * 1024:(ch + 1) * 1024],
                   dram["wgT_hbm"][ch, :, oc * 1024:(oc + 1) * 1024])
            em.dma("qs", ts["s_wuc"][:, ch * 1024:(ch + 1) * 1024],
                   dram["wuT_hbm"][ch, :, oc * 1024:(oc + 1) * 1024])
        em.wait(t, ("qs",))
        for tt in range(NTO):
            em.wait(t, ("qt", "aa"))
            for ch in range(8):
                for hf in range(2):
                    em.op(t, lambda e, ch=ch, tt=tt, hf=hf: e.matmul(
                        out=ts["p_A"][:, hf * 512:(hf + 1) * 512],
                        lhsT=ts["s_xqT"][:, ch * 2048 + tt * 128: ch * 2048 + (tt + 1) * 128],
                        rhs=ts["s_wgc"][:, ch * 1024 + hf * 512: ch * 1024 + (hf + 1) * 512],
                        start=(ch == 0), stop=(ch == 7)))
            for ch in range(8):
                for hf in range(2):
                    em.op(t, lambda e, ch=ch, tt=tt, hf=hf: e.matmul(
                        out=ts["p_D"][:, hf * 512:(hf + 1) * 512],
                        lhsT=ts["s_xqT"][:, ch * 2048 + tt * 128: ch * 2048 + (tt + 1) * 128],
                        rhs=ts["s_wuc"][:, ch * 1024 + hf * 512: ch * 1024 + (hf + 1) * 512],
                        start=(ch == 0), stop=(ch == 7)),
                        inc="tt" if (ch == 7 and hf == 1) else None)
            em.wait(a, ("tt",))
            em.op(a, lambda e: e.activation(out=ts["s_sg"][:], in_=ts["p_A"][:],
                                            func=AF.Silu), inc="aa")
            em.wait(v, ("aa", "wb"))
            em.op(v, lambda e, tt=tt: e.tensor_tensor(
                out=fhbuf(tt % 2), in0=ts["s_sg"][:], in1=ts["p_D"][:],
                op=ALU.mult), inc="qt")
            em.drain(v)
            if tt > 0:
                emit_fval_head(tt - 1)
            emit_fladder(tt)
            if tt > 0:
                emit_fval_tail(tt - 1, oc)
        emit_fval_head(NTO - 1)
        emit_fval_tail(NTO - 1, oc)
    em.wait(s, ("qt",))
    for ic in range(32):
        em.dma("qs", ts["s_wT"][:, ic * 1024:(ic + 1) * 1024],
               dram["wdT_hbm"][ic, :, :])
    em.wait(t, ("qs",))
    for tb in range(4):
        em.wait("scalar", ("qs",))
        outT = ts["s_hqT"][:].rearrange("p (ic r) -> p ic r", r=512)
        em.dmaT(outT, dram["hq_scratch"][tb * 512:(tb + 1) * 512, :])
        em.wait(t, ("qt",))
        for tsub in range(4):
            tt = tb * 4 + tsub
            em.wait(t, ("aa",))
            for ic in range(32):
                for half in range(2):
                    em.op(t, lambda e, ic=ic, half=half, tsub=tsub: e.matmul(
                        out=ts["p_A"][:, half * 512:(half + 1) * 512],
                        lhsT=ts["s_hqT"][:, ic * 512 + tsub * 128: ic * 512 + (tsub + 1) * 128],
                        rhs=ts["s_wT"][:, ic * 1024 + half * 512: ic * 1024 + (half + 1) * 512],
                        start=(ic == 0), stop=(ic == 31)),
                        inc="tt" if (ic == 31 and half == 1) else None)
            em.wait(a, ("tt", "qs", "vv"))
            em.op(a, lambda e: e.activation(out=ts["s_gf"][:], in_=ts["p_A"][:],
                                            func=AF.Silu), inc="aa")
            em.wait(s, ("aa", "vv"))
            em.dma("qs", ts["s_q1"][:], dram["o_woA"][tt * 128:(tt + 1) * 128, :])
            em.dma("qs", ts["s_q2"][:], dram["o_woB"][tt * 128:(tt + 1) * 128, :])
            em.dma("qs", ts["s_xt"][:], XO[tt * 128:(tt + 1) * 128, :])
            em.wait(v, ("aa", "qs"))
            em.op(v, lambda e: e.tensor_scalar(
                out=ts["s_q1"][:], in0=ts["s_q1"][:],
                scalar1=ts["s_hsel"][:, 0:1], scalar2=None, op0=ALU.mult))
            em.op(v, lambda e: e.tensor_scalar(
                out=ts["s_q2"][:], in0=ts["s_q2"][:],
                scalar1=ts["s_hsel"][:, 1:2], scalar2=None, op0=ALU.mult))
            em.drain(v)
            em.op(v, lambda e: e.tensor_tensor(out=ts["s_q1"][:], in0=ts["s_q1"][:],
                                               in1=ts["s_q2"][:], op=ALU.add))
            em.drain(v)
            em.op(v, lambda e: e.tensor_tensor(out=ts["s_q3"][:], in0=ts["s_q1"][:],
                                               in1=ts["s_gf"][:], op=ALU.mult))
            em.drain(v)
            em.op(v, lambda e: e.tensor_tensor(out=ts["s_q3"][:], in0=ts["s_q3"][:],
                                               in1=ts["s_xt"][:], op=ALU.add),
                  inc="vv")
            em.drain(v)
            em.wait(s, ("vv",))
            em.dma("qs", ts["OUT"][tt * 128:(tt + 1) * 128, :], ts["s_q3"][:])



def phase_a(em, ts):
    v, a, t, s = "vector", "scalar", "tensor", "sync"
    dram = ts["dram"]
    X = ts["X"]
    msk = ts["s_msk"]
    trilS = msk[:, 0:64]
    triuI = msk[:, 64:128]
    id4 = msk[:, 128:384]
    LP = L + 16

    GWh, UWh, DWh = dram["gate_w"], dram["up_w"], dram["down_w"]
    WGTh, WUTh, WDTh = dram["wgT_hbm"], dram["wuT_hbm"], dram["wdT_hbm"]
    WSCh = dram["wsc_scratch"]
    amax_tasks = []
    for mi, (W, gcol) in enumerate(((GWh, 6), (UWh, 9))):
        for pp in range(32):
            amax_tasks.append(lambda W=W, mi=mi, pp=pp: emit_wamax_pair(
                em, ts,
                W[pp * 128:(pp + 1) * 128, 0:512],
                W[pp * 128:(pp + 1) * 128, 512:1024],
                WSCh[mi, :, 64 * pp:64 * pp + 32],
                WSCh[mi, :, 64 * pp + 32:64 * pp + 64], pp % 2))
        amax_tasks.append(lambda gcol=gcol: emit_wbounce(em, ts, gcol))
        for p in range(4):
            amax_tasks.append(lambda mi=mi, gcol=gcol, p=p:
                              emit_wladder(em, ts, mi, gcol, p))
    for rc in range(8):
        for j in range(4):
            amax_tasks.append(lambda rc=rc, j=j: emit_wamax_pair(
                em, ts,
                DWh[rc * 128:(rc + 1) * 128, 1024 * j:1024 * j + 512],
                DWh[rc * 128:(rc + 1) * 128, 1024 * j + 512:1024 * (j + 1)],
                WSCh[2, :, 256 * rc + 64 * j:256 * rc + 64 * j + 32],
                WSCh[2, :, 256 * rc + 64 * j + 32:256 * rc + 64 * (j + 1)],
                (4 * rc + j) % 2))
    amax_tasks.append(lambda: emit_wbounce(em, ts, 12))
    for p in range(4):
        amax_tasks.append(lambda p=p: emit_wladder(em, ts, 2, 12, p))
    quant_tasks = []
    for mi, (W, WT) in enumerate(((GWh, WGTh), (UWh, WUTh))):
        for pp in range(32):
            for h in range(2):
                dests = [WT[4 * h + j, :, pp * 128:(pp + 1) * 128]
                         for j in range(4)]
                quant_tasks.append(
                    (mi, 2 * pp + h,
                     W[pp * 128:(pp + 1) * 128, h * 512:(h + 1) * 512], dests))
    for rc in range(8):
        for cp in range(8):
            dests = [WDTh[4 * cp + j, :, rc * 128:(rc + 1) * 128]
                     for j in range(4)]
            quant_tasks.append(
                (2, 8 * rc + cp,
                 DWh[rc * 128:(rc + 1) * 128, cp * 512:(cp + 1) * 512], dests))
    n_amax = len(amax_tasks)
    n_quant = len(quant_tasks)

    for wname, sbname, dout in [("Wq", "s_Wq", 512), ("Wk", "s_Wk", 512),
                                ("Wv", "s_Wv", 1024), ("Wb", "s_Wb", 4),
                                ("Wg", "s_Wg", 1024), ("Wo", "s_Wo", 1024)]:
        for ch in range(8):
            em.wait(s, ("vv",))
            em.dma("qs", ts["s_xt"][:, 0:dout],
                   dram[wname][ch * 128:(ch + 1) * 128, :])
            em.wait(v, ("qs",))
            em.op(v, lambda e, sbname=sbname, ch=ch, dout=dout: e.tensor_copy(
                out=ts[sbname][:, ch * dout:(ch + 1) * dout],
                in_=ts["s_xt"][:, 0:dout]), inc="vv")
            em.drain(v)

    em.op(v, lambda e: e.memset(ts["s_Sbf"][:], 0.0))
    em.op(v, lambda e: e.memset(ts["s_Sf"][:], 0.0))
    em.op(v, lambda e: e.memset(ts["s_mh"][:], -0.5), inc="vv")
    em.wait("gpsimd", ("vv",))
    for ch in range(8):
        em.op(v, lambda e, ch=ch: e.memset(ts["s_xnT"][:, ch * LP:ch * LP + 16], 0.0))
    em.op(v, lambda e: e.memset(ts["s_uf"][:], 0.0), inc="vv")

    adone = 0
    for tt in range(NT):
        xb = ts["s_xt"] if tt % 2 == 0 else ts["s_o"]
        psem = "wa" if tt % 2 == 0 else "wb"
        if tt < 2:
            em.wait(s, ("vv",))
        em.wait(s, (psem,))
        em.dma("qs", xb[:], X[tt * 128:(tt + 1) * 128, :])
        em.wait(v, ("qs",))
        emit_ln(em, ts, xb[:], ts["s_xn"][:])
        em.op(v, lambda e: e.nop(), inc=psem)
        em.wait(v, ("qt",))
        em.op(v, lambda e: e.tensor_copy(
            out=ts["s_xnb"][:, 0:1024], in_=ts["s_xn"][:]), inc="vv")
        em.drain(v)
        em.wait("scalar", ("vv",))
        nt3 = ts["s_xnT"][:].rearrange("p (dch t) -> p dch t", dch=8)[
            :, :, 16 + tt * 128: 16 + tt * 128 + 128]
        em.dmaT(nt3, ts["s_xnb"][:, 0:1024])
        want = (tt + 1) * n_amax // NT
        while adone < want:
            amax_tasks[adone]()
            adone += 1
    em.wait(v, ("qt",))
    em.wait(t, ("qt",))

    qp = {"ld": 0, "ch": 0, "tr": 0}
    def qstep(lim):
        if qp["tr"] < qp["ch"] and (qp["ch"] - qp["tr"] >= 2 or
                                    qp["ch"] >= min(n_quant, lim)):
            mi, sidx, src, dests = quant_tasks[qp["tr"]]
            emit_wtrans(em, ts, qp["tr"] % 2, dests)
            qp["tr"] += 1
        if qp["ch"] < qp["ld"] and qp["ch"] < min(n_quant, lim):
            emit_wchain(em, ts, qp["ch"] % 2)
            qp["ch"] += 1
        if qp["ld"] < min(n_quant, lim + 1) and qp["ld"] <= qp["ch"]:
            mi, sidx, src, dests = quant_tasks[qp["ld"]]
            emit_wload(em, ts, mi, sidx, src)
            qp["ld"] += 1
    for tt in range(NT):
        cw3 = ts["s_cw"][:].rearrange("p (c i) -> p c i", i=4)
        base = 13 + tt * 128
        def xv(i, base=base):
            return ts["s_xnT"][:].rearrange("p (c t) -> p c t", c=8)[
                :, :, base + i: base + i + 128]
        q13 = ts["s_q1"][:].rearrange("p (c t) -> p c t", t=128)
        q23 = ts["s_q2"][:].rearrange("p (c t) -> p c t", t=128)
        em.wait(v, ("aa", "qs"))
        em.op(v, lambda e: e.tensor_tensor(
            out=q13, in0=xv(0), in1=cw3[:, :, 0:1].broadcast_to((128, 8, 128)),
            op=ALU.mult))
        for i in (1, 2, 3):
            em.op(v, lambda e, i=i: e.tensor_tensor(
                out=q23, in0=xv(i), in1=cw3[:, :, i:i + 1].broadcast_to((128, 8, 128)),
                op=ALU.mult))
            em.drain(v)
            em.op(v, lambda e: e.tensor_tensor(out=ts["s_q1"][:], in0=ts["s_q1"][:],
                                               in1=ts["s_q2"][:], op=ALU.add))
            em.drain(v)
        em.drain(v)
        em.op(v, lambda e: e.nop(), inc="vv")
        em.wait(a, ("vv",))
        em.op(a, lambda e: e.activation(out=ts["s_hT"][:], in_=ts["s_q1"][:],
                                        func=AF.Silu), inc="aa")
        em.wait(t, ("aa", "vv"))

        for ch in range(8):
            hsl = ts["s_hT"][:, ch * 128:(ch + 1) * 128]
            em.op(t, lambda e, ch=ch, hsl=hsl: e.matmul(
                out=ts["p_A"][:, 0:512], lhsT=hsl,
                rhs=ts["s_Wq"][:, ch * 512:(ch + 1) * 512],
                start=(ch == 0), stop=(ch == 7)))
            em.op(t, lambda e, ch=ch, hsl=hsl: e.matmul(
                out=ts["p_A"][:, 512:1024], lhsT=hsl,
                rhs=ts["s_Wk"][:, ch * 512:(ch + 1) * 512],
                start=(ch == 0), stop=(ch == 7)))
            em.op(t, lambda e, ch=ch, hsl=hsl: e.matmul(
                out=ts["p_D"][:, 0:4], lhsT=hsl,
                rhs=ts["s_Wb"][:, ch * 4:(ch + 1) * 4],
                start=(ch == 0), stop=(ch == 7)), inc="tt" if ch == 7 else None)
        em.wait(a, ("tt",))
        st = ts["s_st"]
        em.wait(v, ("tt", "qt"))
        em.op(v, lambda e: e.tensor_copy(out=ts["s_qk"][:], in_=ts["p_A"][:]),
              inc="vv")
        em.drain(v)
        em.wait(a, ("vv",))
        for h in range(8):
            em.op(a, lambda e, h=h: e.activation(
                out=ts["s_q3"][:, h * 128:(h + 1) * 128],
                in_=ts["s_qk"][:, h * 128:(h + 1) * 128],
                func=AF.Square, accum_out=st[:, 4 + h if h < 4 else h - 4:
                                             5 + h if h < 4 else h - 3]))
        em.op(a, lambda e: e.activation(out=ts["s_beta"][:], in_=ts["p_D"][:, 0:4],
                                        func=AF.Sigmoid), inc="aa")
        em.wait("scalar", ("vv", "tt"))
        em.dmaT(ts["s_qkT"][:].rearrange("p (c r) -> p c r", r=128), ts["s_qk"][:])
        qstep(8 * (tt + 1))
        em.wait("gpsimd", ("aa",))
        em.op("gpsimd", lambda e: e.tensor_scalar(out=st[:, 8:16], in0=st[:, 0:8],
                                                  scalar1=1e-24, scalar2=None,
                                                  op0=ALU.max))
        em.drain("gpsimd")
        em.op("gpsimd", lambda e: e.tensor_tensor(
            out=st[:, 8:16], in0=st[:, 8:16],
            in1=ts["s_mh"][:].broadcast_to((128, 8)), op=ALU.pow))
        em.drain("gpsimd")
        em.op("gpsimd", lambda e: e.tensor_scalar(out=st[:, 8:16], in0=st[:, 8:16],
                                                  scalar1=1e12, scalar2=None,
                                                  op0=ALU.min))
        em.drain("gpsimd")
        em.op("gpsimd", lambda e: e.tensor_tensor(out=st[:, 16:20], in0=st[:, 8:12],
                                                  in1=ts["s_beta"][:], op=ALU.mult))
        em.drain("gpsimd")
        em.op("gpsimd", lambda e: e.tensor_copy(out=st[:, 20:24], in_=st[:, 16:20]))
        em.drain("gpsimd")
        em.op("gpsimd", lambda e: e.tensor_tensor(out=st[:, 16:20], in0=st[:, 16:20],
                                                  in1=st[:, 8:12], op=ALU.mult),
              inc="gg")
        em.wait(v, ("gg",))
        em.wait(t, ("gg", "qt"))
        qstep(8 * (tt + 1))
        for ch in range(8):
            hsl = ts["s_hT"][:, ch * 128:(ch + 1) * 128]
            for half in range(2):
                em.op(t, lambda e, ch=ch, hsl=hsl, half=half: e.matmul(
                    out=ts["p_A"][:, half * 512:(half + 1) * 512], lhsT=hsl,
                    rhs=ts["s_Wv"][:, ch * 1024 + half * 512: ch * 1024 + (half + 1) * 512],
                    start=(ch == 0), stop=(ch == 7)),
                    inc="tt" if (ch == 7 and half == 1) else None)
        em.wait(v, ("tt",))
        em.op(v, lambda e: e.tensor_copy(out=ts["s_v"][:], in_=ts["p_A"][:]), inc="vv")
        qstep(8 * (tt + 1))
        em.wait(t, ("vv",))
        for ch in range(8):
            hsl = ts["s_hT"][:, ch * 128:(ch + 1) * 128]
            for half in range(2):
                em.op(t, lambda e, ch=ch, hsl=hsl, half=half: e.matmul(
                    out=ts["p_A"][:, half * 512:(half + 1) * 512], lhsT=hsl,
                    rhs=ts["s_Wg"][:, ch * 1024 + half * 512: ch * 1024 + (half + 1) * 512],
                    start=(ch == 0), stop=(ch == 7)),
                    inc="tt" if (ch == 7 and half == 1) else None)
        em.wait(v, ("tt",))
        em.op(v, lambda e: e.tensor_copy(out=ts["s_g"][:], in_=ts["p_A"][:]), inc="vv")
        em.drain(v)
        qstep(8 * (tt + 1))

        for c in range(2):
            hb = slice(64 * c, 64 * c + 64)
            tpK = 64 * c
            emit_chunk(em, ts, tt, c, hb, tpK, trilS, triuI, id4)
            qstep(8 * (tt + 1))

        st = ts["s_st"]
        em.wait(a, ("vv", "qs"))
        for h in range(4):
            em.op(a, lambda e, h=h: e.activation(
                out=ts["s_q2"][:, h * 256:(h + 1) * 256],
                in_=ts["s_o"][:, h * 256:(h + 1) * 256],
                func=AF.Square, accum_out=st[:, h:h + 1]))
        em.op(a, lambda e: e.activation(out=ts["s_q4"][:], in_=ts["s_g"][:],
                                        func=AF.Silu), inc="aa")
        em.wait("gpsimd", ("aa",))
        em.op("gpsimd", lambda e: e.tensor_scalar(out=st[:, 4:8], in0=st[:, 0:4],
                                                  scalar1=1.0 / 256.0, scalar2=1e-5,
                                                  op0=ALU.mult, op1=ALU.add))
        em.drain("gpsimd")
        em.op("gpsimd", lambda e: e.tensor_tensor(
            out=st[:, 4:8], in0=st[:, 4:8],
            in1=ts["s_mh"][:].broadcast_to((128, 4)), op=ALU.pow), inc="gg")
        em.wait(v, ("gg", "aa"))
        o4 = ts["s_o"][:].rearrange("p (h f) -> p h f", f=256)
        rs4 = st[:, 4:8].rearrange("p (h o) -> p h o", o=1).broadcast_to((128, 4, 256))
        em.op(v, lambda e: e.tensor_tensor(out=o4, in0=o4, in1=rs4, op=ALU.mult))
        em.drain(v)
        nw4 = ts["s_nw"][:].rearrange("p (o f) -> p o f", o=1).broadcast_to((128, 4, 256))
        em.op(v, lambda e: e.tensor_tensor(out=o4, in0=o4, in1=nw4, op=ALU.mult))
        em.drain(v)
        em.wait(v, ("qt",))
        em.op(v, lambda e: e.tensor_tensor(out=ts["s_gtd"][:], in0=ts["s_o"][:],
                                           in1=ts["s_q4"][:], op=ALU.mult), inc="vv")
        em.drain(v)
        qstep(8 * (tt + 1))
        em.wait("scalar", ("vv",))
        outT = ts["s_gTt"][:].rearrange("p (c r) -> p c r", r=128)
        em.dmaT(outT, ts["s_gtd"][:])
        em.wait(t, ("qt", "aa", "vv"))
        for ch in range(8):
            for half in range(2):
                em.op(t, lambda e, ch=ch, half=half: e.matmul(
                    out=ts["p_A"][:, half * 512:(half + 1) * 512],
                    lhsT=ts["s_gTt"][:, ch * 128:(ch + 1) * 128],
                    rhs=ts["s_Wo"][:, ch * 1024 + half * 512: ch * 1024 + (half + 1) * 512],
                    start=(ch == 0), stop=(ch == 7)),
                    inc="tt" if (ch == 7 and half == 1) else None)
        em.wait(v, ("tt", "qs"))
        em.op(v, lambda e: e.tensor_copy(out=ts["s_q2"][:], in_=ts["p_A"][:]),
              inc="vv")
        em.drain(v)
        qstep(8 * (tt + 1))
        em.wait(s, ("vv",))
        rA = tt if tt < NTO else NTO
        rB = tt - NTO if tt >= NTO else NTO
        em.dma("qs", dram["o_woA"][rA * 128:(rA + 1) * 128, :], ts["s_q2"][:])
        em.dma("qs", dram["o_woB"][rB * 128:(rB + 1) * 128, :], ts["s_q2"][:])
        while qp["ch"] < min(n_quant, (tt + 1) * 8) or \
                (tt == NT - 1 and qp["tr"] < n_quant):
            qstep((tt + 1) * 8)


def emit_chunk(em, ts, tt, c, hb, tpK, trilS, triuI, id4):
    v, a, t = "vector", "scalar", "tensor"
    st = ts["s_st"]
    tok0 = 64 * c
    tpO = (0, 64 * c)
    tpKK = (64 * c, 64 * c)

    em.wait(t, ("vv", "aa"))
    for h in range(4):
        kTc = ts["s_qkT"][:, 512 + h * 128 + tok0: 512 + h * 128 + tok0 + 64]
        em.op(t, lambda e, h=h, kTc=kTc: e.matmul(
            out=ts["p_C"][hb, h * 64:(h + 1) * 64], lhsT=kTc, rhs=kTc,
            start=True, stop=True, tile_position=tpO),
            inc="tt" if h == 3 else None)
    em.wait(v, ("tt",))
    g3 = ts["p_C"][hb, :].rearrange("p (h f) -> p h f", f=64)
    tril3 = trilS[hb, :].rearrange("p (o f) -> p o f", o=1).broadcast_to((64, 4, 64))
    em.op(v, lambda e: e.tensor_tensor(out=ts["s_C"][hb, :].rearrange(
        "p (h f) -> p h f", f=64), in0=g3, in1=tril3, op=ALU.mult))
    em.drain(v)
    s3 = st[hb, 16:20].rearrange("p (h o) -> p h o", o=1).broadcast_to((64, 4, 64))
    em.op(v, lambda e: e.tensor_tensor(out=ts["s_dg"][hb, :].rearrange(
        "p (h f) -> p h f", f=64), in0=id4[hb, :].rearrange("p (h f) -> p h f", f=64),
        in1=s3, op=ALU.mult))
    em.op(v, lambda e: e.tensor_tensor(out=ts["s_At"][hb, :].rearrange(
        "p (h f) -> p h f", f=64), in0=ts["s_C"][hb, :].rearrange(
        "p (h f) -> p h f", f=64), in1=s3, op=ALU.mult), inc="vv")
    em.drain(v)
    em.wait(t, ("vv",))
    for h in range(4):
        em.op(t, lambda e, h=h: e.matmul(
            out=ts["p_B"][hb, h * 64:(h + 1) * 64],
            lhsT=ts["s_C"][hb, h * 64:(h + 1) * 64],
            rhs=ts["s_dg"][hb, h * 64:(h + 1) * 64],
            start=True, stop=True, tile_position=tpKK),
            inc="tt" if h == 3 else None)
    em.wait(v, ("tt",))
    em.op(v, lambda e: e.tensor_copy(out=ts["s_AtT"][hb, :], in_=ts["p_B"][hb, 0:256]),
          inc="vv")
    em.drain(v)
    em.wait(t, ("vv",))
    for h in range(4):
        kTc = ts["s_qkT"][:, 512 + h * 128 + tok0: 512 + h * 128 + tok0 + 64]
        em.op(t, lambda e, h=h, kTc=kTc: e.matmul(
            out=ts["p_A"][hb, h * 256:(h + 1) * 256], lhsT=kTc,
            rhs=ts["s_Sbf"][:, h * 256:(h + 1) * 256],
            start=True, stop=True, tile_position=tpO),
            inc="tt" if h == 3 else None)
    def pwY(Lv):
        return ts["s_pw"][hb, Lv * 512: Lv * 512 + 256]
    def pwYT(Lv):
        return ts["s_pw"][hb, Lv * 512 + 256: Lv * 512 + 512]
    prevY, prevYT = ts["s_At"], ts["s_AtT"]
    for Lv in range(4):
        em.wait(t, ("vv",))
        for h in range(4):
            hs = slice(h * 64, (h + 1) * 64)
            em.op(t, lambda e, h=h, hs=hs: e.matmul(
                out=ts["p_B"][hb, h * 64:(h + 1) * 64],
                lhsT=prevYT[hb, hs], rhs=prevY[hb, hs],
                start=True, stop=True, tile_position=tpKK))
            em.op(t, lambda e, h=h, hs=hs: e.matmul(
                out=ts["p_B"][hb, 256 + h * 64:256 + (h + 1) * 64],
                lhsT=prevY[hb, hs], rhs=prevYT[hb, hs],
                start=True, stop=True, tile_position=tpKK),
                inc="tt" if h == 3 else None)
        em.wait(v, ("tt",))
        if Lv < 3:
            em.op(v, lambda e, Lv=Lv: e.tensor_copy(
                out=ts["s_pw"][hb, Lv * 512:(Lv + 1) * 512], in_=ts["p_B"][hb, :]),
                inc="vv")
        else:
            em.op(v, lambda e, Lv=Lv: e.tensor_copy(
                out=ts["s_pw"][hb, Lv * 512:(Lv + 1) * 512], in_=ts["p_B"][hb, :]),
                inc="vv")
        em.drain(v)
        prevY, prevYT = None, None
        prevY = _PwSlice(ts, Lv, 0)
        prevYT = _PwSlice(ts, Lv, 1)
    em.wait(v, ("tt",))
    ks3 = ts["p_A"][hb, :].rearrange("p (h f) -> p h f", f=256)
    rk3 = st[hb, 8:12].rearrange("p (h o) -> p h o", o=1).broadcast_to((64, 4, 256))
    u3 = ts["s_uf"][hb, :].rearrange("p (h f) -> p h f", f=256)
    v3 = ts["s_v"][hb, :].rearrange("p (h f) -> p h f", f=256)
    em.op(v, lambda e: e.tensor_tensor(out=u3, in0=ks3, in1=rk3, op=ALU.mult))
    em.drain(v)
    em.op(v, lambda e: e.tensor_tensor(out=u3, in0=v3, in1=u3, op=ALU.subtract))
    em.drain(v)
    rkb3 = st[hb, 20:24].rearrange("p (h o) -> p h o", o=1).broadcast_to((64, 4, 256))
    ub3 = ts["s_ub"][hb, :].rearrange("p (h f) -> p h f", f=256)
    em.op(v, lambda e: e.tensor_tensor(out=ub3, in0=u3, in1=rkb3, op=ALU.mult),
          inc="vv")
    em.drain(v)
    em.wait(t, ("vv",))
    for h in range(4):
        hs = slice(h * 64, (h + 1) * 64)
        em.op(t, lambda e, h=h, hs=hs: e.matmul(
            out=ts["p_B"][hb, h * 64:(h + 1) * 64],
            lhsT=_PwSlice(ts, 3, 0)[hb, hs], rhs=_PwSlice(ts, 3, 1)[hb, hs],
            start=True, stop=True, tile_position=tpKK),
            inc="tt" if h == 3 else None)
    em.wait(v, ("tt",))
    em.op(v, lambda e: e.tensor_copy(out=ts["s_Lt"][hb, :], in_=ts["p_B"][hb, 0:256]),
          inc="vv")
    em.drain(v)
    app_seq = [(ts["s_Lt"], 1.0), (_PwSlice(ts, 3, 1), 1.0), (_PwSlice(ts, 2, 1), 1.0),
               (_PwSlice(ts, 1, 1), 1.0), (_PwSlice(ts, 0, 1), 1.0),
               (ts["s_AtT"], -1.0)]
    for lhsT_t, sgn in app_seq:
        em.wait(t, ("vv",))
        for h in range(4):
            em.op(t, lambda e, h=h, lhsT_t=lhsT_t: e.matmul(
                out=ts["p_A"][hb, h * 256:(h + 1) * 256],
                lhsT=lhsT_t[hb, h * 64:(h + 1) * 64],
                rhs=ts["s_ub"][hb, h * 256:(h + 1) * 256],
                start=True, stop=True, tile_position=tpKK),
                inc="tt" if h == 3 else None)
        em.wait(v, ("tt",))
        em.op(v, lambda e, sgn=sgn: e.scalar_tensor_tensor(
            out=ts["s_ub"][hb, :], in0=ts["p_A"][hb, :], scalar=sgn,
            in1=ts["s_ub"][hb, :], op0=ALU.mult, op1=ALU.add), inc="vv")
        em.drain(v)
    em.wait(t, ("vv",))
    for h in range(4):
        qTc = ts["s_qkT"][:, h * 128 + tok0: h * 128 + tok0 + 64]
        em.op(t, lambda e, h=h, qTc=qTc: e.matmul(
            out=ts["p_D"][hb, h * 256:(h + 1) * 256], lhsT=qTc,
            rhs=ts["s_Sbf"][:, h * 256:(h + 1) * 256],
            start=(h % 2 == 0), stop=False, tile_position=tpO))
    for h in range(4):
        qTc = ts["s_qkT"][:, h * 128 + tok0: h * 128 + tok0 + 64]
        kTc = ts["s_qkT"][:, 512 + h * 128 + tok0: 512 + h * 128 + tok0 + 64]
        em.op(t, lambda e, h=h, qTc=qTc, kTc=kTc: e.matmul(
            out=ts["p_C"][hb, h * 64:(h + 1) * 64], lhsT=kTc, rhs=qTc,
            start=True, stop=True, tile_position=tpO),
            inc="tt" if h == 3 else None)
    em.wait(v, ("tt",))
    r3 = ts["p_C"][hb, :].rearrange("p (h f) -> p h f", f=64)
    triu3 = triuI[hb, :].rearrange("p (o f) -> p o f", o=1).broadcast_to((64, 4, 64))
    em.op(v, lambda e: e.tensor_tensor(out=ts["s_Lt"][hb, :].rearrange(
        "p (h f) -> p h f", f=64), in0=r3, in1=triu3, op=ALU.mult), inc="vv")
    em.drain(v)
    em.wait(t, ("vv",))
    for h in range(4):
        em.op(t, lambda e, h=h: e.matmul(
            out=ts["p_D"][hb, h * 256:(h + 1) * 256],
            lhsT=ts["s_Lt"][hb, h * 64:(h + 1) * 64],
            rhs=ts["s_ub"][hb, h * 256:(h + 1) * 256],
            start=False, stop=(h % 2 == 1), tile_position=tpKK),
            inc="tt" if h == 3 else None)
    for h in range(4):
        em.op(t, lambda e, h=h: e.matmul(
            out=ts["p_S"][:, h * 256:(h + 1) * 256],
            lhsT=ts["s_qk"][hb, 512 + h * 128: 512 + (h + 1) * 128],
            rhs=ts["s_ub"][hb, h * 256:(h + 1) * 256],
            start=True, stop=True, tile_position=(64 * c, 0)),
            inc="tt" if h == 3 else None)
    em.wait(v, ("tt",))
    o3 = ts["s_o"][hb, :].rearrange("p (h f) -> p h f", f=256)
    rq3 = st[hb, 12:16].rearrange("p (h o) -> p h o", o=1).broadcast_to((64, 4, 256))
    em.op(v, lambda e: e.tensor_tensor(out=o3, in0=ts["p_D"][hb, :].rearrange(
        "p (h f) -> p h f", f=256), in1=rq3, op=ALU.mult))
    em.op(v, lambda e: e.scalar_tensor_tensor(
        out=ts["s_Sf"][:], in0=ts["p_S"][:], scalar=1.0, in1=ts["s_Sf"][:],
        op0=ALU.mult, op1=ALU.add))
    em.drain(v)
    em.op(v, lambda e: e.tensor_copy(out=ts["s_Sbf"][:], in_=ts["s_Sf"][:]), inc="vv")
    em.drain(v)


class _PwSlice:
    def __init__(self, ts, Lv, which):
        self.t = ts["s_pw"]
        self.base = Lv * 512 + which * 256
    def __getitem__(self, key):
        hb, hs = key
        if isinstance(hs, slice):
            return self.t[hb, self.base + hs.start: self.base + hs.stop]
        raise KeyError(key)



_HOST_CONSTS = None

def _host_consts():
    global _HOST_CONSTS
    if _HOST_CONSTS is None:
        tril = np.tril(np.ones((64, 64), np.float32), -1)
        triu = np.triu(np.ones((64, 64), np.float32), 0)
        ident = np.eye(64, dtype=np.float32)
        id4 = np.concatenate([ident] * 4, axis=1)
        row = np.concatenate([tril, triu, id4], axis=1)
        base = np.concatenate([row, row], axis=0)
        _HOST_CONSTS = np.ascontiguousarray(
            np.concatenate([base, np.eye(128, dtype=np.float32)], axis=1))
    return _HOST_CONSTS


def kernel(**inputs):
    x = np.ascontiguousarray(inputs["L0_attn_hidden_states"], dtype=np.float32)
    consts = _host_consts()
    nw_rep = np.ascontiguousarray(
        np.broadcast_to(inputs["L0_attn_norm_w"][None, :], (128, 256)), np.float32)
    sin_rep = np.ascontiguousarray(
        np.broadcast_to(inputs["L0_ffn_s_in"].reshape(1, 1), (128, 1)), np.float32)
    sid_rep = np.ascontiguousarray(
        np.broadcast_to(inputs["L0_ffn_s_in_down"].reshape(1, 1), (128, 1)), np.float32)
    hsel = [np.ascontiguousarray(
        np.broadcast_to(np.array([[1.0 - h, float(h)]], np.float32), (128, 2)))
        for h in (0, 1)]
    shared = {
        "Wq": np.ascontiguousarray(inputs["L0_attn_Wq"], np.float32),
        "Wk": np.ascontiguousarray(inputs["L0_attn_Wk"], np.float32),
        "Wv": np.ascontiguousarray(inputs["L0_attn_Wv"], np.float32),
        "Wb": np.ascontiguousarray(inputs["L0_attn_Wb"], np.float32),
        "Wg": np.ascontiguousarray(inputs["L0_attn_Wg"], np.float32),
        "Wo": np.ascontiguousarray(inputs["L0_attn_Wo"], np.float32),
        "conv_w": np.ascontiguousarray(inputs["L0_attn_conv_w"], np.float32),
        "norm_w_rep": nw_rep,
        "s_in_rep": sin_rep,
        "s_in_down_rep": sid_rep,
        "masks": consts,
        "gate_w": np.ascontiguousarray(inputs["L0_ffn_gate_w"], np.float32),
        "up_w": np.ascontiguousarray(inputs["L0_ffn_up_w"], np.float32),
        "down_w": np.ascontiguousarray(inputs["L0_ffn_down_w"], np.float32),
    }
    in_maps = []
    for c in range(8):
        b, half = c // 2, c % 2
        m = dict(shared)
        m["x_seq"] = np.ascontiguousarray(x[b])
        m["x_own"] = np.ascontiguousarray(x[b, half * HALF:(half + 1) * HALF])
        m["half_sel_rep"] = hsel[half]
        in_maps.append(m)
    nc = build_program()
    res = run_bass_kernel_spmd(nc, in_maps, core_ids=list(range(8)))
    out = np.empty((B, L, D), np.float32)
    for c in range(8):
        b, half = c // 2, c % 2
        out[b, half * HALF:(half + 1) * HALF] = res.results[c]["out_half"]
    return out


# version 18
# speedup vs baseline: 1.0185x; 1.0028x over previous
import numpy as np
import ml_dtypes
from contextlib import ExitStack
import concourse.bass as bass
import concourse.mybir as mybir
from concourse.bass_isa import ReduceOp
from concourse import library_config
from concourse.bass_utils import run_bass_kernel_spmd

F32 = mybir.dt.float32
BF16 = mybir.dt.bfloat16
FP8 = mybir.dt.float8e4
AF = mybir.ActivationFunctionType
ALU = mybir.AluOpType
AX = mybir.AxisListType

B, L, D = 4, 4096, 1024
HALF = L // 2
H, DQH, DVH = 4, 128, 256
NT = L // 128
NTO = HALF // 128
DI = 4096

SEM_NAMES = ("qs", "qt", "vv", "aa", "tt", "gg", "wa", "wb", "wc", "wd")


def build_program():
    nc = bass.Bass("TRN2", target_bir_lowering=False)

    dram = {}
    def din(name, shape, dt=F32):
        dram[name] = nc.dram_tensor(name, list(shape), dt, kind="ExternalInput").ap()
        return dram[name]
    def dscr(name, shape, dt):
        dram[name] = nc.dram_tensor(name, list(shape), dt, kind="Internal").ap()
        return dram[name]

    X = din("x_seq", [L, D])
    XO = din("x_own", [HALF, D])
    WQ = din("Wq", [D, 512]); WK = din("Wk", [D, 512]); WV = din("Wv", [D, 1024])
    WB = din("Wb", [D, 4]); WG = din("Wg", [D, 1024]); WO = din("Wo", [D, 1024])
    CW = din("conv_w", [D, 4])
    NW = din("norm_w_rep", [128, 256])
    GW = din("gate_w", [DI, D]); UW = din("up_w", [DI, D])
    DW = din("down_w", [D, DI])
    SIN = din("s_in_rep", [128, 1]); SID = din("s_in_down_rep", [128, 1])
    HSEL = din("half_sel_rep", [128, 2])
    MSK = din("masks", [128, 6 * 64 + 128])
    OUT = nc.dram_tensor("out_half", [HALF, D], F32, kind="ExternalOutput").ap()
    HQS = dscr("hq_scratch", [HALF, DI], BF16)
    WMX = dscr("wmax_scratch", [128], F32)
    OWA = dscr("o_woA", [(NTO + 1) * 128, D], F32)
    OWB = dscr("o_woB", [(NTO + 1) * 128, D], F32)
    WGT = dscr("wgT_hbm", [8, 128, DI], BF16)
    WUT = dscr("wuT_hbm", [8, 128, DI], BF16)
    WDT = dscr("wdT_hbm", [32, 128, D], BF16)
    WSC = dscr("wsc_scratch", [3, 128, 2048], F32)
    WDQ = dscr("wdq_scratch", [3, 128, 2048], F32)
    WRD = dscr("wrd_scratch", [3, 128, 2048], F32)

    st_outer = ExitStack()
    en = st_outer.enter_context
    sem = {n: en(nc.semaphore(n)) for n in SEM_NAMES}

    s_msk = en(nc.sbuf_tensor("s_msk", [128, 6 * 64 + 128], F32))
    s_nw = en(nc.sbuf_tensor("s_nw", [128, 256], F32))
    s_cw = en(nc.sbuf_tensor("s_cw", [128, 8 * 4], F32))
    s_gs = en(nc.sbuf_tensor("s_gs", [128, 16], F32))
    s_q1 = en(nc.sbuf_tensor("s_q1", [128, 1024], F32))
    s_q2 = en(nc.sbuf_tensor("s_q2", [128, 1024], F32))
    s_q3 = en(nc.sbuf_tensor("s_q3", [128, 1024], F32))
    s_q4 = en(nc.sbuf_tensor("s_q4", [128, 1024], F32))
    s_qsg = en(nc.sbuf_tensor("s_qsg", [128, 1024], F32))
    s_qam = en(nc.sbuf_tensor("s_qam", [128, 64], F32))
    s_qs8 = en(nc.sbuf_tensor("s_qs8", [128, 64], F32))
    s_qdq = en(nc.sbuf_tensor("s_qdq", [128, 64], F32))
    s_qrd = en(nc.sbuf_tensor("s_qrd", [128, 64], F32))
    s_ln = en(nc.sbuf_tensor("s_ln", [128, 16], F32))
    s_xt = en(nc.sbuf_tensor("s_xt", [128, 1024], F32))
    s_xn = en(nc.sbuf_tensor("s_xn", [128, 1024], F32))
    s_mh = en(nc.sbuf_tensor("s_mh", [128, 1], F32))
    s_hsel = en(nc.sbuf_tensor("s_hsel", [128, 2], F32))
    s_id = en(nc.sbuf_tensor("s_id", [128, 128], BF16))

    ts = dict(nc=nc, dram=dram, sem=sem, OUT=OUT, X=X, XO=XO,
              s_msk=s_msk, s_nw=s_nw, s_cw=s_cw, s_gs=s_gs,
              s_q1=s_q1, s_q2=s_q2, s_q3=s_q3, s_q4=s_q4, s_qsg=s_qsg,
              s_qam=s_qam, s_qs8=s_qs8, s_qdq=s_qdq, s_qrd=s_qrd,
              s_ln=s_ln, s_xt=s_xt, s_xn=s_xn, s_mh=s_mh, s_hsel=s_hsel,
              s_id=s_id)

    st_a = ExitStack()
    ena = st_a.enter_context
    s_xnT = ena(nc.sbuf_tensor("s_xnT", [128, 8 * (L + 16)], BF16))
    s_xnb = ena(nc.sbuf_tensor("s_xnb", [128, 1024], BF16))
    s_hT = ena(nc.sbuf_tensor("s_hT", [128, 8 * 128], BF16))
    s_Wq = ena(nc.sbuf_tensor("s_Wq", [128, 8 * 512], BF16))
    s_Wk = ena(nc.sbuf_tensor("s_Wk", [128, 8 * 512], BF16))
    s_Wv = ena(nc.sbuf_tensor("s_Wv", [128, 8 * 1024], BF16))
    s_Wb = ena(nc.sbuf_tensor("s_Wb", [128, 8 * 4], BF16))
    s_Wg = ena(nc.sbuf_tensor("s_Wg", [128, 8 * 1024], BF16))
    s_Wo = ena(nc.sbuf_tensor("s_Wo", [128, 8 * 1024], BF16))
    s_qk = ena(nc.sbuf_tensor("s_qk", [128, 1024], BF16))
    s_v = ena(nc.sbuf_tensor("s_v", [128, 1024], BF16))
    s_g = ena(nc.sbuf_tensor("s_g", [128, 1024], BF16))
    s_qkT = ena(nc.sbuf_tensor("s_qkT", [128, 1024], BF16))
    s_beta = ena(nc.sbuf_tensor("s_beta", [128, 4], F32))
    s_st = ena(nc.sbuf_tensor("s_st", [128, 24], F32))
    s_Sbf = ena(nc.sbuf_tensor("s_Sbf", [128, 1024], BF16))
    s_Sf = ena(nc.sbuf_tensor("s_Sf", [128, 1024], F32))
    s_uf = ena(nc.sbuf_tensor("s_uf", [128, 1024], F32))
    s_ub = ena(nc.sbuf_tensor("s_ub", [128, 1024], BF16))
    s_C = ena(nc.sbuf_tensor("s_C", [128, 256], BF16))
    s_dg = ena(nc.sbuf_tensor("s_dg", [128, 256], BF16))
    s_At = ena(nc.sbuf_tensor("s_At", [128, 256], BF16))
    s_AtT = ena(nc.sbuf_tensor("s_AtT", [128, 256], BF16))
    s_pw = ena(nc.sbuf_tensor("s_pw", [128, 9 * 256], BF16))
    s_Lt = ena(nc.sbuf_tensor("s_Lt", [128, 256], BF16))
    s_o = ena(nc.sbuf_tensor("s_o", [128, 1024], F32))
    s_gtd = ena(nc.sbuf_tensor("s_gtd", [128, 1024], BF16))
    s_gTt = ena(nc.sbuf_tensor("s_gTt", [128, 8 * 128], BF16))
    s_wsrc = ena(nc.sbuf_tensor("s_wsrc", [128, 512], F32))
    s_wq1 = ena(nc.sbuf_tensor("s_wq1", [128, 512], F32))
    s_wq3 = ena(nc.sbuf_tensor("s_wq3", [128, 512], F32))
    s_wlad = ena(nc.sbuf_tensor("s_wlad", [128, 5 * 32], F32))
    ts.update(s_xnT=s_xnT, s_xnb=s_xnb, s_hT=s_hT, s_Wq=s_Wq, s_Wk=s_Wk,
              s_Wv=s_Wv, s_Wb=s_Wb, s_Wg=s_Wg, s_Wo=s_Wo, s_qk=s_qk,
              s_v=s_v, s_g=s_g, s_qkT=s_qkT, s_beta=s_beta,
              s_st=s_st, s_Sbf=s_Sbf, s_uf=s_uf, s_ub=s_ub, s_C=s_C,
              s_dg=s_dg, s_At=s_At, s_AtT=s_AtT, s_pw=s_pw, s_Lt=s_Lt,
              s_o=s_o, s_Sf=s_Sf, s_gtd=s_gtd, s_gTt=s_gTt,
              s_wsrc=s_wsrc, s_wq1=s_wq1, s_wq3=s_wq3, s_wlad=s_wlad)
    st_a.close()
    s_xqT = en(nc.sbuf_tensor("s_xqT", [128, 8 * 2048], BF16))
    s_wgc = en(nc.sbuf_tensor("s_wgc", [128, 8 * 1024], BF16))
    s_wuc = en(nc.sbuf_tensor("s_wuc", [128, 8 * 1024], BF16))
    s_sg = en(nc.sbuf_tensor("s_sg", [128, 1024], F32))
    s_hid = en(nc.sbuf_tensor("s_hid", [128, 1024], F32))
    s_hqb = en(nc.sbuf_tensor("s_hqb", [128, 1024], BF16))
    s_hqT = en(nc.sbuf_tensor("s_hqT", [128, 32 * 512], BF16))
    s_gf = en(nc.sbuf_tensor("s_gf", [128, 1024], F32))
    s_wT = en(nc.sbuf_tensor("s_wT", [128, 32 * 1024], BF16))
    ts.update(s_xqT=s_xqT, s_wgc=s_wgc, s_wuc=s_wuc, s_sg=s_sg,
              s_hid=s_hid, s_hqb=s_hqb, s_hqT=s_hqT, s_gf=s_gf, s_wT=s_wT)
    p_S = en(nc.psum_tensor("p_S", [128, 1024], F32))
    p_A = en(nc.psum_tensor("p_A", [128, 1024], F32))
    p_B = en(nc.psum_tensor("p_B", [128, 512], F32))
    p_C = en(nc.psum_tensor("p_C", [128, 256], F32))
    p_D = en(nc.psum_tensor("p_D", [128, 1024], F32))
    ts.update(p_S=p_S, p_A=p_A, p_B=p_B, p_C=p_C, p_D=p_D)

    with nc.Block() as block:
        @block.sync
        def _(eng):
            emit_all("sync", eng, ts)

        @block.scalar
        def _(eng):
            emit_all("scalar", eng, ts)

        @block.vector
        def _(eng):
            emit_all("vector", eng, ts)

        @block.tensor
        def _(eng):
            emit_all("tensor", eng, ts)

        @block.gpsimd
        def _(eng):
            emit_all("gpsimd", eng, ts)

    st_outer.close()
    return nc



class Emit:
    def __init__(self, which, eng, ts):
        self.w = which
        self.e = eng
        self.ts = ts
        self.sem = ts["sem"]
        self.C = {n: 0 for n in SEM_NAMES}

    def op(self, eng_name, fn, inc=None, incby=1):
        i = fn(self.e) if self.w == eng_name else None
        if inc is not None:
            self.C[inc] += incby
            if i is not None:
                i.then_inc(self.sem[inc], incby)
        return i

    def dma(self, queue, out, in_, sem=None):
        eng_name = "sync" if queue == "qs" else "scalar"
        sname = sem if sem is not None else queue
        if self.w == eng_name:
            self.e.dma_start(out=out, in_=in_).then_inc(self.sem[sname], 16)
        self.C[sname] += 16

    def dmaT(self, out, in_):
        if self.w == "scalar":
            self.e.dma_start_transpose(out=out, in_=in_).then_inc(self.sem["qt"], 16)
        self.C["qt"] += 16

    def wait(self, eng_name, sems):
        for s in sems:
            if self.w == eng_name:
                self.e.wait_ge(self.sem[s], self.C[s])
            elif s == "qs" and self.w == "sync" and eng_name != "sync":
                self.e.wait_ge(self.sem[s], self.C[s])
            elif s in ("qt", "wd") and self.w == "scalar" and eng_name != "scalar":
                self.e.wait_ge(self.sem[s], self.C[s])

    def drain(self, eng_name):
        if self.w == eng_name:
            self.e.drain()

    def barrier(self):
        for s in SEM_NAMES:
            self.e.wait_ge(self.sem[s], self.C[s])


def emit_all(which, eng, ts):
    em = Emit(which, eng, ts)
    phase_p0(em, ts)
    em.barrier()
    phase_a(em, ts)
    em.barrier()
    phase_f(em, ts)
    em.wait("sync", ("qs",))
    em.wait("scalar", ("qt",))



def q3v(ap, nb):
    return ap.rearrange("p (b k) -> p b k", k=16)


def bcast16(ap, nb):
    return ap.rearrange("p (b o) -> p b o", o=1).broadcast_to((128, nb, 16))


def emit_quant(em, ts, src_ap, dst_ap, ncols, gcol):
    v, a = "vector", "scalar"
    gs = ts["s_gs"]
    nb = ncols // 16
    q1, q2, q3, q4, sg = (ts["s_q1"][:, 0:ncols], ts["s_q2"][:, 0:ncols],
                          ts["s_q3"][:, 0:ncols], ts["s_q4"][:, 0:ncols],
                          ts["s_qsg"][:, 0:ncols])
    am, s8, dq, rd = (ts["s_qam"][:, 0:nb], ts["s_qs8"][:, 0:nb],
                      ts["s_qdq"][:, 0:nb], ts["s_qrd"][:, 0:nb])
    zt = ts["s_q2"][:, 0:nb]
    pt = ts["s_q2"][:, nb:2 * nb]
    ct = ts["s_q2"][:, 2 * nb:3 * nb]
    vt = ts["s_q3"][:, 0:nb]
    ut = ts["s_q3"][:, nb:2 * nb]
    em.op(v, lambda e: e.tensor_reduce(out=am, in_=q3v(src_ap, nb), axis=AX.X,
                                       op=ALU.max, apply_absolute_value=True))
    em.drain(v)
    em.op(v, lambda e: e.tensor_scalar(out=zt, in0=am, scalar1=gs[:, gcol + 1:gcol + 2],
                                       scalar2=224.0, op0=ALU.mult, op1=ALU.min))
    em.drain(v)
    em.op(v, lambda e: e.memset(pt, 2.0 ** -7))
    em.drain(v)
    for k in (8, 4, 2, 1):
        em.op(v, lambda e, k=k: e.tensor_scalar(out=ct, in0=pt, scalar1=float(2 ** k),
                                                scalar2=None, op0=ALU.mult))
        em.drain(v)
        em.op(v, lambda e: e.tensor_tensor(out=ct, in0=ct, in1=zt, op=ALU.is_le))
        em.drain(v)
        em.op(v, lambda e, k=k: e.tensor_scalar(out=ct, in0=ct,
                                                scalar1=float(2 ** k - 1),
                                                scalar2=1.0, op0=ALU.mult, op1=ALU.add))
        em.drain(v)
        em.op(v, lambda e: e.tensor_tensor(out=pt, in0=pt, in1=ct, op=ALU.mult))
        em.drain(v)
    em.op(v, lambda e: e.reciprocal(out=ct, in_=pt))
    em.drain(v)
    em.op(v, lambda e: e.scalar_tensor_tensor(out=ut, in0=zt, scalar=8.0, in1=ct,
                                              op0=ALU.mult, op1=ALU.mult))
    em.drain(v)
    em.op(v, lambda e: e.memset(vt, 0.0))
    em.drain(v)
    for k in (16, 8, 4, 2, 1):
        em.op(v, lambda e, k=k: e.scalar_tensor_tensor(out=ct, in0=vt,
                                                       scalar=float(k) - 0.5, in1=ut,
                                                       op0=ALU.add, op1=ALU.is_le))
        em.drain(v)
        em.op(v, lambda e, k=k: e.scalar_tensor_tensor(out=vt, in0=ct,
                                                       scalar=float(k), in1=vt,
                                                       op0=ALU.mult, op1=ALU.add))
        em.drain(v)
    em.op(v, lambda e: e.scalar_tensor_tensor(out=s8, in0=vt, scalar=0.125, in1=pt,
                                              op0=ALU.mult, op1=ALU.mult))
    em.drain(v)
    em.op(v, lambda e: e.tensor_scalar(out=dq, in0=s8, scalar1=gs[:, gcol + 2:gcol + 3],
                                       scalar2=None, op0=ALU.mult))
    em.op(v, lambda e: e.tensor_scalar(out=rd, in0=s8, scalar1=1e-30,
                                       scalar2=None, op0=ALU.max))
    em.drain(v)
    em.op(v, lambda e: e.reciprocal(out=rd, in_=rd))
    em.drain(v)
    em.op(v, lambda e: e.tensor_scalar(out=rd, in0=rd, scalar1=gs[:, gcol:gcol + 1],
                                       scalar2=None, op0=ALU.mult), inc="vv")
    em.drain(v)
    em.wait("gpsimd", ("vv", "gg"))
    em.op("gpsimd", lambda e: e.tensor_tensor(out=q3v(q1, nb),
                                              in0=q3v(src_ap, nb),
                                              in1=bcast16(rd, nb), op=ALU.mult),
          inc="gg")
    em.wait(a, ("gg",))
    em.op(a, lambda e: e.activation(out=sg, in_=q1, func=AF.Sign))
    em.op(a, lambda e: e.activation(out=q2, in_=q1, func=AF.Abs), inc="gg")
    em.wait(v, ("gg",))
    em.op(v, lambda e: e.tensor_scalar(out=q3, in0=q2, scalar1=0.5,
                                       scalar2=None, op0=ALU.is_ge))
    em.drain(v)
    for th in (1.5, 2.5, 3.5):
        em.op(v, lambda e, th=th: e.scalar_tensor_tensor(
            out=q3, in0=q2, scalar=th, in1=q3, op0=ALU.is_ge, op1=ALU.add))
        em.drain(v)
    em.op(v, lambda e: e.tensor_scalar(out=q4, in0=q2, scalar1=5.0,
                                       scalar2=None, op0=ALU.is_ge))
    em.drain(v)
    em.op(v, lambda e: e.scalar_tensor_tensor(
        out=q4, in0=q2, scalar=7.0, in1=q4, op0=ALU.is_ge, op1=ALU.add))
    em.drain(v)
    em.op(v, lambda e: e.scalar_tensor_tensor(
        out=q3, in0=q4, scalar=2.0, in1=q3, op0=ALU.mult, op1=ALU.add))
    em.drain(v)
    em.op(v, lambda e: e.tensor_scalar(out=q4, in0=q2, scalar1=10.0,
                                       scalar2=None, op0=ALU.is_ge))
    em.drain(v)
    em.op(v, lambda e: e.scalar_tensor_tensor(
        out=q3, in0=q4, scalar=4.0, in1=q3, op0=ALU.mult, op1=ALU.add),
        inc="vv")
    em.drain(v)
    em.wait("gpsimd", ("vv",))
    em.op("gpsimd", lambda e: e.tensor_tensor(out=q3v(q3, nb), in0=q3v(q3, nb),
                                              in1=bcast16(dq, nb), op=ALU.mult))
    em.drain("gpsimd")
    em.op("gpsimd", lambda e: e.tensor_tensor(out=dst_ap, in0=q3, in1=sg,
                                              op=ALU.mult), inc="vv")
    em.drain("gpsimd")


def emit_ln(em, ts, x_ap, xn_ap, out_ap=None):
    v, a = "vector", "scalar"
    ln = ts["s_ln"]
    em.op(v, lambda e: e.bn_stats(out=ln[:, 0:6], in_=x_ap[:, 0:512]))
    em.op(v, lambda e: e.bn_stats(out=ln[:, 6:12], in_=x_ap[:, 512:1024]))
    em.drain(v)
    em.op(v, lambda e: e.bn_aggr(out=ln[:, 12:14], in_=ln[:, 0:12]))
    em.drain(v)
    em.op(v, lambda e: e.tensor_scalar(out=ln[:, 14:15], in0=ln[:, 13:14],
                                       scalar1=1e-5, scalar2=None, op0=ALU.add),
          inc="vv")
    em.drain(v)
    em.wait("gpsimd", ("vv",))
    em.op("gpsimd", lambda e: e.tensor_tensor(
        out=ln[:, 15:16], in0=ln[:, 14:15], in1=ts["s_mh"][:], op=ALU.pow),
        inc="gg")
    em.wait(v, ("gg",))
    em.op(v, lambda e: e.tensor_scalar(out=xn_ap, in0=x_ap, scalar1=ln[:, 12:13],
                                       scalar2=None, op0=ALU.subtract))
    em.drain(v)
    em.op(v, lambda e: e.tensor_scalar(out=(out_ap if out_ap is not None
                                            else xn_ap),
                                       in0=xn_ap, scalar1=ln[:, 15:16],
                                       scalar2=None, op0=ALU.mult), inc="vv")
    em.drain(v)



def phase_p0(em, ts):
    v, a, t, s, g = "vector", "scalar", "tensor", "sync", "gpsimd"
    dram = ts["dram"]
    nc_gs = ts["s_gs"]
    em.op(v, lambda e: e.memset(ts["s_mh"][:], -0.5))
    em.dma("qs", ts["s_msk"][:], dram["masks"][:, :])
    em.dma("qs", ts["s_nw"][:], dram["norm_w_rep"][:, :])
    em.dma("qs", nc_gs[:, 0:1], dram["s_in_rep"][:, :])
    em.dma("qs", nc_gs[:, 3:4], dram["s_in_down_rep"][:, :])
    em.dma("qs", ts["s_hsel"][:], dram["half_sel_rep"][:, :])
    cw3 = ts["s_cw"][:].rearrange("p (c i) -> p c i", i=4)
    em.dma("qs", cw3, dram["conv_w"][:, :].rearrange("(c p) i -> p c i", p=128))
    em.wait(v, ("qs",))
    em.op(v, lambda e: e.tensor_copy(out=ts["s_id"][:],
                                     in_=ts["s_msk"][:, 384:512]))
    for c in (0, 3):
        em.op(v, lambda e, c=c: e.tensor_scalar(out=nc_gs[:, c + 1:c + 2],
                                                in0=nc_gs[:, c:c + 1],
                                                scalar1=1.0 / 12.0, scalar2=None,
                                                op0=ALU.mult))
        em.op(v, lambda e, c=c: e.tensor_copy(out=nc_gs[:, c + 2:c + 3],
                                              in_=nc_gs[:, c:c + 1]))
        em.drain(v)
        em.op(v, lambda e, c=c: e.reciprocal(out=nc_gs[:, c + 2:c + 3],
                                             in_=nc_gs[:, c + 2:c + 3]))
        em.drain(v)
    em.op(v, lambda e: e.memset(nc_gs[:, 15:16], 0.0), inc="vv")
    em.drain(v)



def emit_wamax_pair(em, ts, srcA, srcB, sheetA, sheetB, par):
    v, s = "vector", "sync"
    if par == 0:
        slotA, slotB = ts["s_wsrc"][:], ts["s_wq1"][:]
        grp, glob, sem = ts["s_qam"], ts["s_qdq"][:, 0:1], "vv"
    else:
        slotA, slotB = ts["s_wq3"][:], ts["s_qsg"][:, 0:512]
        grp, glob, sem = ts["s_qs8"], ts["s_qrd"][:, 0:1], "gg"
    em.wait(s, (sem,))
    em.dma("qs", slotA, srcA)
    em.dma("qs", slotB, srcB)
    em.wait(v, ("qs",))
    em.op(v, lambda e: e.tensor_reduce(
        out=grp[:, 0:32], in_=q3v(slotA, 32), axis=AX.X,
        op=ALU.max, apply_absolute_value=True))
    em.op(v, lambda e: e.tensor_reduce(
        out=grp[:, 32:64], in_=q3v(slotB, 32), axis=AX.X,
        op=ALU.max, apply_absolute_value=True))
    em.drain(v)
    em.op(v, lambda e: e.tensor_reduce(
        out=glob, in_=grp[:, 0:64], axis=AX.X,
        op=ALU.max, apply_absolute_value=True))
    em.drain(v)
    em.op(v, lambda e: e.tensor_tensor(
        out=ts["s_gs"][:, 15:16], in0=ts["s_gs"][:, 15:16],
        in1=glob, op=ALU.max), inc=sem)
    em.drain(v)
    em.wait(s, (sem,))
    em.dma("qs", sheetA, grp[:, 0:32])
    em.dma("qs", sheetB, grp[:, 32:64])


def emit_wladder(em, ts, mi, gcol, p):
    v, s = "vector", "sync"
    gs = ts["s_gs"]
    dram = ts["dram"]
    src = ts["s_wsrc"][:]
    zt = ts["s_wq1"][:]
    pt = ts["s_wq3"][:]
    ct = ts["s_qsg"][:, 0:512]
    vt = ts["s_qsg"][:, 512:1024]
    if True:
        em.wait(s, ("vv", "gg"))
        em.dma("qs", src, dram["wsc_scratch"][mi, :, p * 512:(p + 1) * 512])
        em.wait(v, ("qs",))
        em.op(v, lambda e: e.tensor_scalar(out=zt, in0=src,
                                           scalar1=gs[:, gcol + 1:gcol + 2],
                                           scalar2=224.0, op0=ALU.mult,
                                           op1=ALU.min))
        em.op(v, lambda e: e.memset(pt, 2.0 ** -7))
        em.drain(v)
        for k in (8, 4, 2, 1):
            em.op(v, lambda e, k=k: e.tensor_scalar(out=ct, in0=pt,
                                                    scalar1=float(2 ** k),
                                                    scalar2=None, op0=ALU.mult))
            em.drain(v)
            em.op(v, lambda e: e.tensor_tensor(out=ct, in0=ct, in1=zt,
                                               op=ALU.is_le))
            em.drain(v)
            em.op(v, lambda e, k=k: e.tensor_scalar(out=ct, in0=ct,
                                                    scalar1=float(2 ** k - 1),
                                                    scalar2=1.0, op0=ALU.mult,
                                                    op1=ALU.add))
            em.drain(v)
            em.op(v, lambda e: e.tensor_tensor(out=pt, in0=pt, in1=ct,
                                               op=ALU.mult))
            em.drain(v)
        em.op(v, lambda e: e.reciprocal(out=ct, in_=pt))
        em.drain(v)
        em.op(v, lambda e: e.scalar_tensor_tensor(out=src, in0=zt, scalar=8.0,
                                                  in1=ct, op0=ALU.mult,
                                                  op1=ALU.mult))
        em.op(v, lambda e: e.memset(vt, 0.0))
        em.drain(v)
        for k in (16, 8, 4, 2, 1):
            em.op(v, lambda e, k=k: e.scalar_tensor_tensor(out=ct, in0=vt,
                                                           scalar=float(k) - 0.5,
                                                           in1=src, op0=ALU.add,
                                                           op1=ALU.is_le))
            em.drain(v)
            em.op(v, lambda e, k=k: e.scalar_tensor_tensor(out=vt, in0=ct,
                                                           scalar=float(k),
                                                           in1=vt, op0=ALU.mult,
                                                           op1=ALU.add))
            em.drain(v)
        em.op(v, lambda e: e.scalar_tensor_tensor(out=zt, in0=vt, scalar=0.125,
                                                  in1=pt, op0=ALU.mult,
                                                  op1=ALU.mult))
        em.drain(v)
        em.op(v, lambda e: e.tensor_scalar(out=pt, in0=zt,
                                           scalar1=gs[:, gcol + 2:gcol + 3],
                                           scalar2=None, op0=ALU.mult))
        em.op(v, lambda e: e.tensor_scalar(out=ct, in0=zt, scalar1=1e-30,
                                           scalar2=None, op0=ALU.max))
        em.drain(v)
        em.op(v, lambda e: e.reciprocal(out=ct, in_=ct))
        em.drain(v)
        em.op(v, lambda e: e.tensor_scalar(out=ct, in0=ct,
                                           scalar1=gs[:, gcol:gcol + 1],
                                           scalar2=None, op0=ALU.mult),
              inc="vv")
        em.drain(v)
        em.wait(s, ("vv",))
        em.dma("qs", dram["wdq_scratch"][mi, :, p * 512:(p + 1) * 512], pt)
        em.dma("qs", dram["wrd_scratch"][mi, :, p * 512:(p + 1) * 512], ct)


def emit_wbounce(em, ts, gcol):
    v, s = "vector", "sync"
    nc_gs = ts["s_gs"]
    em.wait(s, ("vv", "gg"))
    em.dma("qs", ts["dram"]["wmax_scratch"][:].rearrange("(p o) -> p o", o=1),
           nc_gs[:, 15:16])
    em.wait(s, ("qs",))
    em.dma("qs", ts["s_wlad"][0:1, 0:128],
           ts["dram"]["wmax_scratch"][:].rearrange("(o f) -> o f", o=1))
    em.wait(v, ("qs",))
    em.op(v, lambda e: e.tensor_reduce(out=ts["s_wlad"][0:1, 128:129],
                                       in_=ts["s_wlad"][0:1, 0:128], axis=AX.X,
                                       op=ALU.max, apply_absolute_value=True))
    em.drain(v)
    em.op(v, lambda e: e.tensor_copy(
        out=ts["s_wlad"][0:1, 0:128],
        in_=ts["s_wlad"][0:1, 128:129].broadcast_to((1, 128))), inc="vv")
    em.drain(v)
    em.wait(s, ("vv",))
    em.dma("qs", ts["dram"]["wmax_scratch"][:].rearrange("(o f) -> o f", o=1),
           ts["s_wlad"][0:1, 0:128])
    em.wait(s, ("qs",))
    em.dma("qs", nc_gs[:, gcol:gcol + 1],
           ts["dram"]["wmax_scratch"][:].rearrange("(p o) -> p o", o=1))
    em.wait(v, ("qs",))
    em.op(v, lambda e: e.reciprocal(out=nc_gs[:, gcol:gcol + 1],
                                    in_=nc_gs[:, gcol:gcol + 1]))
    em.drain(v)
    em.op(v, lambda e: e.tensor_scalar(out=nc_gs[:, gcol:gcol + 1],
                                       in0=nc_gs[:, gcol:gcol + 1],
                                       scalar1=2688.0, scalar2=None, op0=ALU.mult))
    em.drain(v)
    em.op(v, lambda e: e.tensor_scalar(out=nc_gs[:, gcol + 1:gcol + 2],
                                       in0=nc_gs[:, gcol:gcol + 1],
                                       scalar1=1.0 / 12.0, scalar2=None,
                                       op0=ALU.mult))
    em.op(v, lambda e: e.tensor_copy(out=nc_gs[:, gcol + 2:gcol + 3],
                                     in_=nc_gs[:, gcol:gcol + 1]))
    em.drain(v)
    em.op(v, lambda e: e.reciprocal(out=nc_gs[:, gcol + 2:gcol + 3],
                                    in_=nc_gs[:, gcol + 2:gcol + 3]))
    em.drain(v)
    em.op(v, lambda e: e.memset(nc_gs[:, 15:16], 0.0), inc="vv")
    em.drain(v)


def emit_wload(em, ts, mi, sidx, src_hbm):
    s = "sync"
    dram = ts["dram"]
    em.wait(s, ("wa",))
    em.dma("qs", ts["s_wsrc"][:], src_hbm)
    em.dma("qs", ts["s_qdq"][:, 0:32],
           dram["wdq_scratch"][mi, :, sidx * 32:(sidx + 1) * 32])
    em.dma("qs", ts["s_qrd"][:, 0:32],
           dram["wrd_scratch"][mi, :, sidx * 32:(sidx + 1) * 32])


def emit_wchain(em, ts, par):
    v, a, s, g = "vector", "scalar", "sync", "gpsimd"
    nb = 32
    q1, q3 = ts["s_wq1"][:], ts["s_wq3"][:]
    q2 = ts["s_qsg"][:, 0:512]
    sg = ts["s_qsg"][:, 512:1024]
    wnt = ts["s_xnb"][:, par * 512:(par + 1) * 512]
    dq, rd = ts["s_qdq"][:, 0:nb], ts["s_qrd"][:, 0:nb]
    src = ts["s_wsrc"][:]
    em.wait(g, ("qs", "wb"))
    em.op(g, lambda e: e.tensor_tensor(out=q3v(q1, nb), in0=q3v(src, nb),
                                       in1=bcast16(rd, nb), op=ALU.mult),
          inc="wa")
    em.wait(a, ("wa", "wc"))
    em.op(a, lambda e: e.activation(out=sg, in_=q1, func=AF.Sign))
    em.op(a, lambda e: e.activation(out=q2, in_=q1, func=AF.Abs), inc="wb")
    em.wait(v, ("wb", "wa"))
    em.op(v, lambda e: e.tensor_scalar(out=q3, in0=q2, scalar1=0.5,
                                       scalar2=None, op0=ALU.is_ge))
    em.drain(v)
    for th in (1.5, 2.5, 3.5):
        em.op(v, lambda e, th=th: e.scalar_tensor_tensor(
            out=q3, in0=q2, scalar=th, in1=q3, op0=ALU.is_ge, op1=ALU.add))
        em.drain(v)
    em.op(v, lambda e: e.tensor_scalar(out=q1, in0=q2, scalar1=5.0,
                                       scalar2=None, op0=ALU.is_ge))
    em.drain(v)
    em.op(v, lambda e: e.scalar_tensor_tensor(
        out=q1, in0=q2, scalar=7.0, in1=q1, op0=ALU.is_ge, op1=ALU.add))
    em.drain(v)
    em.op(v, lambda e: e.scalar_tensor_tensor(
        out=q3, in0=q1, scalar=2.0, in1=q3, op0=ALU.mult, op1=ALU.add))
    em.drain(v)
    em.op(v, lambda e: e.tensor_scalar(out=q1, in0=q2, scalar1=10.0,
                                       scalar2=None, op0=ALU.is_ge))
    em.drain(v)
    em.op(v, lambda e: e.scalar_tensor_tensor(
        out=q3, in0=q1, scalar=4.0, in1=q3, op0=ALU.mult, op1=ALU.add),
        inc="wc")
    em.drain(v)
    em.wait(g, ("wc", "wd"))
    em.op(g, lambda e: e.tensor_tensor(out=q3v(q3, nb), in0=q3v(q3, nb),
                                       in1=bcast16(dq, nb), op=ALU.mult))
    em.drain(g)
    em.op(g, lambda e: e.tensor_tensor(out=wnt, in0=q3, in1=sg,
                                       op=ALU.mult), inc="wa")
    em.drain(g)


def emit_wtrans(em, ts, par, dests):
    a, t, s = "scalar", "tensor", "sync"
    base = par * 512
    em.wait(t, ("wa", "wc"))
    for j in range(4):
        em.op(t, lambda e, j=j: e.matmul(
            out=ts["p_B"][:, j * 128:(j + 1) * 128],
            lhsT=ts["s_xnb"][:, base + j * 128:base + (j + 1) * 128],
            rhs=ts["s_id"][:], start=(j == 0), stop=(j == 3)),
            inc="wb" if j == 3 else None)
    em.wait(a, ("wb", "wd"))
    em.op(a, lambda e: e.copy(out=ts["s_xnb"][:, base:base + 512],
                              in_=ts["p_B"][:, 0:512]), inc="wc")
    em.wait(a, ("wc",))
    for pj in range(4):
        em.dma("qt", dests[pj],
               ts["s_xnb"][:, base + pj * 128:base + (pj + 1) * 128],
               sem="wd")
    em.wait(t, ("wc",))



def phase_f(em, ts):
    v, a, t, s = "vector", "scalar", "tensor", "sync"
    dram = ts["dram"]
    XO = ts["XO"]
    for tt in range(NTO):
        em.wait(s, ("vv",))
        em.dma("qs", ts["s_xt"][:], XO[tt * 128:(tt + 1) * 128, :])
        em.wait(v, ("qs",))
        emit_ln(em, ts, ts["s_xt"][:], ts["s_xn"][:])
        em.wait(v, ("qt",))
        emit_quant(em, ts, ts["s_xn"][:], ts["s_hqb"][:, 0:1024], 1024, 0)
        em.wait("scalar", ("vv",))
        xt3 = ts["s_xqT"][:].rearrange("p (hch t) -> p hch t", hch=8)[
            :, :, tt * 128: tt * 128 + 128]
        em.dmaT(xt3, ts["s_hqb"][:, 0:1024])
    em.wait(t, ("qt",))
    em.wait(v, ("qt",))
    gs = ts["s_gs"]
    nbF = 64
    scr = ts["s_gf"]
    lzt, lpt, lct, lvt, lut = (scr[:, 256:320], scr[:, 320:384],
                               scr[:, 384:448], scr[:, 448:512],
                               scr[:, 512:576])
    def fsmalls(par):
        if par == 0:
            return (ts["s_qam"][:, 0:64], ts["s_qs8"][:, 0:64],
                    ts["s_qdq"][:, 0:64], ts["s_qrd"][:, 0:64])
        return (scr[:, 0:64], scr[:, 64:128], scr[:, 128:192], scr[:, 192:256])
    def fhbuf(par):
        return ts["s_hid"][:] if par == 0 else ts["s_q4"][:]

    def emit_fladder(k):
        par = k % 2
        am, s8, dq, rd = fsmalls(par)
        hb = fhbuf(par)
        em.wait(v, ("vv", "wb"))
        em.op(v, lambda e: e.tensor_reduce(out=am, in_=q3v(hb, nbF), axis=AX.X,
                                           op=ALU.max, apply_absolute_value=True))
        em.drain(v)
        em.op(v, lambda e: e.tensor_scalar(out=lzt, in0=am,
                                           scalar1=gs[:, 4:5], scalar2=224.0,
                                           op0=ALU.mult, op1=ALU.min))
        em.op(v, lambda e: e.memset(lpt, 2.0 ** -7))
        em.drain(v)
        for kk in (8, 4, 2, 1):
            em.op(v, lambda e, kk=kk: e.tensor_scalar(out=lct, in0=lpt,
                                                      scalar1=float(2 ** kk),
                                                      scalar2=None, op0=ALU.mult))
            em.drain(v)
            em.op(v, lambda e: e.tensor_tensor(out=lct, in0=lct, in1=lzt,
                                               op=ALU.is_le))
            em.drain(v)
            em.op(v, lambda e, kk=kk: e.tensor_scalar(out=lct, in0=lct,
                                                      scalar1=float(2 ** kk - 1),
                                                      scalar2=1.0, op0=ALU.mult,
                                                      op1=ALU.add))
            em.drain(v)
            em.op(v, lambda e: e.tensor_tensor(out=lpt, in0=lpt, in1=lct,
                                               op=ALU.mult))
            em.drain(v)
        em.op(v, lambda e: e.reciprocal(out=lct, in_=lpt))
        em.drain(v)
        em.op(v, lambda e: e.scalar_tensor_tensor(out=lut, in0=lzt, scalar=8.0,
                                                  in1=lct, op0=ALU.mult,
                                                  op1=ALU.mult))
        em.op(v, lambda e: e.memset(lvt, 0.0))
        em.drain(v)
        for kk in (16, 8, 4, 2, 1):
            em.op(v, lambda e, kk=kk: e.scalar_tensor_tensor(
                out=lct, in0=lvt, scalar=float(kk) - 0.5, in1=lut,
                op0=ALU.add, op1=ALU.is_le))
            em.drain(v)
            em.op(v, lambda e, kk=kk: e.scalar_tensor_tensor(
                out=lvt, in0=lct, scalar=float(kk), in1=lvt,
                op0=ALU.mult, op1=ALU.add))
            em.drain(v)
        em.op(v, lambda e: e.scalar_tensor_tensor(out=s8, in0=lvt, scalar=0.125,
                                                  in1=lpt, op0=ALU.mult,
                                                  op1=ALU.mult))
        em.drain(v)
        em.op(v, lambda e: e.tensor_scalar(out=dq, in0=s8, scalar1=gs[:, 5:6],
                                           scalar2=None, op0=ALU.mult))
        em.op(v, lambda e: e.tensor_scalar(out=rd, in0=s8, scalar1=1e-30,
                                           scalar2=None, op0=ALU.max))
        em.drain(v)
        em.op(v, lambda e: e.reciprocal(out=rd, in_=rd))
        em.drain(v)
        em.op(v, lambda e: e.tensor_scalar(out=rd, in0=rd, scalar1=gs[:, 3:4],
                                           scalar2=None, op0=ALU.mult),
              inc="wa")
        em.drain(v)

    def emit_fval_head(k):
        par = k % 2
        am, s8, dq, rd = fsmalls(par)
        hb = fhbuf(par)
        q1 = ts["s_q1"][:]
        q2, sgn = ts["s_q2"][:], ts["s_qsg"][:]
        em.wait("gpsimd", ("wa", "wd", "vv"))
        em.op("gpsimd", lambda e: e.tensor_tensor(
            out=q3v(q1, nbF), in0=q3v(hb, nbF), in1=bcast16(rd, nbF),
            op=ALU.mult), inc="wb")
        em.wait(a, ("wb", "vv"))
        em.op(a, lambda e: e.activation(out=sgn, in_=q1, func=AF.Sign))
        em.op(a, lambda e: e.activation(out=q2, in_=q1, func=AF.Abs), inc="wc")

    def emit_fval_tail(k, oc):
        par = k % 2
        am, s8, dq, rd = fsmalls(par)
        q1, q2, q3 = ts["s_q1"][:], ts["s_q2"][:], ts["s_q3"][:]
        sgn = ts["s_qsg"][:]
        em.wait(v, ("wc",))
        em.op(v, lambda e: e.tensor_scalar(out=q3, in0=q2, scalar1=0.5,
                                           scalar2=None, op0=ALU.is_ge))
        em.drain(v)
        for th in (1.5, 2.5, 3.5):
            em.op(v, lambda e, th=th: e.scalar_tensor_tensor(
                out=q3, in0=q2, scalar=th, in1=q3, op0=ALU.is_ge, op1=ALU.add))
            em.drain(v)
        em.op(v, lambda e: e.tensor_scalar(out=q1, in0=q2, scalar1=5.0,
                                           scalar2=None, op0=ALU.is_ge))
        em.drain(v)
        em.op(v, lambda e: e.scalar_tensor_tensor(
            out=q1, in0=q2, scalar=7.0, in1=q1, op0=ALU.is_ge, op1=ALU.add))
        em.drain(v)
        em.op(v, lambda e: e.scalar_tensor_tensor(
            out=q3, in0=q1, scalar=2.0, in1=q3, op0=ALU.mult, op1=ALU.add))
        em.drain(v)
        em.op(v, lambda e: e.tensor_scalar(out=q1, in0=q2, scalar1=10.0,
                                           scalar2=None, op0=ALU.is_ge))
        em.drain(v)
        em.op(v, lambda e: e.scalar_tensor_tensor(
            out=q3, in0=q1, scalar=4.0, in1=q3, op0=ALU.mult, op1=ALU.add),
            inc="wd")
        em.drain(v)
        em.wait("gpsimd", ("wd", "qs"))
        em.op("gpsimd", lambda e: e.tensor_tensor(out=q3v(q3, nbF),
                                                  in0=q3v(q3, nbF),
                                                  in1=bcast16(dq, nbF),
                                                  op=ALU.mult))
        em.drain("gpsimd")
        em.op("gpsimd", lambda e: e.tensor_tensor(out=ts["s_hqb"][:], in0=q3,
                                                  in1=sgn, op=ALU.mult),
              inc="vv")
        em.drain("gpsimd")
        em.wait(s, ("vv",))
        em.dma("qs", dram["hq_scratch"][k * 128:(k + 1) * 128,
                                        oc * 1024:(oc + 1) * 1024],
               ts["s_hqb"][:])

    for oc in range(4):
        em.wait(s, ("tt",))
        for ch in range(8):
            em.dma("qs", ts["s_wgc"][:, ch * 1024:(ch + 1) * 1024],
                   dram["wgT_hbm"][ch, :, oc * 1024:(oc + 1) * 1024])
            em.dma("qs", ts["s_wuc"][:, ch * 1024:(ch + 1) * 1024],
                   dram["wuT_hbm"][ch, :, oc * 1024:(oc + 1) * 1024])
        em.wait(t, ("qs",))
        for tt in range(NTO):
            em.wait(t, ("qt", "aa"))
            for ch in range(8):
                for hf in range(2):
                    em.op(t, lambda e, ch=ch, tt=tt, hf=hf: e.matmul(
                        out=ts["p_A"][:, hf * 512:(hf + 1) * 512],
                        lhsT=ts["s_xqT"][:, ch * 2048 + tt * 128: ch * 2048 + (tt + 1) * 128],
                        rhs=ts["s_wgc"][:, ch * 1024 + hf * 512: ch * 1024 + (hf + 1) * 512],
                        start=(ch == 0), stop=(ch == 7)))
            for ch in range(8):
                for hf in range(2):
                    em.op(t, lambda e, ch=ch, tt=tt, hf=hf: e.matmul(
                        out=ts["p_D"][:, hf * 512:(hf + 1) * 512],
                        lhsT=ts["s_xqT"][:, ch * 2048 + tt * 128: ch * 2048 + (tt + 1) * 128],
                        rhs=ts["s_wuc"][:, ch * 1024 + hf * 512: ch * 1024 + (hf + 1) * 512],
                        start=(ch == 0), stop=(ch == 7)),
                        inc="tt" if (ch == 7 and hf == 1) else None)
            em.wait(a, ("tt",))
            em.op(a, lambda e: e.activation(out=ts["s_sg"][:], in_=ts["p_A"][:],
                                            func=AF.Silu), inc="aa")
            em.wait(v, ("aa", "wb"))
            em.op(v, lambda e, tt=tt: e.tensor_tensor(
                out=fhbuf(tt % 2), in0=ts["s_sg"][:], in1=ts["p_D"][:],
                op=ALU.mult), inc="qt")
            em.drain(v)
            if tt > 0:
                emit_fval_head(tt - 1)
            emit_fladder(tt)
            if tt > 0:
                emit_fval_tail(tt - 1, oc)
        emit_fval_head(NTO - 1)
        emit_fval_tail(NTO - 1, oc)
    em.wait(s, ("qt",))
    for ic in range(32):
        em.dma("qs", ts["s_wT"][:, ic * 1024:(ic + 1) * 1024],
               dram["wdT_hbm"][ic, :, :])
    em.wait(t, ("qs",))
    for tb in range(4):
        em.wait("scalar", ("qs",))
        outT = ts["s_hqT"][:].rearrange("p (ic r) -> p ic r", r=512)
        em.dmaT(outT, dram["hq_scratch"][tb * 512:(tb + 1) * 512, :])
        em.wait(t, ("qt",))
        for tsub in range(4):
            tt = tb * 4 + tsub
            em.wait(t, ("aa",))
            for ic in range(32):
                for half in range(2):
                    em.op(t, lambda e, ic=ic, half=half, tsub=tsub: e.matmul(
                        out=ts["p_A"][:, half * 512:(half + 1) * 512],
                        lhsT=ts["s_hqT"][:, ic * 512 + tsub * 128: ic * 512 + (tsub + 1) * 128],
                        rhs=ts["s_wT"][:, ic * 1024 + half * 512: ic * 1024 + (half + 1) * 512],
                        start=(ic == 0), stop=(ic == 31)),
                        inc="tt" if (ic == 31 and half == 1) else None)
            em.wait(a, ("tt", "qs", "vv"))
            em.op(a, lambda e: e.activation(out=ts["s_gf"][:], in_=ts["p_A"][:],
                                            func=AF.Silu), inc="aa")
            em.wait(s, ("aa", "vv"))
            em.dma("qs", ts["s_q1"][:], dram["o_woA"][tt * 128:(tt + 1) * 128, :])
            em.dma("qs", ts["s_q2"][:], dram["o_woB"][tt * 128:(tt + 1) * 128, :])
            em.dma("qs", ts["s_xt"][:], XO[tt * 128:(tt + 1) * 128, :])
            em.wait(v, ("aa", "qs"))
            em.op(v, lambda e: e.tensor_scalar(
                out=ts["s_q1"][:], in0=ts["s_q1"][:],
                scalar1=ts["s_hsel"][:, 0:1], scalar2=None, op0=ALU.mult))
            em.op(v, lambda e: e.tensor_scalar(
                out=ts["s_q2"][:], in0=ts["s_q2"][:],
                scalar1=ts["s_hsel"][:, 1:2], scalar2=None, op0=ALU.mult))
            em.drain(v)
            em.op(v, lambda e: e.tensor_tensor(out=ts["s_q1"][:], in0=ts["s_q1"][:],
                                               in1=ts["s_q2"][:], op=ALU.add))
            em.drain(v)
            em.op(v, lambda e: e.tensor_tensor(out=ts["s_q3"][:], in0=ts["s_q1"][:],
                                               in1=ts["s_gf"][:], op=ALU.mult))
            em.drain(v)
            em.op(v, lambda e: e.tensor_tensor(out=ts["s_q3"][:], in0=ts["s_q3"][:],
                                               in1=ts["s_xt"][:], op=ALU.add),
                  inc="vv")
            em.drain(v)
            em.wait(s, ("vv",))
            em.dma("qs", ts["OUT"][tt * 128:(tt + 1) * 128, :], ts["s_q3"][:])



def phase_a(em, ts):
    v, a, t, s = "vector", "scalar", "tensor", "sync"
    dram = ts["dram"]
    X = ts["X"]
    msk = ts["s_msk"]
    trilS = msk[:, 0:64]
    triuI = msk[:, 64:128]
    id4 = msk[:, 128:384]
    LP = L + 16

    GWh, UWh, DWh = dram["gate_w"], dram["up_w"], dram["down_w"]
    WGTh, WUTh, WDTh = dram["wgT_hbm"], dram["wuT_hbm"], dram["wdT_hbm"]
    WSCh = dram["wsc_scratch"]
    amax_tasks = []
    for mi, (W, gcol) in enumerate(((GWh, 6), (UWh, 9))):
        for pp in range(32):
            amax_tasks.append(lambda W=W, mi=mi, pp=pp: emit_wamax_pair(
                em, ts,
                W[pp * 128:(pp + 1) * 128, 0:512],
                W[pp * 128:(pp + 1) * 128, 512:1024],
                WSCh[mi, :, 64 * pp:64 * pp + 32],
                WSCh[mi, :, 64 * pp + 32:64 * pp + 64], pp % 2))
        amax_tasks.append(lambda gcol=gcol: emit_wbounce(em, ts, gcol))
        for p in range(4):
            amax_tasks.append(lambda mi=mi, gcol=gcol, p=p:
                              emit_wladder(em, ts, mi, gcol, p))
    for rc in range(8):
        for j in range(4):
            amax_tasks.append(lambda rc=rc, j=j: emit_wamax_pair(
                em, ts,
                DWh[rc * 128:(rc + 1) * 128, 1024 * j:1024 * j + 512],
                DWh[rc * 128:(rc + 1) * 128, 1024 * j + 512:1024 * (j + 1)],
                WSCh[2, :, 256 * rc + 64 * j:256 * rc + 64 * j + 32],
                WSCh[2, :, 256 * rc + 64 * j + 32:256 * rc + 64 * (j + 1)],
                (4 * rc + j) % 2))
    amax_tasks.append(lambda: emit_wbounce(em, ts, 12))
    for p in range(4):
        amax_tasks.append(lambda p=p: emit_wladder(em, ts, 2, 12, p))
    quant_tasks = []
    for mi, (W, WT) in enumerate(((GWh, WGTh), (UWh, WUTh))):
        for pp in range(32):
            for h in range(2):
                dests = [WT[4 * h + j, :, pp * 128:(pp + 1) * 128]
                         for j in range(4)]
                quant_tasks.append(
                    (mi, 2 * pp + h,
                     W[pp * 128:(pp + 1) * 128, h * 512:(h + 1) * 512], dests))
    for rc in range(8):
        for cp in range(8):
            dests = [WDTh[4 * cp + j, :, rc * 128:(rc + 1) * 128]
                     for j in range(4)]
            quant_tasks.append(
                (2, 8 * rc + cp,
                 DWh[rc * 128:(rc + 1) * 128, cp * 512:(cp + 1) * 512], dests))
    n_amax = len(amax_tasks)
    n_quant = len(quant_tasks)

    for wname, sbname, dout in [("Wq", "s_Wq", 512), ("Wk", "s_Wk", 512),
                                ("Wv", "s_Wv", 1024), ("Wb", "s_Wb", 4),
                                ("Wg", "s_Wg", 1024), ("Wo", "s_Wo", 1024)]:
        for ch in range(8):
            em.wait(s, ("vv",))
            em.dma("qs", ts["s_xt"][:, 0:dout],
                   dram[wname][ch * 128:(ch + 1) * 128, :])
            em.wait(v, ("qs",))
            em.op(v, lambda e, sbname=sbname, ch=ch, dout=dout: e.tensor_copy(
                out=ts[sbname][:, ch * dout:(ch + 1) * dout],
                in_=ts["s_xt"][:, 0:dout]), inc="vv")
            em.drain(v)

    em.op(v, lambda e: e.memset(ts["s_Sbf"][:], 0.0))
    em.op(v, lambda e: e.memset(ts["s_Sf"][:], 0.0))
    em.op(v, lambda e: e.memset(ts["s_mh"][:], -0.5), inc="vv")
    em.wait("gpsimd", ("vv",))
    for ch in range(8):
        em.op(v, lambda e, ch=ch: e.memset(ts["s_xnT"][:, ch * LP:ch * LP + 16], 0.0))
    em.op(v, lambda e: e.memset(ts["s_uf"][:], 0.0), inc="vv")

    adone = 0
    for tt in range(NT):
        xb = ts["s_xt"] if tt % 2 == 0 else ts["s_o"]
        psem = "wa" if tt % 2 == 0 else "wb"
        if tt < 2:
            em.wait(s, ("vv",))
        em.wait(s, (psem,))
        em.dma("qs", xb[:], X[tt * 128:(tt + 1) * 128, :])
        em.wait(v, ("qs",))
        em.wait(v, ("qt",))
        emit_ln(em, ts, xb[:], ts["s_xn"][:], out_ap=ts["s_xnb"][:, 0:1024])
        em.op(v, lambda e: e.nop(), inc=psem)
        em.wait("scalar", ("vv",))
        nt3 = ts["s_xnT"][:].rearrange("p (dch t) -> p dch t", dch=8)[
            :, :, 16 + tt * 128: 16 + tt * 128 + 128]
        em.dmaT(nt3, ts["s_xnb"][:, 0:1024])
        want = (tt + 1) * n_amax // NT
        while adone < want:
            amax_tasks[adone]()
            adone += 1
    em.wait(v, ("qt",))
    em.wait(t, ("qt",))

    qp = {"ld": 0, "ch": 0, "tr": 0}
    def qstep(lim):
        if qp["tr"] < qp["ch"] and (qp["ch"] - qp["tr"] >= 2 or
                                    qp["ch"] >= min(n_quant, lim)):
            mi, sidx, src, dests = quant_tasks[qp["tr"]]
            emit_wtrans(em, ts, qp["tr"] % 2, dests)
            qp["tr"] += 1
        if qp["ch"] < qp["ld"] and qp["ch"] < min(n_quant, lim):
            emit_wchain(em, ts, qp["ch"] % 2)
            qp["ch"] += 1
        if qp["ld"] < min(n_quant, lim + 1) and qp["ld"] <= qp["ch"]:
            mi, sidx, src, dests = quant_tasks[qp["ld"]]
            emit_wload(em, ts, mi, sidx, src)
            qp["ld"] += 1
    for tt in range(NT):
        cw3 = ts["s_cw"][:].rearrange("p (c i) -> p c i", i=4)
        base = 13 + tt * 128
        def xv(i, base=base):
            return ts["s_xnT"][:].rearrange("p (c t) -> p c t", c=8)[
                :, :, base + i: base + i + 128]
        q13 = ts["s_q1"][:].rearrange("p (c t) -> p c t", t=128)
        q23 = ts["s_q2"][:].rearrange("p (c t) -> p c t", t=128)
        em.wait(v, ("aa", "qs"))
        em.op(v, lambda e: e.tensor_tensor(
            out=q13, in0=xv(0), in1=cw3[:, :, 0:1].broadcast_to((128, 8, 128)),
            op=ALU.mult))
        for i in (1, 2, 3):
            em.op(v, lambda e, i=i: e.tensor_tensor(
                out=q23, in0=xv(i), in1=cw3[:, :, i:i + 1].broadcast_to((128, 8, 128)),
                op=ALU.mult))
            em.drain(v)
            em.op(v, lambda e: e.tensor_tensor(out=ts["s_q1"][:], in0=ts["s_q1"][:],
                                               in1=ts["s_q2"][:], op=ALU.add))
            em.drain(v)
        em.drain(v)
        em.op(v, lambda e: e.nop(), inc="vv")
        em.wait(a, ("vv",))
        em.op(a, lambda e: e.activation(out=ts["s_hT"][:], in_=ts["s_q1"][:],
                                        func=AF.Silu), inc="aa")
        em.wait(t, ("aa", "vv"))

        for ch in range(8):
            hsl = ts["s_hT"][:, ch * 128:(ch + 1) * 128]
            em.op(t, lambda e, ch=ch, hsl=hsl: e.matmul(
                out=ts["p_A"][:, 0:512], lhsT=hsl,
                rhs=ts["s_Wq"][:, ch * 512:(ch + 1) * 512],
                start=(ch == 0), stop=(ch == 7)))
            em.op(t, lambda e, ch=ch, hsl=hsl: e.matmul(
                out=ts["p_A"][:, 512:1024], lhsT=hsl,
                rhs=ts["s_Wk"][:, ch * 512:(ch + 1) * 512],
                start=(ch == 0), stop=(ch == 7)))
            em.op(t, lambda e, ch=ch, hsl=hsl: e.matmul(
                out=ts["p_D"][:, 0:4], lhsT=hsl,
                rhs=ts["s_Wb"][:, ch * 4:(ch + 1) * 4],
                start=(ch == 0), stop=(ch == 7)), inc="tt" if ch == 7 else None)
        em.wait(a, ("tt",))
        st = ts["s_st"]
        em.wait(v, ("tt", "qt"))
        em.op(v, lambda e: e.tensor_copy(out=ts["s_qk"][:], in_=ts["p_A"][:]),
              inc="vv")
        em.drain(v)
        em.wait(a, ("vv",))
        for h in range(8):
            em.op(a, lambda e, h=h: e.activation(
                out=ts["s_q3"][:, h * 128:(h + 1) * 128],
                in_=ts["s_qk"][:, h * 128:(h + 1) * 128],
                func=AF.Square, accum_out=st[:, 4 + h if h < 4 else h - 4:
                                             5 + h if h < 4 else h - 3]))
        em.op(a, lambda e: e.activation(out=ts["s_beta"][:], in_=ts["p_D"][:, 0:4],
                                        func=AF.Sigmoid), inc="aa")
        em.wait("scalar", ("vv", "tt"))
        em.dmaT(ts["s_qkT"][:].rearrange("p (c r) -> p c r", r=128), ts["s_qk"][:])
        qstep(8 * (tt + 1))
        em.wait("gpsimd", ("aa",))
        em.op("gpsimd", lambda e: e.tensor_scalar(out=st[:, 8:16], in0=st[:, 0:8],
                                                  scalar1=1e-24, scalar2=None,
                                                  op0=ALU.max))
        em.drain("gpsimd")
        em.op("gpsimd", lambda e: e.tensor_tensor(
            out=st[:, 8:16], in0=st[:, 8:16],
            in1=ts["s_mh"][:].broadcast_to((128, 8)), op=ALU.pow))
        em.drain("gpsimd")
        em.op("gpsimd", lambda e: e.tensor_scalar(out=st[:, 8:16], in0=st[:, 8:16],
                                                  scalar1=1e12, scalar2=None,
                                                  op0=ALU.min))
        em.drain("gpsimd")
        em.op("gpsimd", lambda e: e.tensor_tensor(out=st[:, 16:20], in0=st[:, 8:12],
                                                  in1=ts["s_beta"][:], op=ALU.mult))
        em.drain("gpsimd")
        em.op("gpsimd", lambda e: e.tensor_copy(out=st[:, 20:24], in_=st[:, 16:20]))
        em.drain("gpsimd")
        em.op("gpsimd", lambda e: e.tensor_tensor(out=st[:, 16:20], in0=st[:, 16:20],
                                                  in1=st[:, 8:12], op=ALU.mult),
              inc="gg")
        em.wait(v, ("gg",))
        em.wait(t, ("gg", "qt"))
        qstep(8 * (tt + 1))
        for ch in range(8):
            hsl = ts["s_hT"][:, ch * 128:(ch + 1) * 128]
            for half in range(2):
                em.op(t, lambda e, ch=ch, hsl=hsl, half=half: e.matmul(
                    out=ts["p_A"][:, half * 512:(half + 1) * 512], lhsT=hsl,
                    rhs=ts["s_Wv"][:, ch * 1024 + half * 512: ch * 1024 + (half + 1) * 512],
                    start=(ch == 0), stop=(ch == 7)),
                    inc="tt" if (ch == 7 and half == 1) else None)
        em.wait(v, ("tt",))
        em.op(v, lambda e: e.tensor_copy(out=ts["s_v"][:], in_=ts["p_A"][:]), inc="vv")
        qstep(8 * (tt + 1))
        em.wait(t, ("vv",))
        for ch in range(8):
            hsl = ts["s_hT"][:, ch * 128:(ch + 1) * 128]
            for half in range(2):
                em.op(t, lambda e, ch=ch, hsl=hsl, half=half: e.matmul(
                    out=ts["p_A"][:, half * 512:(half + 1) * 512], lhsT=hsl,
                    rhs=ts["s_Wg"][:, ch * 1024 + half * 512: ch * 1024 + (half + 1) * 512],
                    start=(ch == 0), stop=(ch == 7)),
                    inc="tt" if (ch == 7 and half == 1) else None)
        em.wait(v, ("tt",))
        em.op(v, lambda e: e.tensor_copy(out=ts["s_g"][:], in_=ts["p_A"][:]), inc="vv")
        em.drain(v)
        qstep(8 * (tt + 1))

        for c in range(2):
            hb = slice(64 * c, 64 * c + 64)
            tpK = 64 * c
            emit_chunk(em, ts, tt, c, hb, tpK, trilS, triuI, id4)
            qstep(8 * (tt + 1))

        st = ts["s_st"]
        em.wait(a, ("vv", "qs"))
        for h in range(4):
            em.op(a, lambda e, h=h: e.activation(
                out=ts["s_q2"][:, h * 256:(h + 1) * 256],
                in_=ts["s_o"][:, h * 256:(h + 1) * 256],
                func=AF.Square, accum_out=st[:, h:h + 1]))
        em.op(a, lambda e: e.activation(out=ts["s_q4"][:], in_=ts["s_g"][:],
                                        func=AF.Silu), inc="aa")
        em.wait("gpsimd", ("aa",))
        em.op("gpsimd", lambda e: e.tensor_scalar(out=st[:, 4:8], in0=st[:, 0:4],
                                                  scalar1=1.0 / 256.0, scalar2=1e-5,
                                                  op0=ALU.mult, op1=ALU.add))
        em.drain("gpsimd")
        em.op("gpsimd", lambda e: e.tensor_tensor(
            out=st[:, 4:8], in0=st[:, 4:8],
            in1=ts["s_mh"][:].broadcast_to((128, 4)), op=ALU.pow), inc="gg")
        em.wait(v, ("gg", "aa"))
        o4 = ts["s_o"][:].rearrange("p (h f) -> p h f", f=256)
        rs4 = st[:, 4:8].rearrange("p (h o) -> p h o", o=1).broadcast_to((128, 4, 256))
        em.op(v, lambda e: e.tensor_tensor(out=o4, in0=o4, in1=rs4, op=ALU.mult))
        em.drain(v)
        nw4 = ts["s_nw"][:].rearrange("p (o f) -> p o f", o=1).broadcast_to((128, 4, 256))
        em.op(v, lambda e: e.tensor_tensor(out=o4, in0=o4, in1=nw4, op=ALU.mult))
        em.drain(v)
        em.wait(v, ("qt",))
        em.op(v, lambda e: e.tensor_tensor(out=ts["s_gtd"][:], in0=ts["s_o"][:],
                                           in1=ts["s_q4"][:], op=ALU.mult), inc="vv")
        em.drain(v)
        qstep(8 * (tt + 1))
        em.wait("scalar", ("vv",))
        outT = ts["s_gTt"][:].rearrange("p (c r) -> p c r", r=128)
        em.dmaT(outT, ts["s_gtd"][:])
        em.wait(t, ("qt", "aa", "vv"))
        for ch in range(8):
            for half in range(2):
                em.op(t, lambda e, ch=ch, half=half: e.matmul(
                    out=ts["p_A"][:, half * 512:(half + 1) * 512],
                    lhsT=ts["s_gTt"][:, ch * 128:(ch + 1) * 128],
                    rhs=ts["s_Wo"][:, ch * 1024 + half * 512: ch * 1024 + (half + 1) * 512],
                    start=(ch == 0), stop=(ch == 7)),
                    inc="tt" if (ch == 7 and half == 1) else None)
        em.wait(v, ("tt", "qs"))
        em.op(v, lambda e: e.tensor_copy(out=ts["s_q2"][:], in_=ts["p_A"][:]),
              inc="vv")
        em.drain(v)
        qstep(8 * (tt + 1))
        em.wait(s, ("vv",))
        rA = tt if tt < NTO else NTO
        rB = tt - NTO if tt >= NTO else NTO
        em.dma("qs", dram["o_woA"][rA * 128:(rA + 1) * 128, :], ts["s_q2"][:])
        em.dma("qs", dram["o_woB"][rB * 128:(rB + 1) * 128, :], ts["s_q2"][:])
        while qp["ch"] < min(n_quant, (tt + 1) * 8) or \
                (tt == NT - 1 and qp["tr"] < n_quant):
            qstep((tt + 1) * 8)


def emit_chunk(em, ts, tt, c, hb, tpK, trilS, triuI, id4):
    v, a, t = "vector", "scalar", "tensor"
    st = ts["s_st"]
    tok0 = 64 * c
    tpO = (0, 64 * c)
    tpKK = (64 * c, 64 * c)

    em.wait(t, ("vv", "aa"))
    for h in range(4):
        kTc = ts["s_qkT"][:, 512 + h * 128 + tok0: 512 + h * 128 + tok0 + 64]
        em.op(t, lambda e, h=h, kTc=kTc: e.matmul(
            out=ts["p_C"][hb, h * 64:(h + 1) * 64], lhsT=kTc, rhs=kTc,
            start=True, stop=True, tile_position=tpO),
            inc="tt" if h == 3 else None)
    em.wait(v, ("tt",))
    g3 = ts["p_C"][hb, :].rearrange("p (h f) -> p h f", f=64)
    tril3 = trilS[hb, :].rearrange("p (o f) -> p o f", o=1).broadcast_to((64, 4, 64))
    em.op(v, lambda e: e.tensor_tensor(out=ts["s_C"][hb, :].rearrange(
        "p (h f) -> p h f", f=64), in0=g3, in1=tril3, op=ALU.mult))
    em.drain(v)
    s3 = st[hb, 16:20].rearrange("p (h o) -> p h o", o=1).broadcast_to((64, 4, 64))
    em.op(v, lambda e: e.tensor_tensor(out=ts["s_dg"][hb, :].rearrange(
        "p (h f) -> p h f", f=64), in0=id4[hb, :].rearrange("p (h f) -> p h f", f=64),
        in1=s3, op=ALU.mult))
    em.op(v, lambda e: e.tensor_tensor(out=ts["s_At"][hb, :].rearrange(
        "p (h f) -> p h f", f=64), in0=ts["s_C"][hb, :].rearrange(
        "p (h f) -> p h f", f=64), in1=s3, op=ALU.mult), inc="vv")
    em.drain(v)
    em.wait(t, ("vv",))
    for h in range(4):
        em.op(t, lambda e, h=h: e.matmul(
            out=ts["p_B"][hb, h * 64:(h + 1) * 64],
            lhsT=ts["s_C"][hb, h * 64:(h + 1) * 64],
            rhs=ts["s_dg"][hb, h * 64:(h + 1) * 64],
            start=True, stop=True, tile_position=tpKK),
            inc="tt" if h == 3 else None)
    em.wait(v, ("tt",))
    em.op(v, lambda e: e.tensor_copy(out=ts["s_AtT"][hb, :], in_=ts["p_B"][hb, 0:256]),
          inc="vv")
    em.drain(v)
    em.wait(t, ("vv",))
    for h in range(4):
        kTc = ts["s_qkT"][:, 512 + h * 128 + tok0: 512 + h * 128 + tok0 + 64]
        em.op(t, lambda e, h=h, kTc=kTc: e.matmul(
            out=ts["p_A"][hb, h * 256:(h + 1) * 256], lhsT=kTc,
            rhs=ts["s_Sbf"][:, h * 256:(h + 1) * 256],
            start=True, stop=True, tile_position=tpO),
            inc="tt" if h == 3 else None)
    def pwY(Lv):
        return ts["s_pw"][hb, Lv * 512: Lv * 512 + 256]
    def pwYT(Lv):
        return ts["s_pw"][hb, Lv * 512 + 256: Lv * 512 + 512]
    prevY, prevYT = ts["s_At"], ts["s_AtT"]
    for Lv in range(4):
        em.wait(t, ("vv",))
        for h in range(4):
            hs = slice(h * 64, (h + 1) * 64)
            em.op(t, lambda e, h=h, hs=hs: e.matmul(
                out=ts["p_B"][hb, h * 64:(h + 1) * 64],
                lhsT=prevYT[hb, hs], rhs=prevY[hb, hs],
                start=True, stop=True, tile_position=tpKK))
            em.op(t, lambda e, h=h, hs=hs: e.matmul(
                out=ts["p_B"][hb, 256 + h * 64:256 + (h + 1) * 64],
                lhsT=prevY[hb, hs], rhs=prevYT[hb, hs],
                start=True, stop=True, tile_position=tpKK),
                inc="tt" if h == 3 else None)
        em.wait(v, ("tt",))
        if Lv < 3:
            em.op(v, lambda e, Lv=Lv: e.tensor_copy(
                out=ts["s_pw"][hb, Lv * 512:(Lv + 1) * 512], in_=ts["p_B"][hb, :]),
                inc="vv")
        else:
            em.op(v, lambda e, Lv=Lv: e.tensor_copy(
                out=ts["s_pw"][hb, Lv * 512:(Lv + 1) * 512], in_=ts["p_B"][hb, :]),
                inc="vv")
        em.drain(v)
        prevY, prevYT = None, None
        prevY = _PwSlice(ts, Lv, 0)
        prevYT = _PwSlice(ts, Lv, 1)
    em.wait(v, ("tt",))
    ks3 = ts["p_A"][hb, :].rearrange("p (h f) -> p h f", f=256)
    rk3 = st[hb, 8:12].rearrange("p (h o) -> p h o", o=1).broadcast_to((64, 4, 256))
    u3 = ts["s_uf"][hb, :].rearrange("p (h f) -> p h f", f=256)
    v3 = ts["s_v"][hb, :].rearrange("p (h f) -> p h f", f=256)
    em.op(v, lambda e: e.tensor_tensor(out=u3, in0=ks3, in1=rk3, op=ALU.mult))
    em.drain(v)
    em.op(v, lambda e: e.tensor_tensor(out=u3, in0=v3, in1=u3, op=ALU.subtract))
    em.drain(v)
    rkb3 = st[hb, 20:24].rearrange("p (h o) -> p h o", o=1).broadcast_to((64, 4, 256))
    ub3 = ts["s_ub"][hb, :].rearrange("p (h f) -> p h f", f=256)
    em.op(v, lambda e: e.tensor_tensor(out=ub3, in0=u3, in1=rkb3, op=ALU.mult),
          inc="vv")
    em.drain(v)
    em.wait(t, ("vv",))
    for h in range(4):
        hs = slice(h * 64, (h + 1) * 64)
        em.op(t, lambda e, h=h, hs=hs: e.matmul(
            out=ts["p_B"][hb, h * 64:(h + 1) * 64],
            lhsT=_PwSlice(ts, 3, 0)[hb, hs], rhs=_PwSlice(ts, 3, 1)[hb, hs],
            start=True, stop=True, tile_position=tpKK),
            inc="tt" if h == 3 else None)
    em.wait(v, ("tt",))
    em.op(v, lambda e: e.tensor_copy(out=ts["s_Lt"][hb, :], in_=ts["p_B"][hb, 0:256]),
          inc="vv")
    em.drain(v)
    app_seq = [(ts["s_Lt"], 1.0), (_PwSlice(ts, 3, 1), 1.0), (_PwSlice(ts, 2, 1), 1.0),
               (_PwSlice(ts, 1, 1), 1.0), (_PwSlice(ts, 0, 1), 1.0),
               (ts["s_AtT"], -1.0)]
    for lhsT_t, sgn in app_seq:
        em.wait(t, ("vv",))
        for h in range(4):
            em.op(t, lambda e, h=h, lhsT_t=lhsT_t: e.matmul(
                out=ts["p_A"][hb, h * 256:(h + 1) * 256],
                lhsT=lhsT_t[hb, h * 64:(h + 1) * 64],
                rhs=ts["s_ub"][hb, h * 256:(h + 1) * 256],
                start=True, stop=True, tile_position=tpKK),
                inc="tt" if h == 3 else None)
        em.wait(v, ("tt",))
        em.op(v, lambda e, sgn=sgn: e.scalar_tensor_tensor(
            out=ts["s_ub"][hb, :], in0=ts["p_A"][hb, :], scalar=sgn,
            in1=ts["s_ub"][hb, :], op0=ALU.mult, op1=ALU.add), inc="vv")
        em.drain(v)
    em.wait(t, ("vv",))
    for h in range(4):
        qTc = ts["s_qkT"][:, h * 128 + tok0: h * 128 + tok0 + 64]
        em.op(t, lambda e, h=h, qTc=qTc: e.matmul(
            out=ts["p_D"][hb, h * 256:(h + 1) * 256], lhsT=qTc,
            rhs=ts["s_Sbf"][:, h * 256:(h + 1) * 256],
            start=(h % 2 == 0), stop=False, tile_position=tpO))
    for h in range(4):
        qTc = ts["s_qkT"][:, h * 128 + tok0: h * 128 + tok0 + 64]
        kTc = ts["s_qkT"][:, 512 + h * 128 + tok0: 512 + h * 128 + tok0 + 64]
        em.op(t, lambda e, h=h, qTc=qTc, kTc=kTc: e.matmul(
            out=ts["p_C"][hb, h * 64:(h + 1) * 64], lhsT=kTc, rhs=qTc,
            start=True, stop=True, tile_position=tpO),
            inc="tt" if h == 3 else None)
    em.wait(v, ("tt",))
    r3 = ts["p_C"][hb, :].rearrange("p (h f) -> p h f", f=64)
    triu3 = triuI[hb, :].rearrange("p (o f) -> p o f", o=1).broadcast_to((64, 4, 64))
    em.op(v, lambda e: e.tensor_tensor(out=ts["s_Lt"][hb, :].rearrange(
        "p (h f) -> p h f", f=64), in0=r3, in1=triu3, op=ALU.mult), inc="vv")
    em.drain(v)
    em.wait(t, ("vv",))
    for h in range(4):
        em.op(t, lambda e, h=h: e.matmul(
            out=ts["p_D"][hb, h * 256:(h + 1) * 256],
            lhsT=ts["s_Lt"][hb, h * 64:(h + 1) * 64],
            rhs=ts["s_ub"][hb, h * 256:(h + 1) * 256],
            start=False, stop=(h % 2 == 1), tile_position=tpKK),
            inc="tt" if h == 3 else None)
    for h in range(4):
        em.op(t, lambda e, h=h: e.matmul(
            out=ts["p_S"][:, h * 256:(h + 1) * 256],
            lhsT=ts["s_qk"][hb, 512 + h * 128: 512 + (h + 1) * 128],
            rhs=ts["s_ub"][hb, h * 256:(h + 1) * 256],
            start=True, stop=True, tile_position=(64 * c, 0)),
            inc="tt" if h == 3 else None)
    em.wait(v, ("tt",))
    o3 = ts["s_o"][hb, :].rearrange("p (h f) -> p h f", f=256)
    rq3 = st[hb, 12:16].rearrange("p (h o) -> p h o", o=1).broadcast_to((64, 4, 256))
    em.op(v, lambda e: e.tensor_tensor(out=o3, in0=ts["p_D"][hb, :].rearrange(
        "p (h f) -> p h f", f=256), in1=rq3, op=ALU.mult))
    em.op(v, lambda e: e.scalar_tensor_tensor(
        out=ts["s_Sf"][:], in0=ts["p_S"][:], scalar=1.0, in1=ts["s_Sf"][:],
        op0=ALU.mult, op1=ALU.add))
    em.drain(v)
    em.op(v, lambda e: e.tensor_copy(out=ts["s_Sbf"][:], in_=ts["s_Sf"][:]), inc="vv")
    em.drain(v)


class _PwSlice:
    def __init__(self, ts, Lv, which):
        self.t = ts["s_pw"]
        self.base = Lv * 512 + which * 256
    def __getitem__(self, key):
        hb, hs = key
        if isinstance(hs, slice):
            return self.t[hb, self.base + hs.start: self.base + hs.stop]
        raise KeyError(key)



_HOST_CONSTS = None

def _host_consts():
    global _HOST_CONSTS
    if _HOST_CONSTS is None:
        tril = np.tril(np.ones((64, 64), np.float32), -1)
        triu = np.triu(np.ones((64, 64), np.float32), 0)
        ident = np.eye(64, dtype=np.float32)
        id4 = np.concatenate([ident] * 4, axis=1)
        row = np.concatenate([tril, triu, id4], axis=1)
        base = np.concatenate([row, row], axis=0)
        _HOST_CONSTS = np.ascontiguousarray(
            np.concatenate([base, np.eye(128, dtype=np.float32)], axis=1))
    return _HOST_CONSTS


def kernel(**inputs):
    x = np.ascontiguousarray(inputs["L0_attn_hidden_states"], dtype=np.float32)
    consts = _host_consts()
    nw_rep = np.ascontiguousarray(
        np.broadcast_to(inputs["L0_attn_norm_w"][None, :], (128, 256)), np.float32)
    sin_rep = np.ascontiguousarray(
        np.broadcast_to(inputs["L0_ffn_s_in"].reshape(1, 1), (128, 1)), np.float32)
    sid_rep = np.ascontiguousarray(
        np.broadcast_to(inputs["L0_ffn_s_in_down"].reshape(1, 1), (128, 1)), np.float32)
    hsel = [np.ascontiguousarray(
        np.broadcast_to(np.array([[1.0 - h, float(h)]], np.float32), (128, 2)))
        for h in (0, 1)]
    shared = {
        "Wq": np.ascontiguousarray(inputs["L0_attn_Wq"], np.float32),
        "Wk": np.ascontiguousarray(inputs["L0_attn_Wk"], np.float32),
        "Wv": np.ascontiguousarray(inputs["L0_attn_Wv"], np.float32),
        "Wb": np.ascontiguousarray(inputs["L0_attn_Wb"], np.float32),
        "Wg": np.ascontiguousarray(inputs["L0_attn_Wg"], np.float32),
        "Wo": np.ascontiguousarray(inputs["L0_attn_Wo"], np.float32),
        "conv_w": np.ascontiguousarray(inputs["L0_attn_conv_w"], np.float32),
        "norm_w_rep": nw_rep,
        "s_in_rep": sin_rep,
        "s_in_down_rep": sid_rep,
        "masks": consts,
        "gate_w": np.ascontiguousarray(inputs["L0_ffn_gate_w"], np.float32),
        "up_w": np.ascontiguousarray(inputs["L0_ffn_up_w"], np.float32),
        "down_w": np.ascontiguousarray(inputs["L0_ffn_down_w"], np.float32),
    }
    in_maps = []
    for c in range(8):
        b, half = c // 2, c % 2
        m = dict(shared)
        m["x_seq"] = np.ascontiguousarray(x[b])
        m["x_own"] = np.ascontiguousarray(x[b, half * HALF:(half + 1) * HALF])
        m["half_sel_rep"] = hsel[half]
        in_maps.append(m)
    nc = build_program()
    res = run_bass_kernel_spmd(nc, in_maps, core_ids=list(range(8)))
    out = np.empty((B, L, D), np.float32)
    for c in range(8):
        b, half = c // 2, c % 2
        out[b, half * HALF:(half + 1) * HALF] = res.results[c]["out_half"]
    return out


# version 19 (final)
# speedup vs baseline: 1.0258x; 1.0072x over previous
import numpy as np
import ml_dtypes
from contextlib import ExitStack
import concourse.bass as bass
import concourse.mybir as mybir
from concourse.bass_isa import ReduceOp
from concourse import library_config
from concourse.bass_utils import run_bass_kernel_spmd

F32 = mybir.dt.float32
BF16 = mybir.dt.bfloat16
FP8 = mybir.dt.float8e4
AF = mybir.ActivationFunctionType
ALU = mybir.AluOpType
AX = mybir.AxisListType

B, L, D = 4, 4096, 1024
HALF = L // 2
H, DQH, DVH = 4, 128, 256
NT = L // 128
NTO = HALF // 128
DI = 4096

SEM_NAMES = ("qs", "qt", "vv", "aa", "tt", "gg", "wa", "wb", "wc", "wd")


def build_program():
    nc = bass.Bass("TRN2", target_bir_lowering=False)

    dram = {}
    def din(name, shape, dt=F32):
        dram[name] = nc.dram_tensor(name, list(shape), dt, kind="ExternalInput").ap()
        return dram[name]
    def dscr(name, shape, dt):
        dram[name] = nc.dram_tensor(name, list(shape), dt, kind="Internal").ap()
        return dram[name]

    X = din("x_seq", [L, D])
    XO = din("x_own", [HALF, D])
    WQ = din("Wq", [D, 512]); WK = din("Wk", [D, 512]); WV = din("Wv", [D, 1024])
    WB = din("Wb", [D, 4]); WG = din("Wg", [D, 1024]); WO = din("Wo", [D, 1024])
    CW = din("conv_w", [D, 4])
    NW = din("norm_w_rep", [128, 256])
    GW = din("gate_w", [DI, D]); UW = din("up_w", [DI, D])
    DW = din("down_w", [D, DI])
    SIN = din("s_in_rep", [128, 1]); SID = din("s_in_down_rep", [128, 1])
    HSEL = din("half_sel_rep", [128, 2])
    MSK = din("masks", [128, 6 * 64 + 128])
    OUT = nc.dram_tensor("out_half", [HALF, D], F32, kind="ExternalOutput").ap()
    HQS = dscr("hq_scratch", [HALF, DI], BF16)
    WMX = dscr("wmax_scratch", [128], F32)
    OWA = dscr("o_woA", [(NTO + 1) * 128, D], F32)
    OWB = dscr("o_woB", [(NTO + 1) * 128, D], F32)
    WGT = dscr("wgT_hbm", [8, 128, DI], BF16)
    WUT = dscr("wuT_hbm", [8, 128, DI], BF16)
    WDT = dscr("wdT_hbm", [32, 128, D], BF16)
    WSC = dscr("wsc_scratch", [3, 128, 2048], F32)
    WDQ = dscr("wdq_scratch", [3, 128, 2048], F32)
    WRD = dscr("wrd_scratch", [3, 128, 2048], F32)

    st_outer = ExitStack()
    en = st_outer.enter_context
    sem = {n: en(nc.semaphore(n)) for n in SEM_NAMES}

    s_msk = en(nc.sbuf_tensor("s_msk", [128, 6 * 64 + 128], F32))
    s_nw = en(nc.sbuf_tensor("s_nw", [128, 256], F32))
    s_cw = en(nc.sbuf_tensor("s_cw", [128, 8 * 4], F32))
    s_gs = en(nc.sbuf_tensor("s_gs", [128, 16], F32))
    s_q1 = en(nc.sbuf_tensor("s_q1", [128, 1024], F32))
    s_q2 = en(nc.sbuf_tensor("s_q2", [128, 1024], F32))
    s_q3 = en(nc.sbuf_tensor("s_q3", [128, 1024], F32))
    s_q4 = en(nc.sbuf_tensor("s_q4", [128, 1024], F32))
    s_qsg = en(nc.sbuf_tensor("s_qsg", [128, 1024], F32))
    s_qam = en(nc.sbuf_tensor("s_qam", [128, 64], F32))
    s_qs8 = en(nc.sbuf_tensor("s_qs8", [128, 64], F32))
    s_qdq = en(nc.sbuf_tensor("s_qdq", [128, 64], F32))
    s_qrd = en(nc.sbuf_tensor("s_qrd", [128, 64], F32))
    s_ln = en(nc.sbuf_tensor("s_ln", [128, 16], F32))
    s_xt = en(nc.sbuf_tensor("s_xt", [128, 1024], F32))
    s_xn = en(nc.sbuf_tensor("s_xn", [128, 1024], F32))
    s_mh = en(nc.sbuf_tensor("s_mh", [128, 1], F32))
    s_hsel = en(nc.sbuf_tensor("s_hsel", [128, 2], F32))
    s_id = en(nc.sbuf_tensor("s_id", [128, 128], BF16))

    ts = dict(nc=nc, dram=dram, sem=sem, OUT=OUT, X=X, XO=XO,
              s_msk=s_msk, s_nw=s_nw, s_cw=s_cw, s_gs=s_gs,
              s_q1=s_q1, s_q2=s_q2, s_q3=s_q3, s_q4=s_q4, s_qsg=s_qsg,
              s_qam=s_qam, s_qs8=s_qs8, s_qdq=s_qdq, s_qrd=s_qrd,
              s_ln=s_ln, s_xt=s_xt, s_xn=s_xn, s_mh=s_mh, s_hsel=s_hsel,
              s_id=s_id)

    st_a = ExitStack()
    ena = st_a.enter_context
    s_xnT = ena(nc.sbuf_tensor("s_xnT", [128, 8 * (L + 16)], BF16))
    s_xnb = ena(nc.sbuf_tensor("s_xnb", [128, 1024], BF16))
    s_hT = ena(nc.sbuf_tensor("s_hT", [128, 8 * 128], BF16))
    s_Wq = ena(nc.sbuf_tensor("s_Wq", [128, 8 * 512], BF16))
    s_Wk = ena(nc.sbuf_tensor("s_Wk", [128, 8 * 512], BF16))
    s_Wv = ena(nc.sbuf_tensor("s_Wv", [128, 8 * 1024], BF16))
    s_Wb = ena(nc.sbuf_tensor("s_Wb", [128, 8 * 4], BF16))
    s_Wg = ena(nc.sbuf_tensor("s_Wg", [128, 8 * 1024], BF16))
    s_Wo = ena(nc.sbuf_tensor("s_Wo", [128, 8 * 1024], BF16))
    s_qk = ena(nc.sbuf_tensor("s_qk", [128, 1024], BF16))
    s_v = ena(nc.sbuf_tensor("s_v", [128, 1024], BF16))
    s_g = ena(nc.sbuf_tensor("s_g", [128, 1024], BF16))
    s_qkT = ena(nc.sbuf_tensor("s_qkT", [128, 1024], BF16))
    s_beta = ena(nc.sbuf_tensor("s_beta", [128, 4], F32))
    s_st = ena(nc.sbuf_tensor("s_st", [128, 24], F32))
    s_Sbf = ena(nc.sbuf_tensor("s_Sbf", [128, 1024], BF16))
    s_Sf = ena(nc.sbuf_tensor("s_Sf", [128, 1024], F32))
    s_uf = ena(nc.sbuf_tensor("s_uf", [128, 1024], F32))
    s_ub = ena(nc.sbuf_tensor("s_ub", [128, 1024], BF16))
    s_C = ena(nc.sbuf_tensor("s_C", [128, 256], BF16))
    s_dg = ena(nc.sbuf_tensor("s_dg", [128, 256], BF16))
    s_At = ena(nc.sbuf_tensor("s_At", [128, 256], BF16))
    s_AtT = ena(nc.sbuf_tensor("s_AtT", [128, 256], BF16))
    s_pw = ena(nc.sbuf_tensor("s_pw", [128, 9 * 256], BF16))
    s_Lt = ena(nc.sbuf_tensor("s_Lt", [128, 256], BF16))
    s_o = ena(nc.sbuf_tensor("s_o", [128, 1024], F32))
    s_gtd = ena(nc.sbuf_tensor("s_gtd", [128, 1024], BF16))
    s_gTt = ena(nc.sbuf_tensor("s_gTt", [128, 8 * 128], BF16))
    s_wsrc = ena(nc.sbuf_tensor("s_wsrc", [128, 512], F32))
    s_wq1 = ena(nc.sbuf_tensor("s_wq1", [128, 512], F32))
    s_wq3 = ena(nc.sbuf_tensor("s_wq3", [128, 512], F32))
    s_wlad = ena(nc.sbuf_tensor("s_wlad", [128, 5 * 32], F32))
    ts.update(s_xnT=s_xnT, s_xnb=s_xnb, s_hT=s_hT, s_Wq=s_Wq, s_Wk=s_Wk,
              s_Wv=s_Wv, s_Wb=s_Wb, s_Wg=s_Wg, s_Wo=s_Wo, s_qk=s_qk,
              s_v=s_v, s_g=s_g, s_qkT=s_qkT, s_beta=s_beta,
              s_st=s_st, s_Sbf=s_Sbf, s_uf=s_uf, s_ub=s_ub, s_C=s_C,
              s_dg=s_dg, s_At=s_At, s_AtT=s_AtT, s_pw=s_pw, s_Lt=s_Lt,
              s_o=s_o, s_Sf=s_Sf, s_gtd=s_gtd, s_gTt=s_gTt,
              s_wsrc=s_wsrc, s_wq1=s_wq1, s_wq3=s_wq3, s_wlad=s_wlad)
    st_a.close()
    s_xqT = en(nc.sbuf_tensor("s_xqT", [128, 8 * 2048], BF16))
    s_wgc = en(nc.sbuf_tensor("s_wgc", [128, 8 * 1024], BF16))
    s_wuc = en(nc.sbuf_tensor("s_wuc", [128, 8 * 1024], BF16))
    s_sg = en(nc.sbuf_tensor("s_sg", [128, 1024], F32))
    s_hid = en(nc.sbuf_tensor("s_hid", [128, 1024], F32))
    s_hqb = en(nc.sbuf_tensor("s_hqb", [128, 1024], BF16))
    s_hqT = en(nc.sbuf_tensor("s_hqT", [128, 32 * 512], BF16))
    s_gf = en(nc.sbuf_tensor("s_gf", [128, 1024], F32))
    s_wT = en(nc.sbuf_tensor("s_wT", [128, 32 * 1024], BF16))
    ts.update(s_xqT=s_xqT, s_wgc=s_wgc, s_wuc=s_wuc, s_sg=s_sg,
              s_hid=s_hid, s_hqb=s_hqb, s_hqT=s_hqT, s_gf=s_gf, s_wT=s_wT)
    p_S = en(nc.psum_tensor("p_S", [128, 1024], F32))
    p_A = en(nc.psum_tensor("p_A", [128, 1024], F32))
    p_B = en(nc.psum_tensor("p_B", [128, 512], F32))
    p_C = en(nc.psum_tensor("p_C", [128, 256], F32))
    p_D = en(nc.psum_tensor("p_D", [128, 1024], F32))
    ts.update(p_S=p_S, p_A=p_A, p_B=p_B, p_C=p_C, p_D=p_D)

    with nc.Block() as block:
        @block.sync
        def _(eng):
            emit_all("sync", eng, ts)

        @block.scalar
        def _(eng):
            emit_all("scalar", eng, ts)

        @block.vector
        def _(eng):
            emit_all("vector", eng, ts)

        @block.tensor
        def _(eng):
            emit_all("tensor", eng, ts)

        @block.gpsimd
        def _(eng):
            emit_all("gpsimd", eng, ts)

    st_outer.close()
    return nc



class Emit:
    def __init__(self, which, eng, ts):
        self.w = which
        self.e = eng
        self.ts = ts
        self.sem = ts["sem"]
        self.C = {n: 0 for n in SEM_NAMES}

    def op(self, eng_name, fn, inc=None, incby=1):
        i = fn(self.e) if self.w == eng_name else None
        if inc is not None:
            self.C[inc] += incby
            if i is not None:
                i.then_inc(self.sem[inc], incby)
        return i

    def dma(self, queue, out, in_, sem=None):
        eng_name = "sync" if queue == "qs" else "scalar"
        sname = sem if sem is not None else queue
        if self.w == eng_name:
            self.e.dma_start(out=out, in_=in_).then_inc(self.sem[sname], 16)
        self.C[sname] += 16

    def dmaT(self, out, in_):
        if self.w == "scalar":
            self.e.dma_start_transpose(out=out, in_=in_).then_inc(self.sem["qt"], 16)
        self.C["qt"] += 16

    def wait(self, eng_name, sems):
        for s in sems:
            if self.w == eng_name:
                self.e.wait_ge(self.sem[s], self.C[s])
            elif s == "qs" and self.w == "sync" and eng_name != "sync":
                self.e.wait_ge(self.sem[s], self.C[s])
            elif s in ("qt", "wd") and self.w == "scalar" and eng_name != "scalar":
                self.e.wait_ge(self.sem[s], self.C[s])

    def drain(self, eng_name):
        if self.w == eng_name:
            self.e.drain()

    def barrier(self):
        for s in SEM_NAMES:
            self.e.wait_ge(self.sem[s], self.C[s])


def emit_all(which, eng, ts):
    em = Emit(which, eng, ts)
    phase_p0(em, ts)
    em.barrier()
    phase_a(em, ts)
    em.barrier()
    phase_f(em, ts)
    em.wait("sync", ("qs",))
    em.wait("scalar", ("qt",))



def q3v(ap, nb):
    return ap.rearrange("p (b k) -> p b k", k=16)


def bcast16(ap, nb):
    return ap.rearrange("p (b o) -> p b o", o=1).broadcast_to((128, nb, 16))


def emit_quant(em, ts, src_ap, dst_ap, ncols, gcol):
    v, a = "vector", "scalar"
    gs = ts["s_gs"]
    nb = ncols // 16
    q1, q2, q3, q4, sg = (ts["s_q1"][:, 0:ncols], ts["s_q2"][:, 0:ncols],
                          ts["s_q3"][:, 0:ncols], ts["s_q4"][:, 0:ncols],
                          ts["s_qsg"][:, 0:ncols])
    am, s8, dq, rd = (ts["s_qam"][:, 0:nb], ts["s_qs8"][:, 0:nb],
                      ts["s_qdq"][:, 0:nb], ts["s_qrd"][:, 0:nb])
    zt = ts["s_q2"][:, 0:nb]
    pt = ts["s_q2"][:, nb:2 * nb]
    ct = ts["s_q2"][:, 2 * nb:3 * nb]
    vt = ts["s_q3"][:, 0:nb]
    ut = ts["s_q3"][:, nb:2 * nb]
    em.op(v, lambda e: e.tensor_reduce(out=am, in_=q3v(src_ap, nb), axis=AX.X,
                                       op=ALU.max, apply_absolute_value=True))
    em.drain(v)
    em.op(v, lambda e: e.tensor_scalar(out=zt, in0=am, scalar1=gs[:, gcol + 1:gcol + 2],
                                       scalar2=224.0, op0=ALU.mult, op1=ALU.min))
    em.drain(v)
    em.op(v, lambda e: e.memset(pt, 2.0 ** -7))
    em.drain(v)
    for k in (8, 4, 2, 1):
        em.op(v, lambda e, k=k: e.tensor_scalar(out=ct, in0=pt, scalar1=float(2 ** k),
                                                scalar2=None, op0=ALU.mult))
        em.drain(v)
        em.op(v, lambda e: e.tensor_tensor(out=ct, in0=ct, in1=zt, op=ALU.is_le))
        em.drain(v)
        em.op(v, lambda e, k=k: e.tensor_scalar(out=ct, in0=ct,
                                                scalar1=float(2 ** k - 1),
                                                scalar2=1.0, op0=ALU.mult, op1=ALU.add))
        em.drain(v)
        em.op(v, lambda e: e.tensor_tensor(out=pt, in0=pt, in1=ct, op=ALU.mult))
        em.drain(v)
    em.op(v, lambda e: e.reciprocal(out=ct, in_=pt))
    em.drain(v)
    em.op(v, lambda e: e.scalar_tensor_tensor(out=ut, in0=zt, scalar=8.0, in1=ct,
                                              op0=ALU.mult, op1=ALU.mult))
    em.drain(v)
    em.op(v, lambda e: e.memset(vt, 0.0))
    em.drain(v)
    for k in (16, 8, 4, 2, 1):
        em.op(v, lambda e, k=k: e.scalar_tensor_tensor(out=ct, in0=vt,
                                                       scalar=float(k) - 0.5, in1=ut,
                                                       op0=ALU.add, op1=ALU.is_le))
        em.drain(v)
        em.op(v, lambda e, k=k: e.scalar_tensor_tensor(out=vt, in0=ct,
                                                       scalar=float(k), in1=vt,
                                                       op0=ALU.mult, op1=ALU.add))
        em.drain(v)
    em.op(v, lambda e: e.scalar_tensor_tensor(out=s8, in0=vt, scalar=0.125, in1=pt,
                                              op0=ALU.mult, op1=ALU.mult))
    em.drain(v)
    em.op(v, lambda e: e.tensor_scalar(out=dq, in0=s8, scalar1=gs[:, gcol + 2:gcol + 3],
                                       scalar2=None, op0=ALU.mult))
    em.op(v, lambda e: e.tensor_scalar(out=rd, in0=s8, scalar1=1e-30,
                                       scalar2=None, op0=ALU.max))
    em.drain(v)
    em.op(v, lambda e: e.reciprocal(out=rd, in_=rd))
    em.drain(v)
    em.op(v, lambda e: e.tensor_scalar(out=rd, in0=rd, scalar1=gs[:, gcol:gcol + 1],
                                       scalar2=None, op0=ALU.mult), inc="vv")
    em.drain(v)
    em.wait("gpsimd", ("vv", "gg"))
    em.op("gpsimd", lambda e: e.tensor_tensor(out=q3v(q1, nb),
                                              in0=q3v(src_ap, nb),
                                              in1=bcast16(rd, nb), op=ALU.mult),
          inc="gg")
    em.wait(a, ("gg",))
    em.op(a, lambda e: e.activation(out=sg, in_=q1, func=AF.Sign))
    em.op(a, lambda e: e.activation(out=q2, in_=q1, func=AF.Abs), inc="gg")
    em.wait(v, ("gg",))
    em.op(v, lambda e: e.tensor_scalar(out=q3, in0=q2, scalar1=0.5,
                                       scalar2=None, op0=ALU.is_ge))
    em.drain(v)
    for th in (1.5, 2.5, 3.5):
        em.op(v, lambda e, th=th: e.scalar_tensor_tensor(
            out=q3, in0=q2, scalar=th, in1=q3, op0=ALU.is_ge, op1=ALU.add))
        em.drain(v)
    em.op(v, lambda e: e.tensor_scalar(out=q4, in0=q2, scalar1=5.0,
                                       scalar2=None, op0=ALU.is_ge))
    em.drain(v)
    em.op(v, lambda e: e.scalar_tensor_tensor(
        out=q4, in0=q2, scalar=7.0, in1=q4, op0=ALU.is_ge, op1=ALU.add))
    em.drain(v)
    em.op(v, lambda e: e.scalar_tensor_tensor(
        out=q3, in0=q4, scalar=2.0, in1=q3, op0=ALU.mult, op1=ALU.add))
    em.drain(v)
    em.op(v, lambda e: e.tensor_scalar(out=q4, in0=q2, scalar1=10.0,
                                       scalar2=None, op0=ALU.is_ge))
    em.drain(v)
    em.op(v, lambda e: e.scalar_tensor_tensor(
        out=q3, in0=q4, scalar=4.0, in1=q3, op0=ALU.mult, op1=ALU.add),
        inc="vv")
    em.drain(v)
    em.wait("gpsimd", ("vv",))
    em.op("gpsimd", lambda e: e.tensor_tensor(out=q3v(q3, nb), in0=q3v(q3, nb),
                                              in1=bcast16(dq, nb), op=ALU.mult))
    em.drain("gpsimd")
    em.op("gpsimd", lambda e: e.tensor_tensor(out=dst_ap, in0=q3, in1=sg,
                                              op=ALU.mult), inc="vv")
    em.drain("gpsimd")


def emit_ln(em, ts, x_ap, xn_ap):
    v, a = "vector", "scalar"
    ln = ts["s_ln"]
    em.op(v, lambda e: e.bn_stats(out=ln[:, 0:6], in_=x_ap[:, 0:512]))
    em.op(v, lambda e: e.bn_stats(out=ln[:, 6:12], in_=x_ap[:, 512:1024]))
    em.drain(v)
    em.op(v, lambda e: e.bn_aggr(out=ln[:, 12:14], in_=ln[:, 0:12]))
    em.drain(v)
    em.op(v, lambda e: e.tensor_scalar(out=ln[:, 14:15], in0=ln[:, 13:14],
                                       scalar1=1e-5, scalar2=None, op0=ALU.add),
          inc="vv")
    em.drain(v)
    em.wait("gpsimd", ("vv",))
    em.op("gpsimd", lambda e: e.tensor_tensor(
        out=ln[:, 15:16], in0=ln[:, 14:15], in1=ts["s_mh"][:], op=ALU.pow),
        inc="gg")
    em.wait(v, ("gg",))
    em.op(v, lambda e: e.tensor_scalar(out=xn_ap, in0=x_ap, scalar1=ln[:, 12:13],
                                       scalar2=None, op0=ALU.subtract))
    em.drain(v)
    em.op(v, lambda e: e.tensor_scalar(out=xn_ap, in0=xn_ap, scalar1=ln[:, 15:16],
                                       scalar2=None, op0=ALU.mult), inc="vv")
    em.drain(v)



def phase_p0(em, ts):
    v, a, t, s, g = "vector", "scalar", "tensor", "sync", "gpsimd"
    dram = ts["dram"]
    nc_gs = ts["s_gs"]
    em.op(v, lambda e: e.memset(ts["s_mh"][:], -0.5))
    em.dma("qs", ts["s_msk"][:], dram["masks"][:, :])
    em.dma("qs", ts["s_nw"][:], dram["norm_w_rep"][:, :])
    em.dma("qs", nc_gs[:, 0:1], dram["s_in_rep"][:, :])
    em.dma("qs", nc_gs[:, 3:4], dram["s_in_down_rep"][:, :])
    em.dma("qs", ts["s_hsel"][:], dram["half_sel_rep"][:, :])
    cw3 = ts["s_cw"][:].rearrange("p (c i) -> p c i", i=4)
    em.dma("qs", cw3, dram["conv_w"][:, :].rearrange("(c p) i -> p c i", p=128))
    em.wait(v, ("qs",))
    em.op(v, lambda e: e.tensor_copy(out=ts["s_id"][:],
                                     in_=ts["s_msk"][:, 384:512]))
    for c in (0, 3):
        em.op(v, lambda e, c=c: e.tensor_scalar(out=nc_gs[:, c + 1:c + 2],
                                                in0=nc_gs[:, c:c + 1],
                                                scalar1=1.0 / 12.0, scalar2=None,
                                                op0=ALU.mult))
        em.op(v, lambda e, c=c: e.tensor_copy(out=nc_gs[:, c + 2:c + 3],
                                              in_=nc_gs[:, c:c + 1]))
        em.drain(v)
        em.op(v, lambda e, c=c: e.reciprocal(out=nc_gs[:, c + 2:c + 3],
                                             in_=nc_gs[:, c + 2:c + 3]))
        em.drain(v)
    em.op(v, lambda e: e.memset(nc_gs[:, 15:16], 0.0), inc="vv")
    em.drain(v)



def emit_wamax_pair(em, ts, srcA, srcB, sheetA, sheetB, par):
    v, s = "vector", "sync"
    if par == 0:
        slotA, slotB = ts["s_wsrc"][:], ts["s_wq1"][:]
        grp, glob, sem = ts["s_qam"], ts["s_qdq"][:, 0:1], "vv"
    else:
        slotA, slotB = ts["s_wq3"][:], ts["s_qsg"][:, 0:512]
        grp, glob, sem = ts["s_qs8"], ts["s_qrd"][:, 0:1], "gg"
    em.wait(s, (sem,))
    em.dma("qs", slotA, srcA)
    em.dma("qs", slotB, srcB)
    em.wait(v, ("qs",))
    em.op(v, lambda e: e.tensor_reduce(
        out=grp[:, 0:32], in_=q3v(slotA, 32), axis=AX.X,
        op=ALU.max, apply_absolute_value=True))
    em.op(v, lambda e: e.tensor_reduce(
        out=grp[:, 32:64], in_=q3v(slotB, 32), axis=AX.X,
        op=ALU.max, apply_absolute_value=True))
    em.drain(v)
    em.op(v, lambda e: e.tensor_reduce(
        out=glob, in_=grp[:, 0:64], axis=AX.X,
        op=ALU.max, apply_absolute_value=True))
    em.drain(v)
    em.op(v, lambda e: e.tensor_tensor(
        out=ts["s_gs"][:, 15:16], in0=ts["s_gs"][:, 15:16],
        in1=glob, op=ALU.max), inc=sem)
    em.drain(v)
    em.wait(s, (sem,))
    em.dma("qs", sheetA, grp[:, 0:32])
    em.dma("qs", sheetB, grp[:, 32:64])


def emit_wladder(em, ts, mi, gcol, p):
    v, s = "vector", "sync"
    gs = ts["s_gs"]
    dram = ts["dram"]
    src = ts["s_wsrc"][:]
    zt = ts["s_wq1"][:]
    pt = ts["s_wq3"][:]
    ct = ts["s_qsg"][:, 0:512]
    vt = ts["s_qsg"][:, 512:1024]
    if True:
        em.wait(s, ("vv", "gg"))
        em.dma("qs", src, dram["wsc_scratch"][mi, :, p * 512:(p + 1) * 512])
        em.wait(v, ("qs",))
        em.op(v, lambda e: e.tensor_scalar(out=zt, in0=src,
                                           scalar1=gs[:, gcol + 1:gcol + 2],
                                           scalar2=224.0, op0=ALU.mult,
                                           op1=ALU.min))
        em.op(v, lambda e: e.memset(pt, 2.0 ** -7))
        em.drain(v)
        for k in (8, 4, 2, 1):
            em.op(v, lambda e, k=k: e.tensor_scalar(out=ct, in0=pt,
                                                    scalar1=float(2 ** k),
                                                    scalar2=None, op0=ALU.mult))
            em.drain(v)
            em.op(v, lambda e: e.tensor_tensor(out=ct, in0=ct, in1=zt,
                                               op=ALU.is_le))
            em.drain(v)
            em.op(v, lambda e, k=k: e.tensor_scalar(out=ct, in0=ct,
                                                    scalar1=float(2 ** k - 1),
                                                    scalar2=1.0, op0=ALU.mult,
                                                    op1=ALU.add))
            em.drain(v)
            em.op(v, lambda e: e.tensor_tensor(out=pt, in0=pt, in1=ct,
                                               op=ALU.mult))
            em.drain(v)
        em.op(v, lambda e: e.reciprocal(out=ct, in_=pt))
        em.drain(v)
        em.op(v, lambda e: e.scalar_tensor_tensor(out=src, in0=zt, scalar=8.0,
                                                  in1=ct, op0=ALU.mult,
                                                  op1=ALU.mult))
        em.op(v, lambda e: e.memset(vt, 0.0))
        em.drain(v)
        for k in (16, 8, 4, 2, 1):
            em.op(v, lambda e, k=k: e.scalar_tensor_tensor(out=ct, in0=vt,
                                                           scalar=float(k) - 0.5,
                                                           in1=src, op0=ALU.add,
                                                           op1=ALU.is_le))
            em.drain(v)
            em.op(v, lambda e, k=k: e.scalar_tensor_tensor(out=vt, in0=ct,
                                                           scalar=float(k),
                                                           in1=vt, op0=ALU.mult,
                                                           op1=ALU.add))
            em.drain(v)
        em.op(v, lambda e: e.scalar_tensor_tensor(out=zt, in0=vt, scalar=0.125,
                                                  in1=pt, op0=ALU.mult,
                                                  op1=ALU.mult))
        em.drain(v)
        em.op(v, lambda e: e.tensor_scalar(out=pt, in0=zt,
                                           scalar1=gs[:, gcol + 2:gcol + 3],
                                           scalar2=None, op0=ALU.mult))
        em.op(v, lambda e: e.tensor_scalar(out=ct, in0=zt, scalar1=1e-30,
                                           scalar2=None, op0=ALU.max))
        em.drain(v)
        em.op(v, lambda e: e.reciprocal(out=ct, in_=ct))
        em.drain(v)
        em.op(v, lambda e: e.tensor_scalar(out=ct, in0=ct,
                                           scalar1=gs[:, gcol:gcol + 1],
                                           scalar2=None, op0=ALU.mult),
              inc="vv")
        em.drain(v)
        em.wait(s, ("vv",))
        em.dma("qs", dram["wdq_scratch"][mi, :, p * 512:(p + 1) * 512], pt)
        em.dma("qs", dram["wrd_scratch"][mi, :, p * 512:(p + 1) * 512], ct)


def emit_wbounce(em, ts, gcol):
    v, s = "vector", "sync"
    nc_gs = ts["s_gs"]
    em.wait(s, ("vv", "gg"))
    em.dma("qs", ts["dram"]["wmax_scratch"][:].rearrange("(p o) -> p o", o=1),
           nc_gs[:, 15:16])
    em.wait(s, ("qs",))
    em.dma("qs", ts["s_wlad"][0:1, 0:128],
           ts["dram"]["wmax_scratch"][:].rearrange("(o f) -> o f", o=1))
    em.wait(v, ("qs",))
    em.op(v, lambda e: e.tensor_reduce(out=ts["s_wlad"][0:1, 128:129],
                                       in_=ts["s_wlad"][0:1, 0:128], axis=AX.X,
                                       op=ALU.max, apply_absolute_value=True))
    em.drain(v)
    em.op(v, lambda e: e.tensor_copy(
        out=ts["s_wlad"][0:1, 0:128],
        in_=ts["s_wlad"][0:1, 128:129].broadcast_to((1, 128))), inc="vv")
    em.drain(v)
    em.wait(s, ("vv",))
    em.dma("qs", ts["dram"]["wmax_scratch"][:].rearrange("(o f) -> o f", o=1),
           ts["s_wlad"][0:1, 0:128])
    em.wait(s, ("qs",))
    em.dma("qs", nc_gs[:, gcol:gcol + 1],
           ts["dram"]["wmax_scratch"][:].rearrange("(p o) -> p o", o=1))
    em.wait(v, ("qs",))
    em.op(v, lambda e: e.reciprocal(out=nc_gs[:, gcol:gcol + 1],
                                    in_=nc_gs[:, gcol:gcol + 1]))
    em.drain(v)
    em.op(v, lambda e: e.tensor_scalar(out=nc_gs[:, gcol:gcol + 1],
                                       in0=nc_gs[:, gcol:gcol + 1],
                                       scalar1=2688.0, scalar2=None, op0=ALU.mult))
    em.drain(v)
    em.op(v, lambda e: e.tensor_scalar(out=nc_gs[:, gcol + 1:gcol + 2],
                                       in0=nc_gs[:, gcol:gcol + 1],
                                       scalar1=1.0 / 12.0, scalar2=None,
                                       op0=ALU.mult))
    em.op(v, lambda e: e.tensor_copy(out=nc_gs[:, gcol + 2:gcol + 3],
                                     in_=nc_gs[:, gcol:gcol + 1]))
    em.drain(v)
    em.op(v, lambda e: e.reciprocal(out=nc_gs[:, gcol + 2:gcol + 3],
                                    in_=nc_gs[:, gcol + 2:gcol + 3]))
    em.drain(v)
    em.op(v, lambda e: e.memset(nc_gs[:, 15:16], 0.0), inc="vv")
    em.drain(v)


def emit_wload(em, ts, mi, sidx, src_hbm):
    s = "sync"
    dram = ts["dram"]
    em.wait(s, ("wa",))
    em.dma("qs", ts["s_wsrc"][:], src_hbm)
    em.dma("qs", ts["s_qdq"][:, 0:32],
           dram["wdq_scratch"][mi, :, sidx * 32:(sidx + 1) * 32])
    em.dma("qs", ts["s_qrd"][:, 0:32],
           dram["wrd_scratch"][mi, :, sidx * 32:(sidx + 1) * 32])


def emit_wchain(em, ts, par):
    v, a, s, g = "vector", "scalar", "sync", "gpsimd"
    nb = 32
    q1, q3 = ts["s_wq1"][:], ts["s_wq3"][:]
    q2 = ts["s_qsg"][:, 0:512]
    sg = ts["s_qsg"][:, 512:1024]
    wnt = ts["s_xnb"][:, par * 512:(par + 1) * 512]
    dq, rd = ts["s_qdq"][:, 0:nb], ts["s_qrd"][:, 0:nb]
    src = ts["s_wsrc"][:]
    em.wait(g, ("qs", "wb"))
    em.op(g, lambda e: e.tensor_tensor(out=q3v(q1, nb), in0=q3v(src, nb),
                                       in1=bcast16(rd, nb), op=ALU.mult),
          inc="wa")
    em.wait(a, ("wa", "wc"))
    em.op(a, lambda e: e.activation(out=sg, in_=q1, func=AF.Sign))
    em.op(a, lambda e: e.activation(out=q2, in_=q1, func=AF.Abs), inc="wb")
    em.wait(v, ("wb", "wa"))
    em.op(v, lambda e: e.tensor_scalar(out=q3, in0=q2, scalar1=0.5,
                                       scalar2=None, op0=ALU.is_ge))
    em.drain(v)
    for th in (1.5, 2.5, 3.5):
        em.op(v, lambda e, th=th: e.scalar_tensor_tensor(
            out=q3, in0=q2, scalar=th, in1=q3, op0=ALU.is_ge, op1=ALU.add))
        em.drain(v)
    em.op(v, lambda e: e.tensor_scalar(out=q1, in0=q2, scalar1=5.0,
                                       scalar2=None, op0=ALU.is_ge))
    em.drain(v)
    em.op(v, lambda e: e.scalar_tensor_tensor(
        out=q1, in0=q2, scalar=7.0, in1=q1, op0=ALU.is_ge, op1=ALU.add))
    em.drain(v)
    em.op(v, lambda e: e.scalar_tensor_tensor(
        out=q3, in0=q1, scalar=2.0, in1=q3, op0=ALU.mult, op1=ALU.add))
    em.drain(v)
    em.op(v, lambda e: e.tensor_scalar(out=q1, in0=q2, scalar1=10.0,
                                       scalar2=None, op0=ALU.is_ge))
    em.drain(v)
    em.op(v, lambda e: e.scalar_tensor_tensor(
        out=q3, in0=q1, scalar=4.0, in1=q3, op0=ALU.mult, op1=ALU.add),
        inc="wc")
    em.drain(v)
    em.wait(g, ("wc", "wd"))
    em.op(g, lambda e: e.tensor_tensor(out=q3v(q3, nb), in0=q3v(q3, nb),
                                       in1=bcast16(dq, nb), op=ALU.mult))
    em.drain(g)
    em.op(g, lambda e: e.tensor_tensor(out=wnt, in0=q3, in1=sg,
                                       op=ALU.mult), inc="wa")
    em.drain(g)


def emit_wtrans(em, ts, par, dests):
    a, t, s = "scalar", "tensor", "sync"
    base = par * 512
    em.wait(t, ("wa", "wc"))
    for j in range(4):
        em.op(t, lambda e, j=j: e.matmul(
            out=ts["p_B"][:, j * 128:(j + 1) * 128],
            lhsT=ts["s_xnb"][:, base + j * 128:base + (j + 1) * 128],
            rhs=ts["s_id"][:], start=(j == 0), stop=(j == 3)),
            inc="wb" if j == 3 else None)
    em.wait(a, ("wb", "wd"))
    em.op(a, lambda e: e.copy(out=ts["s_xnb"][:, base:base + 512],
                              in_=ts["p_B"][:, 0:512]), inc="wc")
    em.wait(a, ("wc",))
    for pj in range(4):
        em.dma("qt", dests[pj],
               ts["s_xnb"][:, base + pj * 128:base + (pj + 1) * 128],
               sem="wd")
    em.wait(t, ("wc",))



def phase_f(em, ts):
    v, a, t, s = "vector", "scalar", "tensor", "sync"
    dram = ts["dram"]
    XO = ts["XO"]
    for tt in range(NTO):
        em.wait(s, ("vv",))
        em.dma("qs", ts["s_xt"][:], XO[tt * 128:(tt + 1) * 128, :])
        em.wait(v, ("qs",))
        emit_ln(em, ts, ts["s_xt"][:], ts["s_xn"][:])
        em.wait(v, ("qt",))
        emit_quant(em, ts, ts["s_xn"][:], ts["s_hqb"][:, 0:1024], 1024, 0)
        em.wait("scalar", ("vv",))
        xt3 = ts["s_xqT"][:].rearrange("p (hch t) -> p hch t", hch=8)[
            :, :, tt * 128: tt * 128 + 128]
        em.dmaT(xt3, ts["s_hqb"][:, 0:1024])
    em.wait(t, ("qt",))
    em.wait(v, ("qt",))
    gs = ts["s_gs"]
    nbF = 64
    scr = ts["s_gf"]
    lzt, lpt, lct, lvt, lut = (scr[:, 256:320], scr[:, 320:384],
                               scr[:, 384:448], scr[:, 448:512],
                               scr[:, 512:576])
    def fsmalls(par):
        if par == 0:
            return (ts["s_qam"][:, 0:64], ts["s_qs8"][:, 0:64],
                    ts["s_qdq"][:, 0:64], ts["s_qrd"][:, 0:64])
        return (scr[:, 0:64], scr[:, 64:128], scr[:, 128:192], scr[:, 192:256])
    def fhbuf(par):
        return ts["s_hid"][:] if par == 0 else ts["s_q4"][:]

    def emit_fladder(k):
        par = k % 2
        am, s8, dq, rd = fsmalls(par)
        hb = fhbuf(par)
        em.wait(v, ("vv", "wb"))
        em.op(v, lambda e: e.tensor_reduce(out=am, in_=q3v(hb, nbF), axis=AX.X,
                                           op=ALU.max, apply_absolute_value=True))
        em.drain(v)
        em.op(v, lambda e: e.tensor_scalar(out=lzt, in0=am,
                                           scalar1=gs[:, 4:5], scalar2=224.0,
                                           op0=ALU.mult, op1=ALU.min))
        em.op(v, lambda e: e.memset(lpt, 2.0 ** -7))
        em.drain(v)
        for kk in (8, 4, 2, 1):
            em.op(v, lambda e, kk=kk: e.tensor_scalar(out=lct, in0=lpt,
                                                      scalar1=float(2 ** kk),
                                                      scalar2=None, op0=ALU.mult))
            em.drain(v)
            em.op(v, lambda e: e.tensor_tensor(out=lct, in0=lct, in1=lzt,
                                               op=ALU.is_le))
            em.drain(v)
            em.op(v, lambda e, kk=kk: e.tensor_scalar(out=lct, in0=lct,
                                                      scalar1=float(2 ** kk - 1),
                                                      scalar2=1.0, op0=ALU.mult,
                                                      op1=ALU.add))
            em.drain(v)
            em.op(v, lambda e: e.tensor_tensor(out=lpt, in0=lpt, in1=lct,
                                               op=ALU.mult))
            em.drain(v)
        em.op(v, lambda e: e.reciprocal(out=lct, in_=lpt))
        em.drain(v)
        em.op(v, lambda e: e.scalar_tensor_tensor(out=lut, in0=lzt, scalar=8.0,
                                                  in1=lct, op0=ALU.mult,
                                                  op1=ALU.mult))
        em.op(v, lambda e: e.memset(lvt, 0.0))
        em.drain(v)
        for kk in (16, 8, 4, 2, 1):
            em.op(v, lambda e, kk=kk: e.scalar_tensor_tensor(
                out=lct, in0=lvt, scalar=float(kk) - 0.5, in1=lut,
                op0=ALU.add, op1=ALU.is_le))
            em.drain(v)
            em.op(v, lambda e, kk=kk: e.scalar_tensor_tensor(
                out=lvt, in0=lct, scalar=float(kk), in1=lvt,
                op0=ALU.mult, op1=ALU.add))
            em.drain(v)
        em.op(v, lambda e: e.scalar_tensor_tensor(out=s8, in0=lvt, scalar=0.125,
                                                  in1=lpt, op0=ALU.mult,
                                                  op1=ALU.mult))
        em.drain(v)
        em.op(v, lambda e: e.tensor_scalar(out=dq, in0=s8, scalar1=gs[:, 5:6],
                                           scalar2=None, op0=ALU.mult))
        em.op(v, lambda e: e.tensor_scalar(out=rd, in0=s8, scalar1=1e-30,
                                           scalar2=None, op0=ALU.max))
        em.drain(v)
        em.op(v, lambda e: e.reciprocal(out=rd, in_=rd))
        em.drain(v)
        em.op(v, lambda e: e.tensor_scalar(out=rd, in0=rd, scalar1=gs[:, 3:4],
                                           scalar2=None, op0=ALU.mult),
              inc="wa")
        em.drain(v)

    def emit_fval_head(k):
        par = k % 2
        am, s8, dq, rd = fsmalls(par)
        hb = fhbuf(par)
        q1 = ts["s_q1"][:]
        q2, sgn = ts["s_q2"][:], ts["s_qsg"][:]
        em.wait("gpsimd", ("wa", "wd", "vv"))
        em.op("gpsimd", lambda e: e.tensor_tensor(
            out=q3v(q1, nbF), in0=q3v(hb, nbF), in1=bcast16(rd, nbF),
            op=ALU.mult), inc="wb")
        em.wait(a, ("wb", "vv"))
        em.op(a, lambda e: e.activation(out=sgn, in_=q1, func=AF.Sign))
        em.op(a, lambda e: e.activation(out=q2, in_=q1, func=AF.Abs), inc="wc")

    def emit_fval_tail(k, oc):
        par = k % 2
        am, s8, dq, rd = fsmalls(par)
        q1, q2, q3 = ts["s_q1"][:], ts["s_q2"][:], ts["s_q3"][:]
        sgn = ts["s_qsg"][:]
        em.wait(v, ("wc",))
        em.op(v, lambda e: e.tensor_scalar(out=q3, in0=q2, scalar1=0.5,
                                           scalar2=None, op0=ALU.is_ge))
        em.drain(v)
        for th in (1.5, 2.5, 3.5):
            em.op(v, lambda e, th=th: e.scalar_tensor_tensor(
                out=q3, in0=q2, scalar=th, in1=q3, op0=ALU.is_ge, op1=ALU.add))
            em.drain(v)
        em.op(v, lambda e: e.tensor_scalar(out=q1, in0=q2, scalar1=5.0,
                                           scalar2=None, op0=ALU.is_ge))
        em.drain(v)
        em.op(v, lambda e: e.scalar_tensor_tensor(
            out=q1, in0=q2, scalar=7.0, in1=q1, op0=ALU.is_ge, op1=ALU.add))
        em.drain(v)
        em.op(v, lambda e: e.scalar_tensor_tensor(
            out=q3, in0=q1, scalar=2.0, in1=q3, op0=ALU.mult, op1=ALU.add))
        em.drain(v)
        em.op(v, lambda e: e.tensor_scalar(out=q1, in0=q2, scalar1=10.0,
                                           scalar2=None, op0=ALU.is_ge))
        em.drain(v)
        em.op(v, lambda e: e.scalar_tensor_tensor(
            out=q3, in0=q1, scalar=4.0, in1=q3, op0=ALU.mult, op1=ALU.add),
            inc="wd")
        em.drain(v)
        em.wait("gpsimd", ("wd", "qs"))
        em.op("gpsimd", lambda e: e.tensor_tensor(out=q3v(q3, nbF),
                                                  in0=q3v(q3, nbF),
                                                  in1=bcast16(dq, nbF),
                                                  op=ALU.mult))
        em.drain("gpsimd")
        em.op("gpsimd", lambda e: e.tensor_tensor(out=ts["s_hqb"][:], in0=q3,
                                                  in1=sgn, op=ALU.mult),
              inc="vv")
        em.drain("gpsimd")
        em.wait(s, ("vv",))
        em.dma("qs", dram["hq_scratch"][k * 128:(k + 1) * 128,
                                        oc * 1024:(oc + 1) * 1024],
               ts["s_hqb"][:])

    for oc in range(4):
        em.wait(s, ("tt",))
        for ch in range(8):
            em.dma("qs", ts["s_wgc"][:, ch * 1024:(ch + 1) * 1024],
                   dram["wgT_hbm"][ch, :, oc * 1024:(oc + 1) * 1024])
            em.dma("qs", ts["s_wuc"][:, ch * 1024:(ch + 1) * 1024],
                   dram["wuT_hbm"][ch, :, oc * 1024:(oc + 1) * 1024])
        em.wait(t, ("qs",))
        for tt in range(NTO):
            em.wait(t, ("qt", "aa"))
            for ch in range(8):
                for hf in range(2):
                    em.op(t, lambda e, ch=ch, tt=tt, hf=hf: e.matmul(
                        out=ts["p_A"][:, hf * 512:(hf + 1) * 512],
                        lhsT=ts["s_xqT"][:, ch * 2048 + tt * 128: ch * 2048 + (tt + 1) * 128],
                        rhs=ts["s_wgc"][:, ch * 1024 + hf * 512: ch * 1024 + (hf + 1) * 512],
                        start=(ch == 0), stop=(ch == 7)))
            for ch in range(8):
                for hf in range(2):
                    em.op(t, lambda e, ch=ch, tt=tt, hf=hf: e.matmul(
                        out=ts["p_D"][:, hf * 512:(hf + 1) * 512],
                        lhsT=ts["s_xqT"][:, ch * 2048 + tt * 128: ch * 2048 + (tt + 1) * 128],
                        rhs=ts["s_wuc"][:, ch * 1024 + hf * 512: ch * 1024 + (hf + 1) * 512],
                        start=(ch == 0), stop=(ch == 7)),
                        inc="tt" if (ch == 7 and hf == 1) else None)
            em.wait(a, ("tt",))
            em.op(a, lambda e: e.activation(out=ts["s_sg"][:], in_=ts["p_A"][:],
                                            func=AF.Silu), inc="aa")
            em.wait(v, ("aa", "wb"))
            em.op(v, lambda e, tt=tt: e.tensor_tensor(
                out=fhbuf(tt % 2), in0=ts["s_sg"][:], in1=ts["p_D"][:],
                op=ALU.mult), inc="qt")
            em.drain(v)
            if tt > 0:
                emit_fval_head(tt - 1)
            emit_fladder(tt)
            if tt > 0:
                emit_fval_tail(tt - 1, oc)
        emit_fval_head(NTO - 1)
        emit_fval_tail(NTO - 1, oc)
    em.wait(s, ("qt",))
    for ic in range(32):
        em.dma("qs", ts["s_wT"][:, ic * 1024:(ic + 1) * 1024],
               dram["wdT_hbm"][ic, :, :])
    em.wait(t, ("qs",))
    for tb in range(4):
        em.wait("scalar", ("qs",))
        outT = ts["s_hqT"][:].rearrange("p (ic r) -> p ic r", r=512)
        em.dmaT(outT, dram["hq_scratch"][tb * 512:(tb + 1) * 512, :])
        em.wait(t, ("qt",))
        for tsub in range(4):
            tt = tb * 4 + tsub
            em.wait(t, ("aa",))
            for ic in range(32):
                for half in range(2):
                    em.op(t, lambda e, ic=ic, half=half, tsub=tsub: e.matmul(
                        out=ts["p_A"][:, half * 512:(half + 1) * 512],
                        lhsT=ts["s_hqT"][:, ic * 512 + tsub * 128: ic * 512 + (tsub + 1) * 128],
                        rhs=ts["s_wT"][:, ic * 1024 + half * 512: ic * 1024 + (half + 1) * 512],
                        start=(ic == 0), stop=(ic == 31)),
                        inc="tt" if (ic == 31 and half == 1) else None)
            em.wait(a, ("tt", "qs", "vv"))
            em.op(a, lambda e: e.activation(out=ts["s_gf"][:], in_=ts["p_A"][:],
                                            func=AF.Silu), inc="aa")
            em.wait(s, ("aa", "vv"))
            em.dma("qs", ts["s_q1"][:], dram["o_woA"][tt * 128:(tt + 1) * 128, :])
            em.dma("qs", ts["s_q2"][:], dram["o_woB"][tt * 128:(tt + 1) * 128, :])
            em.dma("qs", ts["s_xt"][:], XO[tt * 128:(tt + 1) * 128, :])
            em.wait(v, ("aa", "qs"))
            em.op(v, lambda e: e.tensor_scalar(
                out=ts["s_q1"][:], in0=ts["s_q1"][:],
                scalar1=ts["s_hsel"][:, 0:1], scalar2=None, op0=ALU.mult))
            em.op(v, lambda e: e.tensor_scalar(
                out=ts["s_q2"][:], in0=ts["s_q2"][:],
                scalar1=ts["s_hsel"][:, 1:2], scalar2=None, op0=ALU.mult))
            em.drain(v)
            em.op(v, lambda e: e.tensor_tensor(out=ts["s_q1"][:], in0=ts["s_q1"][:],
                                               in1=ts["s_q2"][:], op=ALU.add))
            em.drain(v)
            em.op(v, lambda e: e.tensor_tensor(out=ts["s_q3"][:], in0=ts["s_q1"][:],
                                               in1=ts["s_gf"][:], op=ALU.mult))
            em.drain(v)
            em.op(v, lambda e: e.tensor_tensor(out=ts["s_q3"][:], in0=ts["s_q3"][:],
                                               in1=ts["s_xt"][:], op=ALU.add),
                  inc="vv")
            em.drain(v)
            em.wait(s, ("vv",))
            em.dma("qs", ts["OUT"][tt * 128:(tt + 1) * 128, :], ts["s_q3"][:])



def phase_a(em, ts):
    v, a, t, s = "vector", "scalar", "tensor", "sync"
    dram = ts["dram"]
    X = ts["X"]
    msk = ts["s_msk"]
    trilS = msk[:, 0:64]
    triuI = msk[:, 64:128]
    id4 = msk[:, 128:384]
    LP = L + 16

    GWh, UWh, DWh = dram["gate_w"], dram["up_w"], dram["down_w"]
    WGTh, WUTh, WDTh = dram["wgT_hbm"], dram["wuT_hbm"], dram["wdT_hbm"]
    WSCh = dram["wsc_scratch"]
    amax_tasks = []
    for mi, (W, gcol) in enumerate(((GWh, 6), (UWh, 9))):
        for pp in range(32):
            amax_tasks.append(lambda W=W, mi=mi, pp=pp: emit_wamax_pair(
                em, ts,
                W[pp * 128:(pp + 1) * 128, 0:512],
                W[pp * 128:(pp + 1) * 128, 512:1024],
                WSCh[mi, :, 64 * pp:64 * pp + 32],
                WSCh[mi, :, 64 * pp + 32:64 * pp + 64], pp % 2))
        amax_tasks.append(lambda gcol=gcol: emit_wbounce(em, ts, gcol))
        for p in range(4):
            amax_tasks.append(lambda mi=mi, gcol=gcol, p=p:
                              emit_wladder(em, ts, mi, gcol, p))
    for rc in range(8):
        for j in range(4):
            amax_tasks.append(lambda rc=rc, j=j: emit_wamax_pair(
                em, ts,
                DWh[rc * 128:(rc + 1) * 128, 1024 * j:1024 * j + 512],
                DWh[rc * 128:(rc + 1) * 128, 1024 * j + 512:1024 * (j + 1)],
                WSCh[2, :, 256 * rc + 64 * j:256 * rc + 64 * j + 32],
                WSCh[2, :, 256 * rc + 64 * j + 32:256 * rc + 64 * (j + 1)],
                (4 * rc + j) % 2))
    amax_tasks.append(lambda: emit_wbounce(em, ts, 12))
    for p in range(4):
        amax_tasks.append(lambda p=p: emit_wladder(em, ts, 2, 12, p))
    quant_tasks = []
    for mi, (W, WT) in enumerate(((GWh, WGTh), (UWh, WUTh))):
        for pp in range(32):
            for h in range(2):
                dests = [WT[4 * h + j, :, pp * 128:(pp + 1) * 128]
                         for j in range(4)]
                quant_tasks.append(
                    (mi, 2 * pp + h,
                     W[pp * 128:(pp + 1) * 128, h * 512:(h + 1) * 512], dests))
    for rc in range(8):
        for cp in range(8):
            dests = [WDTh[4 * cp + j, :, rc * 128:(rc + 1) * 128]
                     for j in range(4)]
            quant_tasks.append(
                (2, 8 * rc + cp,
                 DWh[rc * 128:(rc + 1) * 128, cp * 512:(cp + 1) * 512], dests))
    n_amax = len(amax_tasks)
    n_quant = len(quant_tasks)

    for wname, sbname, dout in [("Wq", "s_Wq", 512), ("Wk", "s_Wk", 512),
                                ("Wv", "s_Wv", 1024), ("Wb", "s_Wb", 4),
                                ("Wg", "s_Wg", 1024), ("Wo", "s_Wo", 1024)]:
        for ch in range(8):
            em.wait(s, ("vv",))
            em.dma("qs", ts["s_xt"][:, 0:dout],
                   dram[wname][ch * 128:(ch + 1) * 128, :])
            em.wait(v, ("qs",))
            em.op(v, lambda e, sbname=sbname, ch=ch, dout=dout: e.tensor_copy(
                out=ts[sbname][:, ch * dout:(ch + 1) * dout],
                in_=ts["s_xt"][:, 0:dout]), inc="vv")
            em.drain(v)

    em.op(v, lambda e: e.memset(ts["s_Sbf"][:], 0.0))
    em.op(v, lambda e: e.memset(ts["s_Sf"][:], 0.0))
    em.op(v, lambda e: e.memset(ts["s_mh"][:], -0.5), inc="vv")
    em.wait("gpsimd", ("vv",))
    for ch in range(8):
        em.op(v, lambda e, ch=ch: e.memset(ts["s_xnT"][:, ch * LP:ch * LP + 16], 0.0))
    em.op(v, lambda e: e.memset(ts["s_uf"][:], 0.0), inc="vv")

    adone = 0
    for tt in range(NT):
        xb = ts["s_xt"] if tt % 2 == 0 else ts["s_o"]
        psem = "wa" if tt % 2 == 0 else "wb"
        if tt < 2:
            em.wait(s, ("vv",))
        em.wait(s, (psem,))
        em.dma("qs", xb[:], X[tt * 128:(tt + 1) * 128, :])
        em.wait(v, ("qs",))
        emit_ln(em, ts, xb[:], ts["s_xn"][:])
        em.op(v, lambda e: e.nop(), inc=psem)
        em.wait(v, ("qt",))
        em.op(v, lambda e: e.tensor_copy(
            out=ts["s_xnb"][:, 0:1024], in_=ts["s_xn"][:]), inc="vv")
        em.drain(v)
        em.wait("scalar", ("vv",))
        nt3 = ts["s_xnT"][:].rearrange("p (dch t) -> p dch t", dch=8)[
            :, :, 16 + tt * 128: 16 + tt * 128 + 128]
        em.dmaT(nt3, ts["s_xnb"][:, 0:1024])
        want = (tt + 1) * n_amax // NT
        while adone < want:
            amax_tasks[adone]()
            adone += 1
    em.wait(v, ("qt",))
    em.wait(t, ("qt",))

    qp = {"ld": 0, "ch": 0, "tr": 0}
    def qstep(lim):
        if qp["tr"] < qp["ch"] and (qp["ch"] - qp["tr"] >= 2 or
                                    qp["ch"] >= min(n_quant, lim)):
            mi, sidx, src, dests = quant_tasks[qp["tr"]]
            emit_wtrans(em, ts, qp["tr"] % 2, dests)
            qp["tr"] += 1
        if qp["ch"] < qp["ld"] and qp["ch"] < min(n_quant, lim):
            emit_wchain(em, ts, qp["ch"] % 2)
            qp["ch"] += 1
        if qp["ld"] < min(n_quant, lim + 1) and qp["ld"] <= qp["ch"]:
            mi, sidx, src, dests = quant_tasks[qp["ld"]]
            emit_wload(em, ts, mi, sidx, src)
            qp["ld"] += 1
    for tt in range(NT):
        cw3 = ts["s_cw"][:].rearrange("p (c i) -> p c i", i=4)
        base = 13 + tt * 128
        def xv(i, base=base):
            return ts["s_xnT"][:].rearrange("p (c t) -> p c t", c=8)[
                :, :, base + i: base + i + 128]
        q13 = ts["s_q1"][:].rearrange("p (c t) -> p c t", t=128)
        q23 = ts["s_q2"][:].rearrange("p (c t) -> p c t", t=128)
        em.wait(v, ("aa", "qs"))
        em.op(v, lambda e: e.tensor_tensor(
            out=q13, in0=xv(0), in1=cw3[:, :, 0:1].broadcast_to((128, 8, 128)),
            op=ALU.mult))
        for i in (1, 2, 3):
            em.op(v, lambda e, i=i: e.tensor_tensor(
                out=q23, in0=xv(i), in1=cw3[:, :, i:i + 1].broadcast_to((128, 8, 128)),
                op=ALU.mult))
            em.drain(v)
            em.op(v, lambda e: e.tensor_tensor(out=ts["s_q1"][:], in0=ts["s_q1"][:],
                                               in1=ts["s_q2"][:], op=ALU.add))
            em.drain(v)
        em.drain(v)
        em.op(v, lambda e: e.nop(), inc="vv")
        em.wait(a, ("vv",))
        em.op(a, lambda e: e.activation(out=ts["s_hT"][:], in_=ts["s_q1"][:],
                                        func=AF.Silu), inc="aa")
        em.wait(t, ("aa", "vv"))

        for ch in range(8):
            hsl = ts["s_hT"][:, ch * 128:(ch + 1) * 128]
            em.op(t, lambda e, ch=ch, hsl=hsl: e.matmul(
                out=ts["p_A"][:, 0:512], lhsT=hsl,
                rhs=ts["s_Wq"][:, ch * 512:(ch + 1) * 512],
                start=(ch == 0), stop=(ch == 7)))
            em.op(t, lambda e, ch=ch, hsl=hsl: e.matmul(
                out=ts["p_A"][:, 512:1024], lhsT=hsl,
                rhs=ts["s_Wk"][:, ch * 512:(ch + 1) * 512],
                start=(ch == 0), stop=(ch == 7)))
            em.op(t, lambda e, ch=ch, hsl=hsl: e.matmul(
                out=ts["p_D"][:, 0:4], lhsT=hsl,
                rhs=ts["s_Wb"][:, ch * 4:(ch + 1) * 4],
                start=(ch == 0), stop=(ch == 7)), inc="tt" if ch == 7 else None)
        em.wait(a, ("tt",))
        st = ts["s_st"]
        em.wait(v, ("tt", "qt"))
        em.op(v, lambda e: e.tensor_copy(out=ts["s_qk"][:], in_=ts["p_A"][:]),
              inc="vv")
        em.drain(v)
        em.wait(a, ("vv",))
        for h in range(8):
            em.op(a, lambda e, h=h: e.activation(
                out=ts["s_q3"][:, h * 128:(h + 1) * 128],
                in_=ts["s_qk"][:, h * 128:(h + 1) * 128],
                func=AF.Square, accum_out=st[:, 4 + h if h < 4 else h - 4:
                                             5 + h if h < 4 else h - 3]))
        em.op(a, lambda e: e.activation(out=ts["s_beta"][:], in_=ts["p_D"][:, 0:4],
                                        func=AF.Sigmoid), inc="aa")
        em.wait("scalar", ("vv", "tt"))
        em.dmaT(ts["s_qkT"][:].rearrange("p (c r) -> p c r", r=128), ts["s_qk"][:])
        qstep(8 * (tt + 1))
        em.wait("gpsimd", ("aa",))
        em.op("gpsimd", lambda e: e.tensor_scalar(out=st[:, 8:16], in0=st[:, 0:8],
                                                  scalar1=1e-24, scalar2=None,
                                                  op0=ALU.max))
        em.drain("gpsimd")
        em.op("gpsimd", lambda e: e.tensor_tensor(
            out=st[:, 8:16], in0=st[:, 8:16],
            in1=ts["s_mh"][:].broadcast_to((128, 8)), op=ALU.pow))
        em.drain("gpsimd")
        em.op("gpsimd", lambda e: e.tensor_scalar(out=st[:, 8:16], in0=st[:, 8:16],
                                                  scalar1=1e12, scalar2=None,
                                                  op0=ALU.min))
        em.drain("gpsimd")
        em.op("gpsimd", lambda e: e.tensor_tensor(out=st[:, 16:20], in0=st[:, 8:12],
                                                  in1=ts["s_beta"][:], op=ALU.mult))
        em.drain("gpsimd")
        em.op("gpsimd", lambda e: e.tensor_copy(out=st[:, 20:24], in_=st[:, 16:20]))
        em.drain("gpsimd")
        em.op("gpsimd", lambda e: e.tensor_tensor(out=st[:, 16:20], in0=st[:, 16:20],
                                                  in1=st[:, 8:12], op=ALU.mult),
              inc="gg")
        em.wait(v, ("gg",))
        em.wait(t, ("vv",))
        qstep(8 * (tt + 1))
        for ch in range(8):
            hsl = ts["s_hT"][:, ch * 128:(ch + 1) * 128]
            for half in range(2):
                em.op(t, lambda e, ch=ch, hsl=hsl, half=half: e.matmul(
                    out=ts["p_A"][:, half * 512:(half + 1) * 512], lhsT=hsl,
                    rhs=ts["s_Wv"][:, ch * 1024 + half * 512: ch * 1024 + (half + 1) * 512],
                    start=(ch == 0), stop=(ch == 7)),
                    inc="tt" if (ch == 7 and half == 1) else None)
        em.wait(v, ("tt",))
        em.op(v, lambda e: e.tensor_copy(out=ts["s_v"][:], in_=ts["p_A"][:]), inc="vv")
        qstep(8 * (tt + 1))
        em.wait(t, ("vv",))
        for ch in range(8):
            hsl = ts["s_hT"][:, ch * 128:(ch + 1) * 128]
            for half in range(2):
                em.op(t, lambda e, ch=ch, hsl=hsl, half=half: e.matmul(
                    out=ts["p_A"][:, half * 512:(half + 1) * 512], lhsT=hsl,
                    rhs=ts["s_Wg"][:, ch * 1024 + half * 512: ch * 1024 + (half + 1) * 512],
                    start=(ch == 0), stop=(ch == 7)),
                    inc="tt" if (ch == 7 and half == 1) else None)
        em.wait(v, ("tt",))
        em.op(v, lambda e: e.tensor_copy(out=ts["s_g"][:], in_=ts["p_A"][:]), inc="vv")
        em.drain(v)
        qstep(8 * (tt + 1))

        for c in range(2):
            hb = slice(64 * c, 64 * c + 64)
            tpK = 64 * c
            emit_chunk(em, ts, tt, c, hb, tpK, trilS, triuI, id4)
            qstep(8 * (tt + 1))

        st = ts["s_st"]
        em.wait(a, ("vv", "qs"))
        for h in range(4):
            em.op(a, lambda e, h=h: e.activation(
                out=ts["s_q2"][:, h * 256:(h + 1) * 256],
                in_=ts["s_o"][:, h * 256:(h + 1) * 256],
                func=AF.Square, accum_out=st[:, h:h + 1]))
        em.op(a, lambda e: e.activation(out=ts["s_q4"][:], in_=ts["s_g"][:],
                                        func=AF.Silu), inc="aa")
        em.wait("gpsimd", ("aa",))
        em.op("gpsimd", lambda e: e.tensor_scalar(out=st[:, 4:8], in0=st[:, 0:4],
                                                  scalar1=1.0 / 256.0, scalar2=1e-5,
                                                  op0=ALU.mult, op1=ALU.add))
        em.drain("gpsimd")
        em.op("gpsimd", lambda e: e.tensor_tensor(
            out=st[:, 4:8], in0=st[:, 4:8],
            in1=ts["s_mh"][:].broadcast_to((128, 4)), op=ALU.pow), inc="gg")
        em.wait(v, ("gg", "aa"))
        o4 = ts["s_o"][:].rearrange("p (h f) -> p h f", f=256)
        rs4 = st[:, 4:8].rearrange("p (h o) -> p h o", o=1).broadcast_to((128, 4, 256))
        em.op(v, lambda e: e.tensor_tensor(out=o4, in0=o4, in1=rs4, op=ALU.mult))
        em.drain(v)
        nw4 = ts["s_nw"][:].rearrange("p (o f) -> p o f", o=1).broadcast_to((128, 4, 256))
        em.op(v, lambda e: e.tensor_tensor(out=o4, in0=o4, in1=nw4, op=ALU.mult))
        em.drain(v)
        em.wait(v, ("qt",))
        em.op(v, lambda e: e.tensor_tensor(out=ts["s_gtd"][:], in0=ts["s_o"][:],
                                           in1=ts["s_q4"][:], op=ALU.mult), inc="vv")
        em.drain(v)
        qstep(8 * (tt + 1))
        em.wait("scalar", ("vv",))
        outT = ts["s_gTt"][:].rearrange("p (c r) -> p c r", r=128)
        em.dmaT(outT, ts["s_gtd"][:])
        em.wait(t, ("qt", "aa", "vv"))
        for ch in range(8):
            for half in range(2):
                em.op(t, lambda e, ch=ch, half=half: e.matmul(
                    out=ts["p_A"][:, half * 512:(half + 1) * 512],
                    lhsT=ts["s_gTt"][:, ch * 128:(ch + 1) * 128],
                    rhs=ts["s_Wo"][:, ch * 1024 + half * 512: ch * 1024 + (half + 1) * 512],
                    start=(ch == 0), stop=(ch == 7)),
                    inc="tt" if (ch == 7 and half == 1) else None)
        em.wait(v, ("tt", "qs"))
        em.op(v, lambda e: e.tensor_copy(out=ts["s_q2"][:], in_=ts["p_A"][:]),
              inc="vv")
        em.drain(v)
        qstep(8 * (tt + 1))
        em.wait(s, ("vv",))
        rA = tt if tt < NTO else NTO
        rB = tt - NTO if tt >= NTO else NTO
        em.dma("qs", dram["o_woA"][rA * 128:(rA + 1) * 128, :], ts["s_q2"][:])
        em.dma("qs", dram["o_woB"][rB * 128:(rB + 1) * 128, :], ts["s_q2"][:])
        while qp["ch"] < min(n_quant, (tt + 1) * 8) or \
                (tt == NT - 1 and qp["tr"] < n_quant):
            qstep((tt + 1) * 8)


def emit_chunk(em, ts, tt, c, hb, tpK, trilS, triuI, id4):
    v, a, t = "vector", "scalar", "tensor"
    st = ts["s_st"]
    tok0 = 64 * c
    tpO = (0, 64 * c)
    tpKK = (64 * c, 64 * c)

    em.wait(t, ("vv", "aa", "qt"))
    for h in range(4):
        kTc = ts["s_qkT"][:, 512 + h * 128 + tok0: 512 + h * 128 + tok0 + 64]
        em.op(t, lambda e, h=h, kTc=kTc: e.matmul(
            out=ts["p_C"][hb, h * 64:(h + 1) * 64], lhsT=kTc, rhs=kTc,
            start=True, stop=True, tile_position=tpO),
            inc="tt" if h == 3 else None)
    em.wait(v, ("tt",))
    g3 = ts["p_C"][hb, :].rearrange("p (h f) -> p h f", f=64)
    tril3 = trilS[hb, :].rearrange("p (o f) -> p o f", o=1).broadcast_to((64, 4, 64))
    em.op(v, lambda e: e.tensor_tensor(out=ts["s_C"][hb, :].rearrange(
        "p (h f) -> p h f", f=64), in0=g3, in1=tril3, op=ALU.mult))
    em.drain(v)
    s3 = st[hb, 16:20].rearrange("p (h o) -> p h o", o=1).broadcast_to((64, 4, 64))
    em.op(v, lambda e: e.tensor_tensor(out=ts["s_dg"][hb, :].rearrange(
        "p (h f) -> p h f", f=64), in0=id4[hb, :].rearrange("p (h f) -> p h f", f=64),
        in1=s3, op=ALU.mult))
    em.op(v, lambda e: e.tensor_tensor(out=ts["s_At"][hb, :].rearrange(
        "p (h f) -> p h f", f=64), in0=ts["s_C"][hb, :].rearrange(
        "p (h f) -> p h f", f=64), in1=s3, op=ALU.mult), inc="vv")
    em.drain(v)
    em.wait(t, ("vv",))
    for h in range(4):
        em.op(t, lambda e, h=h: e.matmul(
            out=ts["p_B"][hb, h * 64:(h + 1) * 64],
            lhsT=ts["s_C"][hb, h * 64:(h + 1) * 64],
            rhs=ts["s_dg"][hb, h * 64:(h + 1) * 64],
            start=True, stop=True, tile_position=tpKK),
            inc="tt" if h == 3 else None)
    em.wait(v, ("tt",))
    em.op(v, lambda e: e.tensor_copy(out=ts["s_AtT"][hb, :], in_=ts["p_B"][hb, 0:256]),
          inc="vv")
    em.drain(v)
    em.wait(t, ("vv",))
    for h in range(4):
        kTc = ts["s_qkT"][:, 512 + h * 128 + tok0: 512 + h * 128 + tok0 + 64]
        em.op(t, lambda e, h=h, kTc=kTc: e.matmul(
            out=ts["p_A"][hb, h * 256:(h + 1) * 256], lhsT=kTc,
            rhs=ts["s_Sbf"][:, h * 256:(h + 1) * 256],
            start=True, stop=True, tile_position=tpO),
            inc="tt" if h == 3 else None)
    def pwY(Lv):
        return ts["s_pw"][hb, Lv * 512: Lv * 512 + 256]
    def pwYT(Lv):
        return ts["s_pw"][hb, Lv * 512 + 256: Lv * 512 + 512]
    prevY, prevYT = ts["s_At"], ts["s_AtT"]
    for Lv in range(4):
        em.wait(t, ("vv",))
        for h in range(4):
            hs = slice(h * 64, (h + 1) * 64)
            em.op(t, lambda e, h=h, hs=hs: e.matmul(
                out=ts["p_B"][hb, h * 64:(h + 1) * 64],
                lhsT=prevYT[hb, hs], rhs=prevY[hb, hs],
                start=True, stop=True, tile_position=tpKK))
            em.op(t, lambda e, h=h, hs=hs: e.matmul(
                out=ts["p_B"][hb, 256 + h * 64:256 + (h + 1) * 64],
                lhsT=prevY[hb, hs], rhs=prevYT[hb, hs],
                start=True, stop=True, tile_position=tpKK),
                inc="tt" if h == 3 else None)
        em.wait(v, ("tt",))
        if Lv < 3:
            em.op(v, lambda e, Lv=Lv: e.tensor_copy(
                out=ts["s_pw"][hb, Lv * 512:(Lv + 1) * 512], in_=ts["p_B"][hb, :]),
                inc="vv")
        else:
            em.op(v, lambda e, Lv=Lv: e.tensor_copy(
                out=ts["s_pw"][hb, Lv * 512:(Lv + 1) * 512], in_=ts["p_B"][hb, :]),
                inc="vv")
        em.drain(v)
        prevY, prevYT = None, None
        prevY = _PwSlice(ts, Lv, 0)
        prevYT = _PwSlice(ts, Lv, 1)
    em.wait(v, ("tt",))
    ks3 = ts["p_A"][hb, :].rearrange("p (h f) -> p h f", f=256)
    rk3 = st[hb, 8:12].rearrange("p (h o) -> p h o", o=1).broadcast_to((64, 4, 256))
    u3 = ts["s_uf"][hb, :].rearrange("p (h f) -> p h f", f=256)
    v3 = ts["s_v"][hb, :].rearrange("p (h f) -> p h f", f=256)
    em.op(v, lambda e: e.tensor_tensor(out=u3, in0=ks3, in1=rk3, op=ALU.mult))
    em.drain(v)
    em.op(v, lambda e: e.tensor_tensor(out=u3, in0=v3, in1=u3, op=ALU.subtract))
    em.drain(v)
    rkb3 = st[hb, 20:24].rearrange("p (h o) -> p h o", o=1).broadcast_to((64, 4, 256))
    ub3 = ts["s_ub"][hb, :].rearrange("p (h f) -> p h f", f=256)
    em.op(v, lambda e: e.tensor_tensor(out=ub3, in0=u3, in1=rkb3, op=ALU.mult),
          inc="vv")
    em.drain(v)
    em.wait(t, ("vv",))
    for h in range(4):
        hs = slice(h * 64, (h + 1) * 64)
        em.op(t, lambda e, h=h, hs=hs: e.matmul(
            out=ts["p_B"][hb, h * 64:(h + 1) * 64],
            lhsT=_PwSlice(ts, 3, 0)[hb, hs], rhs=_PwSlice(ts, 3, 1)[hb, hs],
            start=True, stop=True, tile_position=tpKK),
            inc="tt" if h == 3 else None)
    em.wait(v, ("tt",))
    em.op(v, lambda e: e.tensor_copy(out=ts["s_Lt"][hb, :], in_=ts["p_B"][hb, 0:256]),
          inc="vv")
    em.drain(v)
    app_seq = [(ts["s_Lt"], 1.0), (_PwSlice(ts, 3, 1), 1.0), (_PwSlice(ts, 2, 1), 1.0),
               (_PwSlice(ts, 1, 1), 1.0), (_PwSlice(ts, 0, 1), 1.0),
               (ts["s_AtT"], -1.0)]
    for lhsT_t, sgn in app_seq:
        em.wait(t, ("vv",))
        for h in range(4):
            em.op(t, lambda e, h=h, lhsT_t=lhsT_t: e.matmul(
                out=ts["p_A"][hb, h * 256:(h + 1) * 256],
                lhsT=lhsT_t[hb, h * 64:(h + 1) * 64],
                rhs=ts["s_ub"][hb, h * 256:(h + 1) * 256],
                start=True, stop=True, tile_position=tpKK),
                inc="tt" if h == 3 else None)
        em.wait(v, ("tt",))
        em.op(v, lambda e, sgn=sgn: e.scalar_tensor_tensor(
            out=ts["s_ub"][hb, :], in0=ts["p_A"][hb, :], scalar=sgn,
            in1=ts["s_ub"][hb, :], op0=ALU.mult, op1=ALU.add), inc="vv")
        em.drain(v)
    em.wait(t, ("vv",))
    for h in range(4):
        qTc = ts["s_qkT"][:, h * 128 + tok0: h * 128 + tok0 + 64]
        em.op(t, lambda e, h=h, qTc=qTc: e.matmul(
            out=ts["p_D"][hb, h * 256:(h + 1) * 256], lhsT=qTc,
            rhs=ts["s_Sbf"][:, h * 256:(h + 1) * 256],
            start=(h % 2 == 0), stop=False, tile_position=tpO))
    for h in range(4):
        qTc = ts["s_qkT"][:, h * 128 + tok0: h * 128 + tok0 + 64]
        kTc = ts["s_qkT"][:, 512 + h * 128 + tok0: 512 + h * 128 + tok0 + 64]
        em.op(t, lambda e, h=h, qTc=qTc, kTc=kTc: e.matmul(
            out=ts["p_C"][hb, h * 64:(h + 1) * 64], lhsT=kTc, rhs=qTc,
            start=True, stop=True, tile_position=tpO),
            inc="tt" if h == 3 else None)
    em.wait(v, ("tt",))
    r3 = ts["p_C"][hb, :].rearrange("p (h f) -> p h f", f=64)
    triu3 = triuI[hb, :].rearrange("p (o f) -> p o f", o=1).broadcast_to((64, 4, 64))
    em.op(v, lambda e: e.tensor_tensor(out=ts["s_Lt"][hb, :].rearrange(
        "p (h f) -> p h f", f=64), in0=r3, in1=triu3, op=ALU.mult), inc="vv")
    em.drain(v)
    em.wait(t, ("vv",))
    for h in range(4):
        em.op(t, lambda e, h=h: e.matmul(
            out=ts["p_D"][hb, h * 256:(h + 1) * 256],
            lhsT=ts["s_Lt"][hb, h * 64:(h + 1) * 64],
            rhs=ts["s_ub"][hb, h * 256:(h + 1) * 256],
            start=False, stop=(h % 2 == 1), tile_position=tpKK),
            inc="tt" if h == 3 else None)
    for h in range(4):
        em.op(t, lambda e, h=h: e.matmul(
            out=ts["p_S"][:, h * 256:(h + 1) * 256],
            lhsT=ts["s_qk"][hb, 512 + h * 128: 512 + (h + 1) * 128],
            rhs=ts["s_ub"][hb, h * 256:(h + 1) * 256],
            start=True, stop=True, tile_position=(64 * c, 0)),
            inc="tt" if h == 3 else None)
    em.wait(v, ("tt",))
    o3 = ts["s_o"][hb, :].rearrange("p (h f) -> p h f", f=256)
    rq3 = st[hb, 12:16].rearrange("p (h o) -> p h o", o=1).broadcast_to((64, 4, 256))
    em.op(v, lambda e: e.tensor_tensor(out=o3, in0=ts["p_D"][hb, :].rearrange(
        "p (h f) -> p h f", f=256), in1=rq3, op=ALU.mult))
    em.op(v, lambda e: e.scalar_tensor_tensor(
        out=ts["s_Sf"][:], in0=ts["p_S"][:], scalar=1.0, in1=ts["s_Sf"][:],
        op0=ALU.mult, op1=ALU.add))
    em.drain(v)
    em.op(v, lambda e: e.tensor_copy(out=ts["s_Sbf"][:], in_=ts["s_Sf"][:]), inc="vv")
    em.drain(v)


class _PwSlice:
    def __init__(self, ts, Lv, which):
        self.t = ts["s_pw"]
        self.base = Lv * 512 + which * 256
    def __getitem__(self, key):
        hb, hs = key
        if isinstance(hs, slice):
            return self.t[hb, self.base + hs.start: self.base + hs.stop]
        raise KeyError(key)



_HOST_CONSTS = None

def _host_consts():
    global _HOST_CONSTS
    if _HOST_CONSTS is None:
        tril = np.tril(np.ones((64, 64), np.float32), -1)
        triu = np.triu(np.ones((64, 64), np.float32), 0)
        ident = np.eye(64, dtype=np.float32)
        id4 = np.concatenate([ident] * 4, axis=1)
        row = np.concatenate([tril, triu, id4], axis=1)
        base = np.concatenate([row, row], axis=0)
        _HOST_CONSTS = np.ascontiguousarray(
            np.concatenate([base, np.eye(128, dtype=np.float32)], axis=1))
    return _HOST_CONSTS


def kernel(**inputs):
    x = np.ascontiguousarray(inputs["L0_attn_hidden_states"], dtype=np.float32)
    consts = _host_consts()
    nw_rep = np.ascontiguousarray(
        np.broadcast_to(inputs["L0_attn_norm_w"][None, :], (128, 256)), np.float32)
    sin_rep = np.ascontiguousarray(
        np.broadcast_to(inputs["L0_ffn_s_in"].reshape(1, 1), (128, 1)), np.float32)
    sid_rep = np.ascontiguousarray(
        np.broadcast_to(inputs["L0_ffn_s_in_down"].reshape(1, 1), (128, 1)), np.float32)
    hsel = [np.ascontiguousarray(
        np.broadcast_to(np.array([[1.0 - h, float(h)]], np.float32), (128, 2)))
        for h in (0, 1)]
    shared = {
        "Wq": np.ascontiguousarray(inputs["L0_attn_Wq"], np.float32),
        "Wk": np.ascontiguousarray(inputs["L0_attn_Wk"], np.float32),
        "Wv": np.ascontiguousarray(inputs["L0_attn_Wv"], np.float32),
        "Wb": np.ascontiguousarray(inputs["L0_attn_Wb"], np.float32),
        "Wg": np.ascontiguousarray(inputs["L0_attn_Wg"], np.float32),
        "Wo": np.ascontiguousarray(inputs["L0_attn_Wo"], np.float32),
        "conv_w": np.ascontiguousarray(inputs["L0_attn_conv_w"], np.float32),
        "norm_w_rep": nw_rep,
        "s_in_rep": sin_rep,
        "s_in_down_rep": sid_rep,
        "masks": consts,
        "gate_w": np.ascontiguousarray(inputs["L0_ffn_gate_w"], np.float32),
        "up_w": np.ascontiguousarray(inputs["L0_ffn_up_w"], np.float32),
        "down_w": np.ascontiguousarray(inputs["L0_ffn_down_w"], np.float32),
    }
    in_maps = []
    for c in range(8):
        b, half = c // 2, c % 2
        m = dict(shared)
        m["x_seq"] = np.ascontiguousarray(x[b])
        m["x_own"] = np.ascontiguousarray(x[b, half * HALF:(half + 1) * HALF])
        m["half_sel_rep"] = hsel[half]
        in_maps.append(m)
    nc = build_program()
    res = run_bass_kernel_spmd(nc, in_maps, core_ids=list(range(8)))
    out = np.empty((B, L, D), np.float32)
    for c in range(8):
        b, half = c // 2, c % 2
        out[b, half * HALF:(half + 1) * HALF] = res.results[c]["out_half"]
    return out
